# version 08
# baseline (speedup 1.0000x reference)
import numpy as np
import ml_dtypes
import concourse.bass as bass
import concourse.mybir as mybir
from concourse.bass_utils import run_bass_kernel_spmd

F32 = mybir.dt.float32
F32R = mybir.dt.float32r
BF16 = mybir.dt.bfloat16
I8 = mybir.dt.int8
AF = mybir.ActivationFunctionType
ALU = mybir.AluOpType

B = 2
L = 8192
DM = 2048
DI = 4096
DXB = 1024
DST = 128
HD = 64
NH = 64
NG = 32
CHUNK = 256
DCONV = 4
CONV_DIM = DI + 2 * DXB
DIP = DI + CONV_DIM + NH
INTER = 8192
EPS = 1e-5

NCORES = 8
NCS = 4
NT = 4
TT = 512
NCHUNK = 2
KC = DM // 128
ZC = DI // 128
XBC_C = CONV_DIM // 128
IC = INTER // 128
TOKC = TT // 128
NBG = 8

MASK_NEG = -1e30



class Sem:

    def __init__(self, nc, name):
        self.h = nc.alloc_semaphore(name)
        self.n = 0

    def inc(self, k=1):
        self.n += k
        return self.n


class _Rec:
    _PASS = {"wait_ge", "nop"}

    _DMA = {"dma_start", "dma_start_transpose", "collective_compute"}

    def __init__(self, e):
        self.e = e
        self.last = None
        self.last_dma = False

    def __getattr__(self, n):
        f = getattr(self.e, n)
        if not callable(f) or n in _Rec._PASS or n.startswith("_"):
            return f
        def g(*a, **k):
            r = f(*a, **k)
            if r is not None and hasattr(r, "then_inc"):
                self.last = r
                self.last_dma = n in _Rec._DMA
            return r
        return g


def _wrap(fn):
    return lambda e: fn(_Rec(e))


def _fin(e, h, k=1):
    li = e.last if isinstance(e, _Rec) else None
    if li is None or (isinstance(e, _Rec) and e.last_dma):
        li = (e.e if isinstance(e, _Rec) else e).drain()
    li.then_inc(h, k)



def _bf(x):
    return np.ascontiguousarray(x.astype(ml_dtypes.bfloat16))



def prep_inputs(inp):
    u = np.asarray(inp["L0_attn_u"], np.float32)
    W_in = np.asarray(inp["L0_attn_W_in"], np.float32)
    conv_w = np.asarray(inp["L0_attn_conv_w"], np.float32)
    conv_b = np.asarray(inp["L0_attn_conv_b"], np.float32)
    dt_bias = np.asarray(inp["L0_attn_dt_bias"], np.float32)
    A_log = np.asarray(inp["L0_attn_A_log"], np.float32)
    Dv = np.asarray(inp["L0_attn_D"], np.float32)
    nw = np.asarray(inp["L0_attn_norm_weight"], np.float32)
    W_out = np.asarray(inp["L0_attn_W_out"], np.float32)
    wg = np.asarray(inp["L0_ffn_w_gate"])
    wu = np.asarray(inp["L0_ffn_w_up"])
    wd = np.asarray(inp["L0_ffn_w_down"])
    sg = np.asarray(inp["L0_ffn_s_gate"], np.float32)
    su = np.asarray(inp["L0_ffn_s_up"], np.float32)
    sd = np.asarray(inp["L0_ffn_s_down"], np.float32)

    WinT = np.ascontiguousarray(W_in.T)
    Wdt = np.ascontiguousarray(
        WinT[:, DI + CONV_DIM:].reshape(KC, 128, NH))
    WxBC = np.ascontiguousarray(
        WinT[:, DI:DI + CONV_DIM].reshape(KC, 128, XBC_C, 128)
        .transpose(2, 0, 1, 3))
    Wz = np.ascontiguousarray(
        WinT[:, :DI].reshape(KC, 128, ZC, 128)
        .transpose(2, 0, 1, 3))
    WoT = np.ascontiguousarray(W_out.T.reshape(DI // 128, 128, DM))
    Wg = np.ascontiguousarray(
        wg.reshape(KC, 128, IC, 128).transpose(2, 0, 1, 3))
    Wu = np.ascontiguousarray(
        wu.reshape(KC, 128, IC, 128).transpose(2, 0, 1, 3))
    Wd = np.ascontiguousarray(wd.reshape(IC, 128, DM))

    cwp = np.ascontiguousarray(
        conv_w.reshape(XBC_C, 128, DCONV).transpose(1, 0, 2)
        .reshape(128, XBC_C * DCONV))
    cbp = np.ascontiguousarray(conv_b.reshape(XBC_C, 128).T)
    nwp = np.ascontiguousarray(nw.reshape(NG, 128).T)
    p = np.arange(128)
    dpp = np.ascontiguousarray(
        Dv[2 * np.arange(NG)[None, :] + (p[:, None] // 64)].astype(np.float32))
    sgp = np.ascontiguousarray(sg.reshape(IC, 128).T)
    sup = np.ascontiguousarray(su.reshape(IC, 128).T)
    sdp = np.ascontiguousarray(sd.reshape(KC, 128).T)

    mneg = np.zeros((128, 2 * CHUNK), np.float32)
    ll = np.arange(CHUNK)
    for st in range(2):
        s = np.arange(128) + st * 128
        mneg[:, st * CHUNK:(st + 1) * CHUNK] = np.where(
            ll[None, :] < s[:, None], MASK_NEG, 0.0)
    mneg = mneg.astype(ml_dtypes.bfloat16)
    ident_bf = np.eye(128, dtype=ml_dtypes.bfloat16)
    ident_f = np.eye(128, dtype=np.float32)
    sel64 = np.eye(64, dtype=np.float32)

    shared = dict(
        Wdt=Wdt, WxBC=WxBC, Wz=Wz, WoT=WoT, Wg=Wg, Wu=Wu, Wd=Wd,
        cwp=cwp, cbp=cbp, nwp=nwp, sgp=sgp, sup=sup, sdp=sdp,
        dtb=dt_bias.reshape(NH, 1), alog=A_log.reshape(NH, 1),
        dvec=Dv.reshape(NH, 1), dpp=dpp,
        mneg=mneg, ident=ident_bf, identf=ident_f, sel64=sel64,
    )

    per_core = []
    for core in range(NCORES):
        seq = core // NCS
        ci = core % NCS
        ut = np.empty((NT, TT, DM), np.float32)
        uh = np.zeros((NT, 3, DM), np.float32)
        for t in range(NT):
            b = 4 * t + ci
            ut[t] = u[seq, TT * b:TT * (b + 1)]
            if b > 0:
                uh[t] = u[seq, TT * b - 3:TT * b]
        msel = np.zeros((NH, NCS), np.float32)
        msel[:, :ci] = 1.0
        hsel = np.zeros((NH, NCS), np.float32)
        if ci > 0:
            hsel[:, ci - 1] = 1.0
        per_core.append(dict(ut=ut, uh=np.ascontiguousarray(uh),
                             msel=msel, hsel=hsel))
    return shared, per_core


def unpack_output(res_list):
    out = np.empty((B, L, DM), np.float32)
    for core in range(NCORES):
        seq = core // NCS
        ci = core % NCS
        o = res_list[core]["out"]
        for t in range(NT):
            b = 4 * t + ci
            out[seq, TT * b:TT * (b + 1), :] = o[t]
    return out



AR_ALIGN = 64


def _pad(n, a=AR_ALIGN):
    return (n + a - 1) // a * a


class Prog:

    def __init__(self, nc):
        self.nc = nc
        self._mk_dram()
        self._mk_sbuf()
        self._mk_psum()
        self._mk_sems()

    def _mk_dram(self):
        nc = self.nc
        d = {}
        d["ut"] = nc.dram_tensor("ut", [NT, TT, DM], F32, kind="ExternalInput")
        d["uh"] = nc.dram_tensor("uh", [NT, 3, DM], F32, kind="ExternalInput")
        d["Wdt"] = nc.dram_tensor("Wdt", [KC, 128, NH], F32, kind="ExternalInput")
        d["WxBC"] = nc.dram_tensor("WxBC", [XBC_C, KC, 128, 128], F32, kind="ExternalInput")
        d["Wz"] = nc.dram_tensor("Wz", [ZC, KC, 128, 128], F32, kind="ExternalInput")
        d["WoT"] = nc.dram_tensor("WoT", [DI // 128, 128, DM], F32, kind="ExternalInput")
        d["Wg"] = nc.dram_tensor("Wg", [IC, KC, 128, 128], I8, kind="ExternalInput")
        d["Wu"] = nc.dram_tensor("Wu", [IC, KC, 128, 128], I8, kind="ExternalInput")
        d["Wd"] = nc.dram_tensor("Wd", [IC, 128, DM], I8, kind="ExternalInput")
        d["cwp"] = nc.dram_tensor("cwp", [128, XBC_C * DCONV], F32, kind="ExternalInput")
        d["cbp"] = nc.dram_tensor("cbp", [128, XBC_C], F32, kind="ExternalInput")
        d["nwp"] = nc.dram_tensor("nwp", [128, NG], F32, kind="ExternalInput")
        d["sgp"] = nc.dram_tensor("sgp", [128, IC], F32, kind="ExternalInput")
        d["sup"] = nc.dram_tensor("sup", [128, IC], F32, kind="ExternalInput")
        d["sdp"] = nc.dram_tensor("sdp", [128, KC], F32, kind="ExternalInput")
        d["dtb"] = nc.dram_tensor("dtb", [NH, 1], F32, kind="ExternalInput")
        d["alog"] = nc.dram_tensor("alog", [NH, 1], F32, kind="ExternalInput")
        d["dvec"] = nc.dram_tensor("dvec", [NH, 1], F32, kind="ExternalInput")
        d["dpp"] = nc.dram_tensor("dpp", [128, NG], F32, kind="ExternalInput")
        d["mneg"] = nc.dram_tensor("mneg", [128, 2 * CHUNK], BF16, kind="ExternalInput")
        d["ident"] = nc.dram_tensor("ident", [128, 128], BF16, kind="ExternalInput")
        d["identf"] = nc.dram_tensor("identf", [128, 128], F32, kind="ExternalInput")
        d["sel64"] = nc.dram_tensor("sel64", [NH, NH], F32, kind="ExternalInput")
        d["msel"] = nc.dram_tensor("msel", [NH, NCS], F32, kind="ExternalInput")
        d["hsel"] = nc.dram_tensor("hsel", [NH, NCS], F32, kind="ExternalInput")
        d["out"] = nc.dram_tensor("out", [NT, TT, DM], F32, kind="ExternalOutput")
        d["h_sp"] = nc.dram_tensor("h_sp", [IC, 128, 2 * TT], BF16)
        d["Wdtb"] = nc.dram_tensor("Wdtb", [KC, 128, NH], BF16)
        d["WxBCb"] = nc.dram_tensor("WxBCb", [XBC_C, KC, 128, 128], BF16)
        d["Wzb"] = nc.dram_tensor("Wzb", [ZC, KC, 128, 128], BF16)
        d["WoTb"] = nc.dram_tensor("WoTb", [DI // 128, 128, DM], BF16)
        d["Wgb"] = nc.dram_tensor("Wgb", [IC, KC, 128, 128], BF16)
        d["Wub"] = nc.dram_tensor("Wub", [IC, KC, 128, 128], BF16)
        d["Wdb"] = nc.dram_tensor("Wdb", [IC, 128, DM], BF16)
        d["r1_sp"] = nc.dram_tensor("r1_sp", [2, TOKC, 128, DM], BF16)
        for t in range(NT):
            d[f"Sin{t}"] = nc.dram_tensor(f"Sin{t}", [128, DI], BF16)
            d[f"Sout{t}"] = nc.dram_tensor(f"Sout{t}", [NCS * 128, DI], BF16)
            d[f"CEin{t}"] = nc.dram_tensor(f"CEin{t}", [128, DI], BF16)
            d[f"CEout{t}"] = nc.dram_tensor(f"CEout{t}", [NCS * 128, DI], BF16)
            d[f"Tin{t}"] = nc.dram_tensor(f"Tin{t}", [1, NH], F32)
            d[f"Tout{t}"] = nc.dram_tensor(f"Tout{t}", [NCS, NH], F32)
        self.d = d

    def _mk_sbuf(self):
        nc = self.nc
        s = {}
        s["xl2"] = nc.alloc_sbuf_tensor("sb_xl2", [128, 2 * KC * TT], BF16)
        s["cw"] = nc.alloc_sbuf_tensor("sb_cw", [128, XBC_C * DCONV], F32)
        s["cb"] = nc.alloc_sbuf_tensor("sb_cb", [128, XBC_C], F32)
        s["nw"] = nc.alloc_sbuf_tensor("sb_nw", [128, NG], F32)
        s["sg"] = nc.alloc_sbuf_tensor("sb_sg", [128, IC], F32)
        s["su"] = nc.alloc_sbuf_tensor("sb_su", [128, IC], F32)
        s["sd"] = nc.alloc_sbuf_tensor("sb_sd", [128, KC], F32)
        s["dtb"] = nc.alloc_sbuf_tensor("sb_dtb", [NH, 1], F32)
        s["alog"] = nc.alloc_sbuf_tensor("sb_alog", [NH, 1], F32)
        s["negA"] = nc.alloc_sbuf_tensor("sb_negA", [NH, 1], F32)
        s["dvec"] = nc.alloc_sbuf_tensor("sb_dvec", [NH, 1], F32)
        s["dpair"] = nc.alloc_sbuf_tensor("sb_dpair", [128, NG], F32)
        s["mneg"] = nc.alloc_sbuf_tensor("sb_mneg", [128, 2 * CHUNK], BF16)
        s["ident"] = nc.alloc_sbuf_tensor("sb_ident", [128, 128], BF16)
        s["identf"] = nc.alloc_sbuf_tensor("sb_identf", [128, 128], F32)
        s["sel64"] = nc.alloc_sbuf_tensor("sb_sel64", [NH, NH], F32)
        s["sel64b"] = nc.alloc_sbuf_tensor("sb_sel64b", [NH, NH], BF16)
        s["msel"] = nc.alloc_sbuf_tensor("sb_msel", [NH, NCS], F32)
        s["hsel"] = nc.alloc_sbuf_tensor("sb_hsel", [NH, NCS], F32)
        s["epsc"] = nc.alloc_sbuf_tensor("sb_epsc", [128, 1], F32)
        s["onec"] = nc.alloc_sbuf_tensor("sb_onec", [128, 1], F32)
        s["negc"] = nc.alloc_sbuf_tensor("sb_negc", [128, 1], F32)
        s["nhc"] = nc.alloc_sbuf_tensor("sb_nhc", [128, 1], F32)
        s["dtraw"] = nc.alloc_sbuf_tensor("sb_dtraw", [NH, TT], F32)
        s["dt"] = nc.alloc_sbuf_tensor("sb_dt", [NH, TT], F32)
        s["dA"] = nc.alloc_sbuf_tensor("sb_dA", [NH, TT], F32)
        s["cst"] = nc.alloc_sbuf_tensor("sb_cst", [NH, TT], F32)
        s["csc"] = nc.alloc_sbuf_tensor("sb_csc", [NH, TT], F32)
        s["ones64"] = nc.alloc_sbuf_tensor("sb_ones64", [NH, TT], F32)
        s["dtds"] = s["dtraw"]
        s["expda"] = nc.alloc_sbuf_tensor("sb_expda", [NH, CHUNK], BF16)
        s["hcol"] = nc.alloc_sbuf_tensor("sb_hcol", [NH, 8], F32)
        s["tokc"] = nc.alloc_sbuf_tensor("sb_tokc", [128, 8 * NH], F32)
        s["bnst"] = nc.alloc_sbuf_tensor("sb_bnst", [128, 4 * 6], F32)
        s["bnag"] = nc.alloc_sbuf_tensor("sb_bnag", [128, 2], F32)
        s["mcol"] = nc.alloc_sbuf_tensor("sb_mcol", [128, 1], F32)
        s["rcol"] = nc.alloc_sbuf_tensor("sb_rcol", [128, 1], F32)
        s["Tg"] = nc.alloc_sbuf_tensor("sb_Tg", [NH, NCS], F32)
        s["Tcum"] = nc.alloc_sbuf_tensor("sb_Tcum", [NH, NCS], F32)
        s["ones4"] = nc.alloc_sbuf_tensor("sb_ones4", [NH, NCS], F32)
        s["fcol"] = nc.alloc_sbuf_tensor("sb_fcol", [NH, 8], F32)
        s["fbf"] = nc.alloc_sbuf_tensor("sb_fbf", [NH, 8], BF16)
        s["pscol"] = nc.alloc_sbuf_tensor("sb_pscol", [NH, 1], F32)
        s["rstd"] = nc.alloc_sbuf_tensor("sb_rstd", [128, 64], F32)
        s["selp"] = nc.alloc_sbuf_tensor("sb_selp", [NH, 128], F32)
        s["cdp"] = nc.alloc_sbuf_tensor("sb_cdp", [128, NH], BF16)
        s["r64"] = nc.alloc_sbuf_tensor("sb_r64", [NH, NH], BF16)
        s["onesbf"] = nc.alloc_sbuf_tensor("sb_onesbf", [128, 1], BF16)
        s["carry"] = nc.alloc_sbuf_tensor("sb_carry", [128, DI], BF16)
        self.s = s

        off = {}
        o = 0
        def put(name, nbytes):
            nonlocal o
            off[name] = o
            o += _pad(nbytes)
        put("xfm", NBG * TT * 2)
        put("bfm", NBG * TT * 2)
        put("cfm", NG * TT * 2)
        put("zy", ZC * TT * 2)
        put("xln", KC * (TT + 3) * 2)
        put("pc", 3 * (TT + 3) * 2)
        put("xtok", DI * 2)
        put("xds", DI * 2)
        put("btok", DXB * 2)
        put("gl", 4 * CHUNK * 2)
        put("wbuf", 3 * KC * 128 * 2)
        put("ust", DM * 2)
        put("r1", DM * 2)
        off["ln1"] = off["r1"]
        put("scr", 6 * TT * 2)
        m_end = o
        f_o = 0
        def putf(name, nbytes):
            nonlocal f_o
            off[name] = f_o
            f_o += _pad(nbytes)
        putf("fh", 4 * IC * 128 * 2)
        putf("fw", 6 * KC * 128 * 2)
        putf("fwd", 6 * 1024 * 2)
        putf("fr1", 4 * DM * 2)
        putf("fo", 4 * 1024 * 4)
        putf("fg", 2 * 1024 * 2)
        putf("fu", 2 * 1024 * 2)
        putf("fho", 2 * 1024 * 2)
        f_end = f_o
        ar_bytes = _pad(max(m_end, f_end))
        self.off = off
        self.ar_bytes = ar_bytes
        self.arena = nc.alloc_sbuf_tensor("arena", [128, ar_bytes], mybir.dt.uint8)

    def pv(self, pool, off_f32, P, E, dtype):
        if dtype == F32:
            return pool[0:P, off_f32:off_f32 + E]
        ap = pool[0:P, off_f32:off_f32 + (E + 1) // 2].bitcast(BF16)
        return bass.AP(ap.tensor, ap.offset, [[ap.ap[0][0], P], [1, E]])

    def av(self, name, eoff, P, E, dtype, poff=0):
        esz = {F32: 4, BF16: 2, F32R: 4}[dtype]
        b0 = self.off[name] + eoff * esz
        return self.arena[poff:poff + P, b0:b0 + E * esz].bitcast(dtype)

    def _mk_psum(self):
        nc = self.nc
        self.pA = nc.alloc_psum_tensor("pA", [128, 2048], F32)
        self.pB = nc.alloc_psum_tensor("pB", [128, 2048], F32)

    def _mk_sems(self):
        nc = self.nc
        self.sm = {k: Sem(nc, k) for k in
                   ["dg", "dg0", "dg1", "dh0", "dh1", "ds", "du", "dtn",
                    "dr1", "df0", "df1", "do", "t", "v", "sc", "gp", "cc",
                    "fw", "fv", "dg2", "dsh", "dsr"] +
                   [f"do{i}" for i in range(4)] +
                   [f"dgw{i}" for i in range(6)] +
                   [f"dcast{i}" for i in range(7)]}


    def barrier(self, block, engines=("tensor", "vector", "scalar", "gpsimd", "sync")):
        sm = self.sm["fv"]
        base = sm.n
        tgt = base + len(engines)
        sm.n = tgt
        for e in engines:
            def f(eng, _t=tgt):
                _fin(eng, sm.h, 1)
                eng.wait_ge(sm.h, _t)
            getattr(block, e)(f)

    def tap(self, block, name, ap, wait=None):
        if name not in self.taps:
            return
        nc = self.nc
        t = nc.dram_tensor("tap_" + name, list(ap.shape), ap.dtype,
                           kind="ExternalOutput")
        sm = self.sm["ds"]
        tgt = sm.inc(16)
        def f(eng):
            if wait is not None:
                eng.wait_ge(wait[0].h, wait[1])
            eng.dma_start(out=t[tuple(slice(None) for _ in ap.shape)],
                          in_=ap).then_inc(sm.h, 16)
            eng.wait_ge(sm.h, tgt)
        block.sync(_wrap(f))

    def emit_init(self, block):
        nc, s, d = self.nc, self.s, self.d
        dg = self.sm["dg"]
        gp = self.sm["gp"]
        sc = self.sm["sc"]

        def g_fn(g):
            for name in ["cwp", "cbp", "nwp", "sgp", "sup", "sdp", "dtb",
                         "alog", "dvec", "dpp", "mneg", "ident", "identf",
                         "sel64", "msel", "hsel"]:
                dst = {"cwp": s["cw"], "cbp": s["cb"], "nwp": s["nw"],
                       "sgp": s["sg"], "sup": s["su"], "sdp": s["sd"],
                       "dtb": s["dtb"], "alog": s["alog"], "dvec": s["dvec"],
                       "dpp": s["dpair"], "mneg": s["mneg"],
                       "ident": s["ident"], "identf": s["identf"],
                       "sel64": s["sel64"], "msel": s["msel"],
                       "hsel": s["hsel"]}[name]
                sh = dst.shape
                g.dma_start(out=dst[:sh[0], :sh[1]],
                            in_=d[name][:sh[0], :sh[1]]).then_inc(dg.h, 16)
            dg.inc(16 * 16)
            g.memset(s["epsc"][:, :], EPS)
            g.memset(s["onec"][:, :], 1.0)
            g.memset(s["negc"][:, :], -1.0)
            g.memset(s["nhc"][:, :], -0.5)
            g.memset(s["ones64"][:NH, :], 1.0)
            g.memset(s["ones4"][:NH, :], 1.0)
            g.memset(s["onesbf"][:, :], 1.0)
            zv = self.av("xds", 0, 128, DI, BF16)
            g.memset(zv, 0.0)
            g.drain()
            g.dma_start(out=d["CEin0"][:, :], in_=zv).then_inc(dg.h, 16)
            dg.inc(16)
            self.cast_rec = {}
            self._ncast = [0]
            def cast(srcs, dsts, n, name):
                dc = self.sm[f"dcast{self._ncast[0]}"]
                self._ncast[0] += 1
                for i in range(n):
                    g.dma_start(out=dsts[i], in_=srcs[i]).then_inc(dc.h, 16)
                    dc.inc(16)
                self.cast_rec[name] = (dc, dc.n)
            cast([d["Wdt"][:, :, :]], [d["Wdtb"][:, :, :]], 1, "Wdt")
            cast([d["WxBC"][o, :, :, :] for o in range(XBC_C)],
                 [d["WxBCb"][o, :, :, :] for o in range(XBC_C)], XBC_C, "WxBC")
            cast([d["Wz"][o, :, :, :] for o in range(ZC)],
                 [d["Wzb"][o, :, :, :] for o in range(ZC)], ZC, "Wz")
            cast([d["WoT"][k, :, :] for k in range(DI // 128)],
                 [d["WoTb"][k, :, :] for k in range(DI // 128)], DI // 128, "WoT")
            cast([d["Wg"][o, :, :, :] for o in range(IC)],
                 [d["Wgb"][o, :, :, :] for o in range(IC)], IC, "Wg")
            cast([d["Wu"][o, :, :, :] for o in range(IC)],
                 [d["Wub"][o, :, :, :] for o in range(IC)], IC, "Wu")
            cast([d["Wd"][o, :, :] for o in range(IC)],
                 [d["Wdb"][o, :, :] for o in range(IC)], IC, "Wd")
            _fin(g, gp.h, 1)
        gp.inc()
        block.gpsimd(_wrap(g_fn))

        def s_fn(scl):
            scl.wait_ge(dg.h, dg.n)
            scl.wait_ge(gp.h, self.sm["gp"].n)
            scl.activation(s["negA"][:NH, :], s["alog"][:NH, :], AF.Exp)
            scl.drain()
            scl.mul(s["negA"][:NH, :], s["negA"][:NH, :], s["negc"][:NH, 0:1])
            _fin(scl, sc.h, 1)
        sc.inc()
        block.scalar(_wrap(s_fn))

        v = self.sm["v"]
        def v_fn(vec):
            vec.wait_ge(dg.h, dg.n)
            vec.tensor_copy(s["sel64b"][:NH, :], s["sel64"][:NH, :])
            _fin(vec, v.h, 1)
        v.inc()
        block.vector(_wrap(v_fn))
        self.barrier(block)


    def xln_v(self, k):
        return self.av("xln", k * (TT + 3), 128, TT + 3, BF16)

    def wb_v(self, buf, k, M=128):
        return self.av("wbuf", buf * (KC * 128) + k * 128, 128, M, BF16)

    def dma_slab(self, g, dram, o, dstname, dstoff, M=128, kc=KC):
        src = bass.AP(dram, o * kc * 128 * M, [[M, 128], [128 * M, kc], [1, M]])
        dv = self.av(dstname, dstoff, 128, kc * M, BF16)
        dst = bass.AP(dv.tensor, dv.offset, [[dv.ap[0][0], 128], [M, kc], [1, M]])
        return g.dma_start(out=dst, in_=src)

    def emit_ln1(self, block, t):
        s, d = self.s, self.d
        ds, v, sc, tt = self.sm["ds"], self.sm["v"], self.sm["sc"], self.sm["t"]
        ust = self.av("ust", 0, 128, DM, BF16)
        du = self.sm["du"]
        v_snap = self.sm["v"].n
        apply_rec = {}
        trans_rec = {}
        copy_rec = {}
        chunks = list(range(-1, TOKC))
        for ci, tc in enumerate(chunks):
            P = 3 if tc < 0 else 128
            par = ci % 2
            ln1 = self.av("ln1", par * DM, P, DM, BF16)
            r_dma = du.inc(16)
            def sy_fn(sy, tc=tc, prev=chunks[ci - 1] if ci >= 1 else None):
                if prev is not None:
                    sy.wait_ge(v.h, apply_rec[prev])
                else:
                    sy.wait_ge(v.h, v_snap)
                if tc < 0:
                    sy.dma_start(out=ust[0:3, :], in_=d["uh"][t, :, :]).then_inc(du.h, 16)
                else:
                    sy.dma_start(out=ust, in_=d["ut"][t, 128 * tc:128 * (tc + 1), :]).then_inc(du.h, 16)
            block.gpsimd(_wrap(sy_fn))

            r_stats = v.inc()
            def v_fn(vec, P=P, r_dma=r_dma):
                vec.wait_ge(du.h, r_dma)
                for q in range(4):
                    vec.bn_stats(s["bnst"][:P, 6 * q:6 * (q + 1)],
                                 ust[:P, 512 * q:512 * (q + 1)])
                vec.drain()
                vec.bn_aggr(s["bnag"][:P, :], s["bnst"][:P, :])
                _fin(vec, v.h, 1)
            block.vector(_wrap(v_fn))

            r_sqrt = sc.inc()
            def s_fn(scl, P=P, r_stats=r_stats):
                scl.wait_ge(v.h, r_stats)
                scl.activation(s["rcol"][:P, :], s["bnag"][:P, 1:2], AF.Ln,
                               bias=s["epsc"][:P, 0:1])
                scl.drain()
                scl.activation(s["rcol"][:P, :], s["rcol"][:P, :], AF.Exp,
                               scale=s["nhc"][:P, 0:1])
                _fin(scl, sc.h, 1)
            block.scalar(_wrap(s_fn))

            r_apply = v.inc()
            apply_rec[tc] = r_apply
            def v_fn2(vec, P=P, ln1=ln1, r_sqrt=r_sqrt,
                      guard=chunks[ci - 2] if ci >= 2 else None):
                vec.wait_ge(sc.h, r_sqrt)
                if guard is not None:
                    vec.wait_ge(tt.h, trans_rec[guard])
                vec.tensor_scalar(out=ln1[:P, :], in0=ust[:P, :],
                                  scalar1=s["bnag"][:P, 0:1],
                                  scalar2=s["rcol"][:P, 0:1],
                                  op0=ALU.subtract, op1=ALU.mult)
                _fin(vec, v.h, 1)
            block.vector(_wrap(v_fn2))

            for g in range(4):
                r_tr = tt.inc()
                def t_fn(ten, g=g, tc=tc, P=P, ln1=ln1, r_apply=r_apply,
                         prevkey=(tc, g) if False else None, ci=ci):
                    ten.wait_ge(v.h, r_apply)
                    if g >= 1:
                        ten.wait_ge(v.h, copy_rec[(tc, g - 1)])
                    elif ci >= 1:
                        ten.wait_ge(v.h, copy_rec[(chunks[ci - 1], 3)])
                    for j in range(4):
                        k = 4 * g + j
                        ps = self.pv(self.pB, 1024 + 64 * j, 128, P, BF16)
                        ten.transpose(ps, ln1[:P, 128 * k:128 * (k + 1)],
                                      s["ident"][:P, :P])
                    _fin(ten, tt.h, 1)
                block.tensor(_wrap(t_fn))
                if g == 3:
                    trans_rec[tc] = r_tr

                r_cp = v.inc()
                copy_rec[(tc, g)] = r_cp
                def v_fn3(vec, g=g, tc=tc, P=P, r_tr=r_tr):
                    vec.wait_ge(tt.h, r_tr)
                    col = 0 if tc < 0 else 3 + 128 * tc
                    for j in range(4):
                        k = 4 * g + j
                        ps = self.pv(self.pB, 1024 + 64 * j, 128, P, BF16)
                        vec.tensor_copy(self.xln_v(k)[:, col:col + P], ps)
                    _fin(vec, v.h, 1)
                block.vector(_wrap(v_fn3))
        self.ln1_done = (v, v.n)

    def emit_dt(self, block, t):
        s, d = self.s, self.d
        dg, v, sc, tt = self.sm["dg"], self.sm["v"], self.sm["sc"], self.sm["t"]
        dg0 = self.sm["dg0"]
        r_w = dg0.inc(16)
        tt_snap = tt.n
        def g_fn(g):
            g.wait_ge(tt.h, tt_snap)
            g.wait_ge(self.cast_rec["Wdt"][0].h, self.cast_rec["Wdt"][1])
            self.dma_slab(g, d["Wdtb"], 0, "wbuf", 0, M=NH).then_inc(dg0.h, 16)
        block.sync(_wrap(g_fn))

        r_mm = tt.inc()
        self.dt_mm = r_mm
        def t_fn(ten):
            ten.wait_ge(self.sm["dg0"].h, r_w)
            ten.wait_ge(self.ln1_done[0].h, self.ln1_done[1])
            for k in range(KC):
                wv = self.av("wbuf", k * NH, 128, NH, BF16)
                ten.matmul(self.pA[0:NH, 0:TT], wv, self.xln_v(k)[:, 3:3 + TT],
                           start=(k == 0), stop=(k == KC - 1))
            _fin(ten, tt.h, 1)
        block.tensor(_wrap(t_fn))

        r_dt = sc.inc()
        r_dtv = v.inc()
        self.dt_evac = r_dtv
        def v_fn(vec):
            vec.wait_ge(tt.h, r_mm)
            vec.tensor_copy(s["dtraw"][:NH, :], self.pA[0:NH, 0:TT])
            _fin(vec, v.h, 1)
        block.vector(_wrap(v_fn))

        def s_fn(scl):
            scl.wait_ge(v.h, r_dtv)
            scl.activation(s["dt"][:NH, :], s["dtraw"][:NH, :], AF.Identity,
                           bias=s["dtb"][:NH, 0:1])
            scl.drain()
            scl.activation(s["dtds"][:NH, :], s["dt"][:NH, :], AF.Abs)
            scl.drain()
            scl.activation(s["dtds"][:NH, :], s["dtds"][:NH, :], AF.Exp,
                           scale=s["negc"][:NH, 0:1])
            scl.drain()
            scl.activation(s["dtds"][:NH, :], s["dtds"][:NH, :], AF.Ln,
                           bias=s["onec"][:NH, 0:1])
            scl.drain()
            scl.activation(s["dt"][:NH, :], s["dt"][:NH, :], AF.Relu)
            _fin(scl, sc.h, 1)
        block.scalar(_wrap(s_fn))

        r_cs = v.inc()
        def v_fn2(vec):
            vec.wait_ge(sc.h, r_dt)
            vec.tensor_add(s["dt"][:NH, :], s["dt"][:NH, :], s["dtds"][:NH, :])
            vec.drain()
            vec.tensor_scalar(out=s["dA"][:NH, :], in0=s["dt"][:NH, :],
                              scalar1=s["negA"][:NH, 0:1], scalar2=None,
                              op0=ALU.mult)
            vec.drain()
            vec.tensor_tensor_scan(s["cst"][:NH, :], s["ones64"][:NH, :],
                                   s["dA"][:NH, :], 0.0, ALU.mult, ALU.add)
            for c in range(NCHUNK):
                vec.tensor_tensor_scan(
                    s["csc"][:NH, CHUNK * c:CHUNK * (c + 1)],
                    s["ones64"][:NH, CHUNK * c:CHUNK * (c + 1)],
                    s["dA"][:NH, CHUNK * c:CHUNK * (c + 1)],
                    0.0, ALU.mult, ALU.add)
            _fin(vec, v.h, 1)
        block.vector(_wrap(v_fn2))

        r_dtds = sc.inc()
        def s_fn2(scl):
            scl.wait_ge(v.h, r_cs)
            scl.activation(s["dtds"][:NH, :], s["cst"][:NH, :], AF.Exp,
                           scale=s["negc"][:NH, 0:1],
                           bias=s["cst"][:NH, TT - 1:TT])
            _fin(scl, sc.h, 1)
        block.scalar(_wrap(s_fn2))

        r_dtds2 = v.inc()
        def v_fn3(vec):
            vec.wait_ge(sc.h, r_dtds)
            vec.tensor_mul(s["dtds"][:NH, :], s["dtds"][:NH, :], s["dt"][:NH, :])
            _fin(vec, v.h, 1)
        block.vector(_wrap(v_fn3))

        dtn = self.sm["dtn"]
        r_ts = dtn.inc(16)
        def sy_fn(sy):
            sy.wait_ge(v.h, r_dtds2)
            sy.dma_start(out=d[f"Tin{t}"][0:1, 0:NH],
                         in_=s["cst"][0:NH, TT - 1:TT]).then_inc(dtn.h, 16)
        block.sync(_wrap(sy_fn))
        self.tin_rec = r_ts
        self.dt_done = (v, r_dtds2)

    def emit_inproj_xbc(self, block, t):
        s = self.s
        dg, v, sc, tt = self.sm["dg"], self.sm["v"], self.sm["sc"], self.sm["t"]
        NO = XBC_C
        GRP = 3
        ngrp = NO // GRP
        w_rec, mm_rec, ev_rec, cv_rec, si_rec = {}, {}, {}, {}, {}
        stp = TT + 3

        for gi in range(ngrp):
            ppool = self.pA if gi % 2 == 0 else self.pB
            for j in range(GRP):
                o = gi * GRP + j

                def g_fn(g, o=o, t=t):
                    dsl = self.sm[f"dg{o % 3}"]
                    if o < 3:
                        g.wait_ge(tt.h, self.dt_mm)
                        if t == 0 and o == 0:
                            g.wait_ge(self.cast_rec["WxBC"][0].h, self.cast_rec["WxBC"][1])
                    if o - 3 in mm_rec:
                        g.wait_ge(tt.h, mm_rec[o - 3])
                    self.dma_slab(g, self.d["WxBCb"], o, "wbuf",
                                  (o % 3) * KC * 128).then_inc(dsl.h, 16)
                    w_rec[o] = dsl.inc(16)
                block.sync(_wrap(g_fn))

                def t_fn(ten, gi=gi, j=j, o=o, ppool=ppool):
                    dsl = self.sm[f"dg{o % 3}"]
                    if o == 0:
                        ten.wait_ge(v.h, self.dt_evac)
                    ten.wait_ge(dsl.h, w_rec[o])
                    if j == 0 and gi >= 2:
                        ten.wait_ge(sc.h, ev_rec[gi - 2])
                    main = ppool[:, 512 * j:512 * (j + 1)]
                    halo = ppool[:, 1536 + 128 * j:1536 + 128 * j + 3]
                    for k in range(KC):
                        wv = self.wb_v(o % 3, k)
                        ten.matmul(main, wv, self.xln_v(k)[:, 3:3 + TT],
                                   start=(k == 0), stop=(k == KC - 1))
                        ten.matmul(halo, wv, self.xln_v(k)[:, 0:3],
                                   start=(k == 0), stop=(k == KC - 1))
                    mm_rec[o] = tt.inc()
                    _fin(ten, tt.h, 1)
                block.tensor(_wrap(t_fn))

            def s_fn(scl, gi=gi, ppool=ppool):
                scl.wait_ge(tt.h, mm_rec[gi * GRP + GRP - 1])
                if gi >= 1:
                    scl.wait_ge(sc.h, si_rec[gi - 1])
                for j in range(GRP):
                    main = ppool[:, 512 * j:512 * (j + 1)]
                    halo = ppool[:, 1536 + 128 * j:1536 + 128 * j + 3]
                    pc = self.av("pc", j * stp, 128, stp, BF16)
                    scl.copy(pc[:, 0:3], halo)
                    scl.copy(pc[:, 3:3 + TT], main)
                ev_rec[gi] = sc.inc()
                _fin(scl, sc.h, 1)
            block.scalar(_wrap(s_fn))

            def v_fn(vec, gi=gi):
                vec.wait_ge(sc.h, ev_rec[gi])
                o0 = gi * GRP
                cw = s["cw"]
                for j in range(GRP):
                    o = o0 + j
                    pcj = self.av("pc", j * stp, 128, stp, BF16)
                    mt = [self.av("scr", q * TT, 128, TT, BF16) for q in range(4)]
                    cvt = self.av("scr", 4 * TT, 128, 2 * TT, BF16)
                    for kk in range(4):
                        vec.tensor_scalar(out=mt[kk][:, :], in0=pcj[:, kk:kk + TT],
                                          scalar1=cw[:, 4 * o + kk:4 * o + kk + 1],
                                          scalar2=None, op0=ALU.mult)
                    vec.drain()
                    vec.tensor_add(cvt[:, 0:TT], mt[0][:, :], mt[1][:, :])
                    vec.tensor_add(cvt[:, TT:2 * TT], mt[2][:, :], mt[3][:, :])
                    vec.drain()
                    vec.tensor_add(pcj[:, 0:TT], cvt[:, 0:TT], cvt[:, TT:2 * TT])
                    vec.drain()
                cv_rec[gi] = v.inc()
                _fin(vec, v.h, 1)
            block.vector(_wrap(v_fn))

            def s_fn2(scl, gi=gi):
                scl.wait_ge(v.h, cv_rec[gi])
                for j in range(GRP):
                    o = gi * GRP + j
                    pcj = self.av("pc", j * stp, 128, stp, BF16)
                    if o < 8:
                        dst = self.av("xfm", o * TT, 128, TT, BF16)
                    elif o < 16:
                        dst = self.av("bfm", (o - 8) * TT, 128, TT, BF16)
                    else:
                        dst = self.av("cfm", (o - 16) * TT, 128, TT, BF16)
                    scl.activation(dst, pcj[:, 0:TT], AF.Silu,
                                   bias=s["cb"][:, o:o + 1])
                si_rec[gi] = sc.inc()
                _fin(scl, sc.h, 1)
            block.scalar(_wrap(s_fn2))
        self.xbc_done = (sc, si_rec[ngrp - 1])
        self.xbc_mm_done = (tt, mm_rec[NO - 1])

    def emit_tapbar(self, block, specs):
        self.barrier(block)
        ds = self.sm["ds"]
        def sy_fn(sy):
            for name, ap in specs:
                tt_ = self.nc.dram_tensor("tap_" + name, list(ap.shape),
                                          ap.dtype, kind="ExternalOutput")
                idx = tuple(slice(None) for _ in ap.shape)
                sy.dma_start(out=tt_[idx], in_=ap).then_inc(ds.h, 16)
                ds.inc(16)
            sy.wait_ge(ds.h, ds.n)
        block.sync(_wrap(sy_fn))

    def _transpose_batch(self, block, items, pbank_off=0, pool=None):
        tt, v = self.sm["t"], self.sm["v"]
        pool = pool or self.pA
        for g0 in range(0, len(items), 4):
            grp = items[g0:g0 + 4]
            def t_fn(ten, grp=grp, g0=g0):
                if g0 >= 4:
                    ten.wait_ge(v.h, self._tb_cp)
                for j, (ina, outa, dt_) in enumerate(grp):
                    ps = self.pv(pool, pbank_off + 64 * j, ina.shape[1],
                                 ina.shape[0], dt_)
                    idn = self.s["ident"] if dt_ == BF16 else self.s["identf"]
                    ten.transpose(ps, ina, idn[:ina.shape[0], :ina.shape[0]])
                _fin(ten, tt.h, 1)
            self._tb_tr = tt.inc()
            block.tensor(_wrap(t_fn))
            r_tr = self._tb_tr
            def v_fn(vec, grp=grp, r_tr=r_tr):
                vec.wait_ge(tt.h, r_tr)
                for j, (ina, outa, dt_) in enumerate(grp):
                    ps = self.pv(pool, pbank_off + 64 * j, ina.shape[1],
                                 ina.shape[0], dt_)
                    vec.tensor_copy(outa, ps)
                _fin(vec, v.h, 1)
            self._tb_cp = v.inc()
            block.vector(_wrap(v_fn))

    def emit_phaseA(self, block, t):
        s, d = self.s, self.d
        dg, ds, v, sc, tt = (self.sm[k] for k in ["dg", "ds", "v", "sc", "t"])
        xtok = lambda st: self.av("xtok", st * DXB, 128, DXB, BF16)
        btokf = lambda st: self.av("zy", st * DXB, 128, DXB, BF16)
        xds = self.av("xds", 0, 128, DI, BF16)

        if t > 0:
            r_ce = dg.inc(16)
            def g_ce(g):
                g.dma_start(out=d[f"CEin{t}"][:, :],
                            in_=s["carry"][:, :]).then_inc(dg.h, 16)
            block.gpsimd(_wrap(g_ce))
            self.ce_sent = (dg, r_ce)
        else:
            self.ce_sent = (dg, dg.n)

        def t_gate(ten):
            ten.wait_ge(self.xbc_done[0].h, self.xbc_done[1])
        block.tensor(_wrap(t_gate))

        items = []
        for st in range(4):
            for g_ in range(NBG):
                items.append((self.av("xfm", g_ * TT + 128 * st, 128, 128, BF16),
                              bass.AP(xtok(st).tensor, xtok(st).offset + g_ * 128,
                                      [[xtok(st).ap[0][0], 128], [1, 128]]),
                              BF16))
        for st in range(4):
            for g_ in range(NBG):
                items.append((self.av("bfm", g_ * TT + 128 * st, 128, 128, BF16),
                              bass.AP(btokf(st).tensor, btokf(st).offset + g_ * 128,
                                      [[btokf(st).ap[0][0], 128], [1, 128]]),
                              BF16))
        for st in range(4):
            items.append((s["dtds"][0:NH, 128 * st:128 * (st + 1)],
                          s["tokc"][0:128, NH * st:NH * (st + 1)], F32))
        self._tb_cp = v.n
        self._transpose_batch(block, items)
        r_tp = self._tb_cp

        smm = None
        for st in range(4):
            r_xds = v.inc()
            def v_fn(vec, st=st, r_xds=r_xds, smm=smm):
                if st == 0:
                    vec.wait_ge(v.h, r_tp)
                else:
                    vec.wait_ge(tt.h, smm)
                for h in range(NH):
                    xc = (h // 8) * 128 + 64 * (h % 2)
                    vec.tensor_scalar(
                        out=bass.AP(xds.tensor, xds.offset + 64 * h,
                                    [[xds.ap[0][0], 128], [1, 64]]),
                        in0=bass.AP(xtok(st).tensor, xtok(st).offset + xc,
                                    [[xtok(st).ap[0][0], 128], [1, 64]]),
                        scalar1=s["tokc"][0:128, NH * st + h:NH * st + h + 1],
                        scalar2=None, op0=ALU.mult)
                _fin(vec, v.h, 1)
            block.vector(_wrap(v_fn))

            smm = tt.inc()
            def t_fn(ten, st=st, r_xds=r_xds):
                ten.wait_ge(v.h, r_xds)
                for bg in range(NBG):
                    ps = self.pA[:, 512 * bg:512 * (bg + 1)] if bg < 4 else \
                        self.pB[:, 512 * (bg - 4):512 * (bg - 3)]
                    lhs = bass.AP(btokf(st).tensor, btokf(st).offset + bg * 128,
                                  [[btokf(st).ap[0][0], 128], [1, 128]])
                    rhs = bass.AP(xds.tensor, xds.offset + 512 * bg,
                                  [[xds.ap[0][0], 128], [1, 512]])
                    ten.matmul(ps, lhs, rhs, start=(st == 0), stop=(st == 3))
                _fin(ten, tt.h, 1)
            block.tensor(_wrap(t_fn))

        r_sev = v.inc()
        def v_fn2(vec, smm=smm):
            vec.wait_ge(tt.h, smm)
            for bg in range(NBG):
                ps = self.pA[:, 512 * bg:512 * (bg + 1)] if bg < 4 else \
                    self.pB[:, 512 * (bg - 4):512 * (bg - 3)]
                vec.tensor_copy(bass.AP(xds.tensor, xds.offset + 512 * bg,
                                        [[xds.ap[0][0], 128], [1, 512]]), ps)
            _fin(vec, v.h, 1)
        block.vector(_wrap(v_fn2))

        r_ssend = dg.inc(16)
        def g_fn(g):
            g.wait_ge(v.h, r_sev)
            g.dma_start(out=d[f"Sin{t}"][:, :], in_=xds).then_inc(dg.h, 16)
        block.gpsimd(_wrap(g_fn))
        self.s_sent = (dg, r_ssend)
        self.sA_done = (v, r_sev)

    def emit_collective(self, block, t):
        d = self.d
        cc, dg, ds = self.sm["cc"], self.sm["dg"], self.sm["ds"]
        groups = [[0, 1, 2, 3], [4, 5, 6, 7]]
        r_cc = cc.inc(3)
        def g_fn(g):
            g.wait_ge(dg.h, max(self.s_sent[1], self.ce_sent[1]))
            g.wait_ge(self.sm["dtn"].h, self.tin_rec)
            for a, b in [(f"Sin{t}", f"Sout{t}"), (f"CEin{t}", f"CEout{t}"),
                         (f"Tin{t}", f"Tout{t}")]:
                g.collective_compute(
                    "AllGather", ALU.bypass, replica_groups=groups,
                    ins=[d[a][:, :].opt()], outs=[d[b][:, :].opt()],
                ).then_inc(cc.h, 1)
        block.gpsimd(_wrap(g_fn))
        self.cc_done = (cc, r_cc)

    def emit_inproj_z(self, block, t):
        dg0, dg1, v, sc, tt = (self.sm[k] for k in
                               ["dg0", "dg1", "v", "sc", "t"])
        w_rec, mm_rec, si_rec = {}, {}, {}
        for o in range(ZC):
            dsl = self.sm[f"dg{o % 3}"]
            def g_fn(g, o=o, dsl=dsl, t=t):
                if o < 3:
                    g.wait_ge(tt.h, self.xbc_mm_done[1])
                    if t == 0 and o == 0:
                        g.wait_ge(self.cast_rec["Wz"][0].h, self.cast_rec["Wz"][1])
                else:
                    g.wait_ge(tt.h, mm_rec[o - 3])
                self.dma_slab(g, self.d["Wzb"], o, "wbuf",
                              (o % 3) * KC * 128).then_inc(dsl.h, 16)
                w_rec[o] = dsl.inc(16)
            block.sync(_wrap(g_fn))

            def t_fn(ten, o=o, dsl=dsl):
                ten.wait_ge(dsl.h, w_rec[o])
                if o < 2:
                    ten.wait_ge(self.sA_done[0].h, self.sA_done[1])
                else:
                    ten.wait_ge(self.sm["sc"].h, si_rec[o - 2])
                ps = self.pA[:, 512 * (o % 2):512 * (o % 2) + 512]
                for k in range(KC):
                    ten.matmul(ps, self.wb_v(o % 3, k),
                               self.xln_v(k)[:, 3:3 + TT],
                               start=(k == 0), stop=(k == KC - 1))
                mm_rec[o] = tt.inc()
                _fin(ten, tt.h, 1)
            block.tensor(_wrap(t_fn))

            def s_fn(scl, o=o):
                scl.wait_ge(tt.h, mm_rec[o])
                ps = self.pA[:, 512 * (o % 2):512 * (o % 2) + 512]
                scl.activation(self.av("zy", o * TT, 128, TT, BF16), ps,
                               AF.Silu)
                si_rec[o] = self.sm["sc"].inc()
                _fin(scl, self.sm["sc"].h, 1)
            block.scalar(_wrap(s_fn))
        self.z_done = (self.sm["sc"], si_rec[ZC - 1])

    def emit_kasm(self, block, t):
        s, d = self.s, self.d
        dg, ds, v, sc, tt, cc = (self.sm[k] for k in
                                 ["dg", "ds", "v", "sc", "t", "cc"])
        recv = self.av("xds", 0, 128, DI, BF16)
        tmp = self.av("scr", 0, 128, 512, BF16)

        dtn = self.sm["dtn"]
        r_tg = dtn.inc(16)
        def sy_fn(sy):
            sy.wait_ge(cc.h, self.cc_done[1])
            src = bass.AP(d[f"Tout{t}"], 0, [[1, NH], [NH, NCS]])
            sy.dma_start(out=s["Tg"][0:NH, 0:NCS], in_=src,
                         allow_slow_non_contiguous=True).then_inc(dtn.h, 16)
        block.sync(_wrap(sy_fn))

        r_f = v.inc()
        def v_fn(vec):
            vec.wait_ge(dtn.h, r_tg)
            vec.tensor_tensor_scan(s["Tcum"][:NH, :], s["ones4"][:NH, :NCS],
                                   s["Tg"][:NH, :], 0.0, ALU.mult, ALU.add)
            vec.drain()
            vec.tensor_mul(s["fcol"][:NH, 0:NCS], s["hsel"][:NH, :],
                           s["Tcum"][:NH, :])
            vec.drain()
            vec.tensor_reduce(s["pscol"][:NH, 0:1], s["fcol"][:NH, 0:NCS],
                              axis=mybir.AxisListType.X, op=ALU.add)
            _fin(vec, v.h, 1)
        block.vector(_wrap(v_fn))

        r_cl = v.inc()
        def v_cl(vec):
            vec.wait_ge(v.h, r_f)
            for j in range(3):
                vec.tensor_scalar(out=s["fcol"][:NH, 1 + j:2 + j],
                                  in0=s["Tcum"][:NH, j:j + 1],
                                  scalar1=s["pscol"][:NH, 0:1], scalar2=0.0,
                                  op0=ALU.subtract, op1=ALU.max)
            _fin(vec, v.h, 1)
        block.vector(_wrap(v_cl))
        r_fs = sc.inc()
        def s_fn(scl):
            scl.wait_ge(v.h, r_cl)
            scl.activation(s["fcol"][:NH, 0:1], s["pscol"][:NH, 0:1], AF.Exp)
            for j in range(3):
                scl.drain()
                scl.activation(s["fcol"][:NH, 1 + j:2 + j],
                               s["fcol"][:NH, 1 + j:2 + j], AF.Exp,
                               scale=s["negc"][:NH, 0:1])
            _fin(scl, sc.h, 1)
        block.scalar(_wrap(s_fn))

        r_fb = v.inc()
        def v_fn2(vec):
            vec.wait_ge(sc.h, r_fs)
            for j in range(3):
                vec.tensor_mul(s["fcol"][:NH, 1 + j:2 + j],
                               s["fcol"][:NH, 1 + j:2 + j],
                               s["msel"][:NH, j:j + 1])
            vec.drain()
            vec.tensor_copy(s["fbf"][:NH, 0:4], s["fcol"][:NH, 0:4])
            _fin(vec, v.h, 1)
        block.vector(_wrap(v_fn2))

        last_v = r_fb
        for q in range(4):
            src = d[f"CEout{t}"] if q == 0 else d[f"Sout{t}"]
            row0 = 3 * 128 if q == 0 else (q - 1) * 128
            r_recv = dg.inc(16)
            def g_fn(g, src=src, row0=row0, last_v=last_v, q=q):
                if q == 0:
                    g.wait_ge(cc.h, self.cc_done[1])
                    g.wait_ge(dg.h, self.s_sent[1])
                g.wait_ge(v.h, last_v)
                g.dma_start(out=recv,
                            in_=src[row0:row0 + 128, :]).then_inc(dg.h, 16)
            block.gpsimd(_wrap(g_fn))

            for slab in range(8):
                ps = self.pA[:, 512 * (slab % 2):512 * (slab % 2) + 512]
                r_mm = tt.inc()
                def t_fn(ten, q=q, slab=slab, ps=ps, r_recv=r_recv):
                    ten.wait_ge(v.h, r_fb)
                    if q == 0 and slab < 2:
                        ten.wait_ge(self.z_done[0].h, self.z_done[1])
                    if not (q == 0 and slab < 2):
                        ten.wait_ge(v.h, self._ka_v)
                    lhs = s["fbf"][0:NH, q:q + 1].to_broadcast((NH, 128))
                    rhs = bass.AP(s["sel64b"], 8 * slab,
                                  [[NH, NH], [1, 8], [0, 64]])
                    ten.matmul(ps, lhs, rhs, start=True, stop=True)
                    _fin(ten, tt.h, 1)
                block.tensor(_wrap(t_fn))

                self._ka_v = v.inc()
                def v_fn3(vec, q=q, slab=slab, ps=ps, r_mm=r_mm, r_recv=r_recv):
                    vec.wait_ge(tt.h, r_mm)
                    vec.wait_ge(dg.h, r_recv)
                    cslab = s["carry"][:, 512 * slab:512 * (slab + 1)]
                    rslab = bass.AP(recv.tensor, recv.offset + 512 * slab,
                                    [[recv.ap[0][0], 128], [1, 512]])
                    if q == 0:
                        vec.tensor_mul(cslab, rslab, ps)
                    else:
                        vec.tensor_mul(tmp[:, :], rslab, ps)
                        vec.drain()
                        vec.tensor_add(cslab, cslab, tmp[:, :])
                    _fin(vec, v.h, 1)
                block.vector(_wrap(v_fn3))
            last_v = self._ka_v
        self.kasm_done = (v, last_v)

    def emit_chunk(self, block, t, c):
        s = self.s
        v, sc, tt = self.sm["v"], self.sm["sc"], self.sm["t"]
        CC = CHUNK * c
        csc_c = s["csc"][0:NH, CC:CC + CHUNK]
        xds = self.av("xds", 0, 128, DI, BF16)
        xtok = self.av("xtok", 0, 128, 4096, BF16)
        first_gate = self.kasm_done if c == 0 else self.chunk_done

        def xtv(col, w=64):
            return bass.AP(xtok.tensor, xtok.offset + col,
                           [[xtok.ap[0][0], 128], [1, w]])

        def xdv(col, w):
            return bass.AP(xds.tensor, xds.offset + col,
                           [[xds.ap[0][0], 128], [1, w]])

        r_prep_s = sc.inc()
        def s_fn(scl):
            scl.wait_ge(first_gate[0].h, first_gate[1])
            scl.activation(s["expda"][:NH, :], csc_c, AF.Exp)
            scl.activation(s["dtds"][:NH, CC:CC + CHUNK], csc_c, AF.Exp,
                           scale=s["negc"][:NH, 0:1],
                           bias=s["csc"][:NH, CC + CHUNK - 1:CC + CHUNK])
            scl.activation(s["hcol"][:NH, 0:1],
                           s["csc"][:NH, CC + CHUNK - 1:CC + CHUNK], AF.Exp)
            _fin(scl, sc.h, 1)
        block.scalar(_wrap(s_fn))

        r_prep_v = v.inc()
        def v_fn(vec):
            vec.wait_ge(sc.h, r_prep_s)
            vec.tensor_mul(s["dtds"][:NH, CC:CC + CHUNK],
                           s["dtds"][:NH, CC:CC + CHUNK],
                           s["dt"][:NH, CC:CC + CHUNK])
            vec.tensor_scalar(out=s["r64"][:NH, :], in0=s["sel64b"][:NH, :],
                              scalar1=s["hcol"][:NH, 0:1], scalar2=None,
                              op0=ALU.mult)
            _fin(vec, v.h, 1)
        block.vector(_wrap(v_fn))

        r_cdp_t = tt.inc()
        def t_fn(ten):
            ten.wait_ge(v.h, r_prep_v)
            ten.matmul(self.pA[:, 0:64],
                       s["onesbf"][0:NH, 0:1].to_broadcast((NH, 128)),
                       s["r64"][0:NH, :], start=True, stop=True)
            _fin(ten, tt.h, 1)
        block.tensor(_wrap(t_fn))
        r_cdp_v = v.inc()
        def v_fn2(vec):
            vec.wait_ge(tt.h, r_cdp_t)
            vec.tensor_copy(s["cdp"][:, 0:NH], self.pA[:, 0:64])
            _fin(vec, v.h, 1)
        block.vector(_wrap(v_fn2))

        items = []
        for st in range(2):
            sl = slice(CC + 128 * st, CC + 128 * (st + 1))
            items.append((s["csc"][0:NH, sl], s["tokc"][0:128, 64 * st:64 * (st + 1)], F32))
            items.append((s["dt"][0:NH, sl], s["tokc"][0:128, 128 + 64 * st:192 + 64 * st], F32))
            items.append((s["dtds"][0:NH, sl], s["tokc"][0:128, 256 + 64 * st:320 + 64 * st], F32))
        for st in range(2):
            for g_ in range(NBG):
                items.append((self.av("xfm", g_ * TT + CC + 128 * st, 128, 128, BF16),
                              xtv(1024 * st + g_ * 128, 128), BF16))
        self._tb_cp = v.n
        self._transpose_batch(block, items, pbank_off=512, pool=self.pA)
        r_tp = self._tb_cp

        r_neg = v.inc()
        def v_fn3(vec):
            vec.wait_ge(v.h, r_tp)
            vec.tensor_scalar(out=s["tokc"][0:128, 384:512],
                              in0=s["tokc"][0:128, 0:128], scalar1=-1.0,
                              scalar2=None, op0=ALU.mult)
            _fin(vec, v.h, 1)
        block.vector(_wrap(v_fn3))

        btok = self.av("btok", 0, 128, DXB, BF16)
        r_cupd = {}
        for ph in range(2):
            pacc = self.pA if ph == 0 else self.pB
            ptr = self.pB if ph == 0 else self.pA
            smm = None
            for st in range(2):
                bitems = [(self.av("bfm", g_ * TT + CC + 128 * st, 128, 128, BF16),
                           bass.AP(btok.tensor, btok.offset + g_ * 128,
                                   [[btok.ap[0][0], 128], [1, 128]]), BF16)
                          for g_ in range(NBG)]
                def tg(ten, ph=ph, st=st, smm=smm):
                    if ph == 0 and st == 0:
                        ten.wait_ge(v.h, r_cdp_v)
                    if ph == 1 and st == 0:
                        ten.wait_ge(v.h, r_cupd[0])
                    if smm is not None:
                        ten.wait_ge(tt.h, smm)
                block.tensor(_wrap(tg))
                self._tb_cp = v.n
                self._transpose_batch(block, bitems, pbank_off=512 + 1024,
                                      pool=ptr)
                r_bt = self._tb_cp

                r_xds = v.inc()
                def v_fn4(vec, ph=ph, st=st, r_bt=r_bt, smm=smm):
                    vec.wait_ge(v.h, r_bt)
                    if st == 0 and ph == 0:
                        vec.wait_ge(v.h, r_neg)
                    if smm is not None:
                        vec.wait_ge(tt.h, smm)
                    for hh in range(32):
                        h = 32 * ph + hh
                        xc = 1024 * st + (h // 8) * 128 + 64 * (h % 2)
                        vec.tensor_scalar(
                            out=xdv(2048 * ph + 64 * hh, 64), in0=xtv(xc),
                            scalar1=s["tokc"][0:128, 256 + 64 * st + h:257 + 64 * st + h],
                            scalar2=None, op0=ALU.mult)
                    _fin(vec, v.h, 1)
                block.vector(_wrap(v_fn4))

                smm = tt.inc()
                def t_fn2(ten, ph=ph, st=st, r_xds=r_xds, pacc=pacc):
                    ten.wait_ge(v.h, r_xds)
                    for bgl in range(4):
                        bg = 4 * ph + bgl
                        ps = pacc[:, 512 * bgl:512 * (bgl + 1)]
                        lhs = bass.AP(btok.tensor, btok.offset + bg * 128,
                                      [[btok.ap[0][0], 128], [1, 128]])
                        rhs = xdv(2048 * ph + 512 * bgl, 512)
                        ten.matmul(ps, lhs, rhs, start=(st == 0), stop=(st == 1))
                    _fin(ten, tt.h, 1)
                block.tensor(_wrap(t_fn2))

            r_cu = v.inc()
            r_cupd[ph] = r_cu
            def v_fn5(vec, ph=ph, smm=smm, pacc=pacc):
                vec.wait_ge(tt.h, smm)
                vec.tensor_copy(xdv(2048 * ph, 2048),
                                s["carry"][:, 2048 * ph:2048 * (ph + 1)])
                vec.drain()
                for bgl in range(4):
                    bg = 4 * ph + bgl
                    ps = pacc[:, 512 * bgl:512 * (bgl + 1)]
                    csl = s["carry"][:, 512 * bg:512 * (bg + 1)]
                    cdap = bass.AP(s["cdp"], 8 * bg, [[NH, 128], [1, 8], [0, 64]])
                    vec.tensor_tensor(csl, csl, cdap, ALU.mult)
                    vec.drain()
                    vec.tensor_add(csl, csl, ps)
                    vec.drain()
                _fin(vec, v.h, 1)
            block.vector(_wrap(v_fn5))
        r_cup = r_cupd[1]

        ydone, exps_rec, yd_rec, gl_rec = {}, {}, {}, {}

        def pools(j):
            pp = self.pA if j % 2 == 0 else self.pB
            return pp, (pp, 512), (pp, 1024), (pp, 1536)

        def emit_t3(j):
            pG, pe, pYd, pYo = pools(j)
            r = tt.inc()
            def t_fn3(ten, j=j, pG=pG, pe=pe, pYd=pYd):
                if j < 2:
                    ten.wait_ge(v.h, r_cup)
                else:
                    ten.wait_ge(v.h, ydone[j - 2])
                crhs = self.av("cfm", j * TT + CC, 128, CHUNK, BF16)
                crhs1 = self.av("cfm", j * TT + CC + 128, 128, 128, BF16)
                for st in range(2):
                    bl = self.av("bfm", (j // 4) * TT + CC + 128 * st, 128, 128, BF16)
                    if st == 0:
                        ten.matmul(pG[:, 0:CHUNK], bl, crhs, start=True, stop=True)
                    else:
                        ten.matmul(pG[:, CHUNK:CHUNK + 128], bl, crhs1,
                                   start=True, stop=True)
                for hh in range(2):
                    lhs = s["sel64"][0:NH, 2 * j + hh:2 * j + hh + 1] \
                        .to_broadcast((NH, 128))
                    ten.matmul(pe[0][:, pe[1] + 256 * hh:pe[1] + 256 * hh + 256],
                               lhs, csc_c, start=True, stop=True)
                    ten.matmul(pYd[0][:, pYd[1] + 256 + 128 * hh:pYd[1] + 256 + 128 * hh + 128],
                               lhs, s["csc"][0:NH, CC + 128:CC + CHUNK],
                               start=True, stop=True)
                _fin(ten, tt.h, 1)
            block.tensor(_wrap(t_fn3))
            return r

        t3_rec = {0: emit_t3(0), 1: emit_t3(1)}
        for j in range(NG):
            pG, pe, pYd, pYo = pools(j)

            r_xb = v.inc()
            def v_fn6a(vec, j=j):
                if j == 0:
                    vec.wait_ge(v.h, r_neg)
                if j >= 4:
                    vec.wait_ge(tt.h, yd_rec[j - 4])
                so = 2048 + 256 * (j % 4)
                for st in range(2):
                    for hh in range(2):
                        h = 2 * j + hh
                        xc = 1024 * st + (h // 8) * 128 + 64 * (h % 2)
                        vec.tensor_scalar(
                            out=xtv(so + 64 * (2 * st + hh)), in0=xtv(xc),
                            scalar1=s["tokc"][0:128, 128 + 64 * st + h:129 + 64 * st + h],
                            scalar2=None, op0=ALU.mult)
                _fin(vec, v.h, 1)
            block.vector(_wrap(v_fn6a))

            r_vm = v.inc()
            def v_fn6b(vec, j=j, pe=pe, pYd=pYd, r_tj=t3_rec[j]):
                vec.wait_ge(tt.h, r_tj)
                ap_e = bass.AP(pe[0], pe[1], [[2048, 128], [256, 2], [1, 128]])
                mm0 = bass.AP(s["mneg"], 0, [[512, 128], [0, 2], [1, 128]])
                vec.tensor_tensor(ap_e, ap_e, mm0, ALU.add)
                ap_e1 = bass.AP(pYd[0], pYd[1] + 256, [[2048, 128], [128, 2], [1, 128]])
                mm1 = bass.AP(s["mneg"], CHUNK + 128, [[512, 128], [0, 2], [1, 128]])
                vec.tensor_tensor(ap_e1, ap_e1, mm1, ALU.add)
                _fin(vec, v.h, 1)
            block.vector(_wrap(v_fn6b))

            r_exps = sc.inc()
            exps_rec[j] = r_exps
            def s_fn2(scl, j=j, pe=pe, pYd=pYd, r_vm=r_vm):
                scl.wait_ge(v.h, r_vm)
                for hh in range(2):
                    h = 2 * j + hh
                    gl0 = self.av("gl", hh * 2 * CHUNK, 128, CHUNK, BF16)
                    scl.activation(gl0,
                                   pe[0][:, pe[1] + 256 * hh:pe[1] + 256 * hh + 256],
                                   AF.Exp, bias=s["tokc"][0:128, 384 + h:385 + h])
                    gl1 = self.av("gl", hh * 2 * CHUNK + CHUNK, 128, 128, BF16)
                    scl.activation(gl1,
                                   pYd[0][:, pYd[1] + 256 + 128 * hh:pYd[1] + 256 + 128 * hh + 128],
                                   AF.Exp, bias=s["tokc"][0:128, 448 + h:449 + h])
                _fin(scl, sc.h, 1)
            block.scalar(_wrap(s_fn2))

            r_em = tt.inc()
            def t_fn4(ten, j=j, pYd=pYd, r_exps=r_exps):
                ten.wait_ge(sc.h, r_exps)
                if j >= 2:
                    ten.wait_ge(v.h, ydone[j - 2])
                for hh in range(2):
                    lhs = s["sel64b"][0:NH, 2 * j + hh:2 * j + hh + 1] \
                        .to_broadcast((NH, 64))
                    ten.matmul(pYd[0][64 * hh:64 * (hh + 1), pYd[1]:pYd[1] + 256],
                               lhs, s["expda"][0:NH, :],
                               start=True, stop=True)
                _fin(ten, tt.h, 1)
            block.tensor(_wrap(t_fn4))

            r_ee = v.inc()
            def s_fn3(vec, j=j, pYd=pYd, r_em=r_em):
                vec.wait_ge(tt.h, r_em)
                eda = self.av("scr", CHUNK * (j % 2), 128, CHUNK, BF16)
                vec.tensor_copy(eda, pYd[0][:, pYd[1]:pYd[1] + 256])
                _fin(vec, v.h, 1)
            block.vector(_wrap(s_fn3))

            r_gl = v.inc()
            gl_rec[j] = r_gl
            def v_fn7(vec, j=j, pG=pG, r_exps=r_exps):
                vec.wait_ge(sc.h, r_exps)
                for hh in range(2):
                    gl0 = self.av("gl", hh * 2 * CHUNK, 128, CHUNK, BF16)
                    vec.tensor_mul(gl0, gl0, pG[:, 0:CHUNK])
                    gl1 = self.av("gl", hh * 2 * CHUNK + CHUNK, 128, 128, BF16)
                    vec.tensor_mul(gl1, gl1, pG[:, CHUNK:CHUNK + 128])
                _fin(vec, v.h, 1)
            block.vector(_wrap(v_fn7))

            r_ym = tt.inc()
            yd_rec[j] = r_ym
            def t_fn5(ten, j=j, pYd=pYd, pYo=pYo, r_gl=r_gl, r_ee=r_ee):
                ten.wait_ge(v.h, max(r_gl, r_ee))
                so = 2048 + 256 * (j % 4)
                for hh in range(2):
                    for st in range(2):
                        lhsT = xtv(so + 64 * (2 * st + hh))
                        gl0 = self.av("gl", hh * 2 * CHUNK + st * CHUNK, 128,
                                      CHUNK if st == 0 else 128, BF16)
                        out = pYd[0][64 * hh:64 * hh + 64,
                                     pYd[1] + (0 if st == 0 else 128):pYd[1] + 256]
                        ten.matmul(out, lhsT, gl0, start=(st == 0), stop=(st == 1))
                crhs = self.av("cfm", j * TT + CC, 128, CHUNK, BF16)
                ten.matmul(pYo[0][:, pYo[1]:pYo[1] + 256], xdv(128 * j, 128),
                           crhs, start=True, stop=True)
                _fin(ten, tt.h, 1)
            block.tensor(_wrap(t_fn5))

            r_ya = v.inc()
            ydone[j] = r_ya
            def v_fn8(vec, j=j, pYd=pYd, pYo=pYo, r_ym=r_ym):
                vec.wait_ge(tt.h, r_ym)
                eda = self.av("scr", CHUNK * (j % 2), 128, CHUNK, BF16)
                ysb = self.av("scr", 2 * CHUNK, 128, CHUNK, BF16)
                vec.tensor_mul(ysb, eda, pYo[0][:, pYo[1]:pYo[1] + 256])
                vec.drain()
                vec.tensor_add(ysb, ysb, pYd[0][:, pYd[1]:pYd[1] + 256])
                vec.drain()
                xsl = self.av("xfm", (j // 4) * TT + CC, 128, CHUNK, BF16)
                vec.scalar_tensor_tensor(out=ysb, in0=xsl,
                                         scalar=s["dpair"][:, j:j + 1],
                                         in1=ysb, op0=ALU.mult, op1=ALU.add)
                vec.drain()
                zsl = self.av("zy", j * TT + CC, 128, CHUNK, BF16)
                vec.tensor_mul(zsl, ysb, zsl)
                _fin(vec, v.h, 1)
            block.vector(_wrap(v_fn8))

            if j + 2 < NG:
                t3_rec[j + 2] = emit_t3(j + 2)

        sq_rec = {}
        for j in range(NG):
            r_sq = v.inc()
            def v_fn9(vec, j=j):
                vec.wait_ge(v.h, ydone[j])
                if j >= 1:
                    vec.wait_ge(tt.h, sq_rec[j - 1])
                vec.drain()
                ysq = self.av("scr", 2 * CHUNK, 128, CHUNK, BF16)
                zsl = self.av("zy", j * TT + CC, 128, CHUNK, BF16)
                vec.tensor_mul(ysq, zsl, zsl)
                _fin(vec, v.h, 1)
            block.vector(_wrap(v_fn9))
            r_sm = tt.inc()
            sq_rec[j] = r_sm
            def t_fn6(ten, j=j, r_sq=r_sq):
                ten.wait_ge(v.h, r_sq)
                if j == 0:
                    ten.wait_ge(v.h, ydone[NG - 1])
                ysq = self.av("scr", 2 * CHUNK, 128, CHUNK, BF16)
                for half in range(2):
                    lhsT = bass.AP(ysq.tensor, ysq.offset + 128 * half,
                                   [[ysq.ap[0][0], 128], [1, 128]])
                    ten.matmul(self.pA[:, 2 * j + half:2 * j + half + 1],
                               lhsT, s["onesbf"][:, 0:1].to_broadcast((128, 1)),
                               start=True, stop=True)
                _fin(ten, tt.h, 1)
            block.tensor(_wrap(t_fn6))
        r_rs = v.inc()
        def v_fn10(vec):
            vec.wait_ge(tt.h, sq_rec[NG - 1])
            vec.tensor_scalar(out=s["rstd"][:, 0:NH], in0=self.pA[:, 0:64],
                              scalar1=1.0 / 128, scalar2=None, op0=ALU.mult)
            _fin(vec, v.h, 1)
        block.vector(_wrap(v_fn10))
        r_rsq = sc.inc()
        def s_fn4(scl):
            scl.wait_ge(v.h, r_rs)
            scl.activation(s["rstd"][:, 0:NH], s["rstd"][:, 0:NH], AF.Ln,
                           bias=s["epsc"][:, 0:1])
            scl.drain()
            scl.activation(s["rstd"][:, 0:NH], s["rstd"][:, 0:NH], AF.Exp,
                           scale=s["nhc"][:, 0:1])
            _fin(scl, sc.h, 1)
        block.scalar(_wrap(s_fn4))
        r_rrec = r_rsq

        ap_rec = {}
        for j in range(NG):
            r_rb = v.inc()
            def v_fn12(vec, j=j):
                vec.wait_ge(sc.h, r_rrec)
                if j >= 2:
                    vec.wait_ge(v.h, ap_rec[j - 2])
                for half in range(2):
                    rh = self.av("scr", 3 * CHUNK + 128 * half, 128, 128, BF16)
                    vec.tensor_scalar(out=rh, in0=s["ident"][:, :],
                                      scalar1=s["rstd"][:, 2 * j + half:2 * j + half + 1],
                                      scalar2=None, op0=ALU.mult)
                _fin(vec, v.h, 1)
            block.vector(_wrap(v_fn12))
            r_rm = tt.inc()
            def t_fn7(ten, j=j, r_rb=r_rb):
                ten.wait_ge(v.h, r_rb)
                pR = self.pB if j % 2 == 0 else self.pA
                for half in range(2):
                    rh = self.av("scr", 3 * CHUNK + 128 * half, 128, 128, BF16)
                    ten.matmul(pR[:, 128 * half:128 * (half + 1)],
                               s["onesbf"][:, 0:1].to_broadcast((128, 128)),
                               rh, start=True, stop=True)
                _fin(ten, tt.h, 1)
            block.tensor(_wrap(t_fn7))
            r_ap = v.inc()
            ap_rec[j] = r_ap
            def v_fn13(vec, j=j, r_rm=r_rm):
                vec.wait_ge(tt.h, r_rm)
                pR = self.pB if j % 2 == 0 else self.pA
                for half in range(2):
                    zsl = self.av("zy", j * TT + CC + 128 * half, 128, 128, BF16)
                    vec.scalar_tensor_tensor(
                        out=zsl, in0=zsl, scalar=s["nw"][:, j:j + 1],
                        in1=pR[:, 128 * half:128 * (half + 1)],
                        op0=ALU.mult, op1=ALU.mult)
                _fin(vec, v.h, 1)
            block.vector(_wrap(v_fn13))
        self.chunk_done = (v, ap_rec[NG - 1])

    def emit_outproj(self, block, t):
        s, d = self.s, self.d
        dg, ds, v, sc, tt = (self.sm[k] for k in ["dg", "ds", "v", "sc", "t"])
        dg0, dg1 = self.sm["dg0"], self.sm["dg1"]
        ust = self.av("ust", 0, 128, DM, BF16)
        r1 = self.av("r1", 0, 128, DM, BF16)
        w = t % 2
        w_rec, mm_rec, ev_rec, tr_rec = {}, {}, {}, {}
        u_rec, ln_rec = {}, {}
        for tp in range(2):
            for k in range(KC + KC):
                dsl = self.sm[f"dg{k % 3}"]
                def g_fn(g, tp=tp, k=k, dsl=dsl, t=t):
                    if tp == 0 and k < 3:
                        g.wait_ge(self.z_done[0].h, self.z_done[1])
                        if t == 0 and k == 0:
                            g.wait_ge(self.cast_rec["WoT"][0].h, self.cast_rec["WoT"][1])
                    if k - 3 >= 0 or tp > 0:
                        pk = (tp, k - 3) if k >= 3 else (tp - 1, KC + KC - 3 + k)
                        g.wait_ge(tt.h, mm_rec[pk])
                    self.dma_slab(g, d["WoTb"], k, "wbuf",
                                  (k % 3) * KC * 128, M=DM, kc=1
                                  ).then_inc(dsl.h, 16)
                block.sync(_wrap(g_fn))
                w_rec[(tp, k)] = dsl.inc(16)

                def t_fn(ten, tp=tp, k=k, dsl=dsl,
                         cpw=(self._ol_cp if tp == 1 else None)):
                    ten.wait_ge(dsl.h, w_rec[(tp, k)])
                    if k == 0:
                        ten.wait_ge(v.h, self.chunk_done[1])
                        if tp == 1:
                            ten.wait_ge(v.h, cpw)
                    wv = self.av("wbuf", (k % 3) * KC * 128, 128, DM, BF16)
                    for tl in range(2):
                        tok = 2 * tp + tl
                        pp = self.pA if tl == 0 else self.pB
                        lhsT = self.av("zy", k * TT + 128 * tok, 128, 128, BF16)
                        for q in range(4):
                            ten.matmul(pp[:, 512 * q:512 * (q + 1)], lhsT,
                                       bass.AP(wv.tensor, wv.offset + 512 * q,
                                               [[wv.ap[0][0], 128], [1, 512]]),
                                       start=(k == 0), stop=(k == KC + KC - 1))
                    mm_rec[(tp, k)] = tt.inc()
                    _fin(ten, tt.h, 1)
                block.tensor(_wrap(t_fn))

            for tl in range(2):
                tok = 2 * tp + tl
                pp = self.pA if tl == 0 else self.pB
                du = self.sm["du"]
                r_u = du.inc(16)
                u_rec[tok] = r_u
                def sy_fn(sy, t=t, tok=tok, tp=tp, tl=tl):
                    if tok > 0:
                        sy.wait_ge(v.h, ev_rec[tok - 1])
                    sy.dma_start(out=ust, in_=d["ut"][t, 128 * tok:128 * (tok + 1), :]
                                 ).then_inc(du.h, 16)
                block.gpsimd(_wrap(sy_fn))

                r_ev = v.inc()
                ev_rec[tok] = r_ev
                def v_fn(vec, tok=tok, tp=tp, pp=pp, r_u=r_u):
                    vec.wait_ge(tt.h, mm_rec[(tp, KC + KC - 1)])
                    vec.wait_ge(self.sm["du"].h, r_u)
                    if tok > 0:
                        vec.wait_ge(self.sm["dr1"].h, self.r1_rec)
                        vec.wait_ge(tt.h, tr_rec[tok - 1])
                    vec.tensor_add(r1, pp[:, 0:DM], ust)
                    vec.drain()
                    for q in range(4):
                        vec.bn_stats(s["bnst"][:, 6 * q:6 * (q + 1)],
                                     r1[:, 512 * q:512 * (q + 1)])
                    vec.drain()
                    vec.bn_aggr(s["bnag"][:, :], s["bnst"][:, :])
                    _fin(vec, v.h, 1)
                block.vector(_wrap(v_fn))

                self.r1_rec = self.sm["dr1"].inc(16)
                def sy_fn2(sy, t=t, tok=tok, r_ev=r_ev):
                    sy.wait_ge(v.h, r_ev)
                    sy.dma_start(out=d["r1_sp"][t % 2, tok, :, :],
                                 in_=r1).then_inc(self.sm["dr1"].h, 16)
                block.sync(_wrap(sy_fn2))

                r_sq = sc.inc()
                def s_fn(scl, r_ev=r_ev):
                    scl.wait_ge(v.h, r_ev)
                    scl.activation(s["rcol"][:, :], s["bnag"][:, 1:2], AF.Ln,
                                   bias=s["epsc"][:, 0:1])
                    scl.drain()
                    scl.activation(s["rcol"][:, :], s["rcol"][:, :], AF.Exp,
                                   scale=s["nhc"][:, 0:1])
                    _fin(scl, sc.h, 1)
                block.scalar(_wrap(s_fn))

                r_ln = v.inc()
                ln_rec[tok] = r_ln
                def v_fn2(vec, r_sq=r_sq):
                    vec.wait_ge(sc.h, r_sq)
                    vec.wait_ge(self.sm["dr1"].h, self.r1_rec)
                    vec.tensor_scalar(out=r1, in0=r1,
                                      scalar1=s["bnag"][:, 0:1],
                                      scalar2=s["rcol"][:, 0:1],
                                      op0=ALU.subtract, op1=ALU.mult)
                    _fin(vec, v.h, 1)
                block.vector(_wrap(v_fn2))

                for g_ in range(4):
                    r_tr = tt.inc()
                    def t_fn2(ten, g_=g_, tok=tok, r_ln=r_ln):
                        ten.wait_ge(v.h, r_ln)
                        if g_ >= 1:
                            ten.wait_ge(v.h, self._ol_cp)
                        for jj in range(4):
                            k = 4 * g_ + jj
                            ps = self.pv(self.pA, 1024 + 64 * jj, 128, 128, BF16)
                            ten.transpose(ps, r1[:, 128 * k:128 * (k + 1)],
                                          s["ident"][:, :])
                        _fin(ten, tt.h, 1)
                    block.tensor(_wrap(t_fn2))
                    if g_ == 3:
                        tr_rec[tok] = r_tr
                    def v_fn3(vec, g_=g_, tok=tok, w=w, r_tr=r_tr):
                        vec.wait_ge(tt.h, r_tr)
                        for jj in range(4):
                            k = 4 * g_ + jj
                            ps = self.pv(self.pA, 1024 + 64 * jj, 128, 128, BF16)
                            vec.tensor_copy(
                                s["xl2"][:, (w * KC + k) * TT + 128 * tok:
                                         (w * KC + k) * TT + 128 * (tok + 1)], ps)
                        _fin(vec, v.h, 1)
                    self._ol_cp = v.inc()
                    block.vector(_wrap(v_fn3))
        self.op_done = (v, self._ol_cp)

    def emit_ffn(self, block, w):
        s, d = self.s, self.d
        dg, ds, v, sc, tt = (self.sm[k] for k in ["dg", "ds", "v", "sc", "t"])
        dg0, dg1 = self.sm["dg0"], self.sm["dg1"]
        w_rec, mm_rec, hev = {}, {}, {}
        h_rec = {}
        prev_out = self.out_rec.get(7, 0)
        for o in range(IC):
            dsl = dg0 if o % 2 == 0 else dg1
            def g_fn(g, o=o, dsl=dsl, w=w):
                if o < 2:
                    g.wait_ge(self.op_done[0].h, self.op_done[1])
                    if w == 0 and o == 0:
                        g.wait_ge(self.cast_rec["Wg"][0].h, self.cast_rec["Wg"][1])
                        g.wait_ge(self.cast_rec["Wu"][0].h, self.cast_rec["Wu"][1])
                if o >= 2:
                    g.wait_ge(tt.h, mm_rec[o - 2])
                self.dma_slab(g, d["Wgb"], o, "fw", (o % 2) * KC * 128
                              ).then_inc(dsl.h, 16)
                w_rec[o] = dsl.inc(16)
            block.sync(_wrap(g_fn))
            def g_fn2(g, o=o, dsl=dsl):
                if o >= 2:
                    g.wait_ge(tt.h, mm_rec[o - 2])
                self.dma_slab(g, d["Wub"], o, "fw", (2 + o % 2) * KC * 128
                              ).then_inc(dsl.h, 16)
                w_rec[(o, "u")] = dsl.inc(16)
            block.sync(_wrap(g_fn2))

            def t_fn(ten, o=o, dsl=dsl, w=w, ldev=self.last_dev):
                ten.wait_ge(dsl.h, w_rec[(o, "u")])
                if o < 2 and ldev is not None:
                    ten.wait_ge(v.h, ldev)
                if o >= 2:
                    ten.wait_ge(v.h, hev[o - 2])
                pp = self.pA if o % 2 == 0 else self.pB
                for k in range(KC):
                    wg_ = self.av("fw", (o % 2) * KC * 128 + k * 128, 128, 128, BF16)
                    wu_ = self.av("fw", (2 + o % 2) * KC * 128 + k * 128, 128, 128, BF16)
                    for half in range(2):
                        rhs = s["xl2"][:, (half * KC + k) * TT:(half * KC + k) * TT + TT]
                        ten.matmul(pp[:, 512 * half:512 * (half + 1)], wg_, rhs,
                                   start=(k == 0), stop=(k == KC - 1))
                        ten.matmul(pp[:, 1024 + 512 * half:1536 + 512 * half], wu_,
                                   rhs, start=(k == 0), stop=(k == KC - 1))
                mm_rec[o] = tt.inc()
                _fin(ten, tt.h, 1)
            block.tensor(_wrap(t_fn))

            r_si = sc.inc()
            def s_fn(scl, o=o):
                scl.wait_ge(tt.h, mm_rec[o])
                pp = self.pA if o % 2 == 0 else self.pB
                fg = self.av("fg", (o % 2) * 1024, 128, 1024, BF16)
                scl.activation(fg, pp[:, 0:1024], AF.Silu,
                               scale=s["sg"][:, o:o + 1])
                _fin(scl, sc.h, 1)
            block.scalar(_wrap(s_fn))

            r_hv = v.inc()
            hev[o] = r_hv
            def v_fn(vec, o=o, r_su=r_si, r_si=r_si):
                vec.wait_ge(sc.h, r_si)
                if o >= 2:
                    vec.wait_ge(self.sm["dh0" if o % 2 == 0 else "dh1"].h,
                                h_rec[o - 2])
                pp = self.pA if o % 2 == 0 else self.pB
                fu = self.av("fu", (o % 2) * 1024, 128, 1024, BF16)
                fg = self.av("fg", (o % 2) * 1024, 128, 1024, BF16)
                fho = self.av("fho", (o % 2) * 1024, 128, 1024, BF16)
                vec.tensor_scalar(out=fu, in0=pp[:, 1024:2048],
                                  scalar1=s["su"][:, o:o + 1], scalar2=None,
                                  op0=ALU.mult)
                vec.drain()
                vec.tensor_mul(fho, fg, fu)
                _fin(vec, v.h, 1)
            block.vector(_wrap(v_fn))

            dhp = self.sm["dh0" if o % 2 == 0 else "dh1"]
            r_hd = dhp.inc(16)
            h_rec[o] = r_hd
            def g_fn3(g, o=o, r_hv=r_hv, ldn=self.last_dn, dhp=dhp):
                g.wait_ge(v.h, r_hv)
                if o == 0 and ldn is not None:
                    g.wait_ge(tt.h, ldn)
                fho = self.av("fho", (o % 2) * 1024, 128, 1024, BF16)
                g.dma_start(out=d["h_sp"][o, :, :], in_=fho).then_inc(dhp.h, 16)
            block.sync(_wrap(g_fn3))

        sdbc = self.av("fw", 0, 128, DM, BF16)
        r_sdb = {}
        for m in range(KC):
            r_rb = v.inc()
            def v_fn2(vec, m=m):
                if m == 0:
                    vec.wait_ge(tt.h, mm_rec[IC - 1])
                if m >= 2:
                    vec.wait_ge(v.h, r_sdb[m - 2])
                rh = self.av("fg", 0, 128, 128, BF16)
                vec.tensor_scalar(out=rh, in0=s["ident"][:, :],
                                  scalar1=s["sd"][:, m:m + 1], scalar2=None,
                                  op0=ALU.mult)
                _fin(vec, v.h, 1)
            block.vector(_wrap(v_fn2))
            r_mm = tt.inc()
            def t_fn2(ten, m=m, r_rb=r_rb):
                ten.wait_ge(v.h, r_rb)
                if m == 0:
                    ten.wait_ge(v.h, hev[IC - 1])
                rh = self.av("fg", 0, 128, 128, BF16)
                pp = self.pA if m % 2 == 0 else self.pB
                ten.matmul(pp[:, 0:128],
                           s["onesbf"][:, 0:1].to_broadcast((128, 128)),
                           rh, start=True, stop=True)
                _fin(ten, tt.h, 1)
            block.tensor(_wrap(t_fn2))
            r_cp = v.inc()
            r_sdb[m] = r_cp
            def v_fn3(vec, m=m, r_mm=r_mm):
                vec.wait_ge(tt.h, r_mm)
                pp = self.pA if m % 2 == 0 else self.pB
                vec.tensor_copy(bass.AP(sdbc.tensor, sdbc.offset + 128 * m,
                                        [[sdbc.ap[0][0], 128], [1, 128]]),
                                pp[:, 0:128])
                _fin(vec, v.h, 1)
        
            block.vector(_wrap(v_fn3))

        dn_rec, dev_rec, hld_rec, wd_rec = {}, {}, {}, {}
        dsh, dsr = self.sm["dsh"], self.sm["dsr"]
        for q in range(2):
            r_hl = dsh.inc(16 * 4)
            def g_hl(g, q=q, w=w):
                if q == 0:
                    g.wait_ge(self.sm["dh0"].h, h_rec[IC - 2])
                    g.wait_ge(self.sm["dh1"].h, h_rec[IC - 1])
                else:
                    g.wait_ge(tt.h, dn_rec[(0, 1, IC - 1)])
                for tl in range(4):
                    tok = 4 * q + tl
                    fh = self.av("fh", tl * IC * 128, 128, IC * 128, BF16)
                    srcp = bass.AP(d["h_sp"], 128 * tok,
                                   [[2 * TT, 128], [128 * 2 * TT, IC], [1, 128]])
                    g.dma_start(out=bass.AP(fh.tensor, fh.offset,
                                            [[fh.ap[0][0], 128], [128, IC], [1, 128]]),
                                in_=srcp).then_inc(dsh.h, 16)
            block.sync(_wrap(g_hl))
            hld_rec[q] = r_hl

            r_r1 = self.sm["df0"].inc(16 * 4)
            def g_r1(g, q=q, w=w, pldev=self.last_dev):
                if q == 1:
                    for tl in range(4):
                        g.wait_ge(v.h, dev_rec[(0, 1, tl)])
                elif pldev is not None:
                    g.wait_ge(v.h, pldev)
                for tl in range(4):
                    tok = 4 * q + tl
                    tile = 2 * w + tok // 4
                    fr1 = self.av("fr1", tl * DM, 128, DM, BF16)
                    g.dma_start(out=fr1, in_=d["r1_sp"][tile % 2, tok % 4, :, :]
                                ).then_inc(self.sm["df0"].h, 16)
            block.sync(_wrap(g_r1))
            self._fr1w = r_r1

            for half in range(2):
                for k in range(IC):
                    dsl = self.sm[f"dgw{k % 6}"]
                    def g_wd(g, q=q, half=half, k=k, dsl=dsl, w=w):
                        if (q, half, k) == (0, 0, 0):
                            g.wait_ge(v.h, r_sdb[KC - 1])
                            if w == 0:
                                g.wait_ge(self.cast_rec["Wd"][0].h, self.cast_rec["Wd"][1])
                        pk = None
                        if k >= 6:
                            pk = (q, half, k - 6)
                        elif half == 1:
                            pk = (q, 0, IC - 6 + k)
                        elif q == 1:
                            pk = (0, 1, IC - 6 + k)
                        if pk is not None and pk in dn_rec:
                            g.wait_ge(tt.h, dn_rec[pk])
                        srcp = bass.AP(d["Wdb"], k * 128 * DM + 1024 * half,
                                      [[DM, 128], [1, 1024]])
                        g.dma_start(out=self.av("fwd", (k % 6) * 1024, 128, 1024, BF16),
                                    in_=srcp).then_inc(dsl.h, 16)
                        wd_rec[(q, half, k)] = dsl.inc(16)
                    block.sync(_wrap(g_wd))

                    def t_dn(ten, q=q, half=half, k=k, dsl=dsl):
                        ten.wait_ge(dsl.h, wd_rec[(q, half, k)])
                        if k == 0:
                            ten.wait_ge(dsh.h, hld_rec[q])
                            if (q, half) == (0, 0):
                                ten.wait_ge(v.h, hev[IC - 1])
                            else:
                                ph = (q, half - 1) if half == 1 else (q - 1, 1)
                                for tl in range(4):
                                    ten.wait_ge(v.h, dev_rec[(ph[0], ph[1], tl)])
                        wv = self.av("fwd", (k % 6) * 1024, 128, 1024, BF16)
                        for tl in range(4):
                            fh = self.av("fh", tl * IC * 128, 128, IC * 128, BF16)
                            lhsT = bass.AP(fh.tensor, fh.offset + 128 * k,
                                           [[fh.ap[0][0], 128], [1, 128]])
                            pp = self.pA if tl < 2 else self.pB
                            po = 1024 * (tl % 2)
                            for sub in range(2):
                                ten.matmul(pp[:, po + 512 * sub:po + 512 * (sub + 1)],
                                           lhsT,
                                           bass.AP(wv.tensor, wv.offset + 512 * sub,
                                                   [[wv.ap[0][0], 128], [1, 512]]),
                                           start=(k == 0), stop=(k == IC - 1))
                        dn_rec[(q, half, k)] = tt.inc()
                        _fin(ten, tt.h, 1)
                    block.tensor(_wrap(t_dn))

                for tl in range(4):
                    tok = 4 * q + tl
                    tile = 2 * w + tok // 4
                    trow = (tok % 4) * 128
                    pp = self.pA if tl < 2 else self.pB
                    po = 1024 * (tl % 2)
                    r_dev = v.inc()
                    dev_rec[(q, half, tl)] = r_dev
                    def v_dn(vec, q=q, half=half, tl=tl, pp=pp, po=po):
                        vec.wait_ge(tt.h, dn_rec[(q, half, IC - 1)])
                        vec.wait_ge(self.sm["df0"].h, self._fr1w)
                        if self.out_rec.get(("p", tl)) is not None:
                            vec.wait_ge(self.sm[f"do{tl}"].h, self.out_rec[("p", tl)])
                        fr1 = self.av("fr1", tl * DM + 1024 * half, 128, 1024, BF16)
                        fo = self.av("fo", tl * 1024, 128, 1024, F32)
                        sdb = bass.AP(sdbc.tensor, sdbc.offset + 1024 * half,
                                      [[sdbc.ap[0][0], 128], [1, 1024]])
                        vec.tensor_mul(fo, pp[:, po:po + 1024], sdb)
                        vec.drain()
                        vec.tensor_add(fo, fo, fr1)
                        _fin(vec, v.h, 1)
                    block.vector(_wrap(v_dn))

                    self.out_rec[("p", tl)] = self.sm[f"do{tl}"].inc(16)
                    def sy_o(sy, tile=tile, trow=trow, half=half, tl=tl, r_dev=r_dev):
                        sy.wait_ge(v.h, r_dev)
                        fo = self.av("fo", tl * 1024, 128, 1024, F32)
                        sy.dma_start(out=d["out"][tile, trow:trow + 128,
                                                  1024 * half:1024 * (half + 1)],
                                     in_=fo).then_inc(self.sm[f"do{tl}"].h, 16)
                    block.sync(_wrap(sy_o))
            self._fr1w = r_r1
        self.ffn_done = (self.sm["do"], self.sm["do"].n)
        self.last_dev = dev_rec[(1, 1, 3)]
        self.last_dn = dn_rec[(1, 1, IC - 1)]

    def emit_all(self, block):
        self.out_rec = {-1: 0}
        self.last_dev = None
        self.last_dn = None
        self.emit_init(block)
        for t in range(NT):
            self.emit_ln1(block, t)
            self.emit_dt(block, t)
            self.emit_inproj_xbc(block, t)
            self.emit_phaseA(block, t)
            self.emit_collective(block, t)
            self.emit_inproj_z(block, t)
            self.emit_kasm(block, t)
            for c in range(NCHUNK):
                self.emit_chunk(block, t, c)
            self.emit_outproj(block, t)
            if t % 2 == 1:
                self.emit_ffn(block, t // 2)
        dos = [self.sm[f"do{i}"] for i in range(4)]
        def sy_fin(sy):
            for dsem in dos:
                sy.wait_ge(dsem.h, dsem.n)
        block.sync(_wrap(sy_fin))


def build_program():
    nc = bass.Bass("TRN2", target_bir_lowering=False)
    p = Prog(nc)
    with nc.Block() as block:
        p.emit_all(block)
    return nc


_NC_CACHE = None


def kernel(**inputs):
    global _NC_CACHE
    shared, per_core = prep_inputs(inputs)
    if _NC_CACHE is None:
        _NC_CACHE = build_program()
    nc = _NC_CACHE
    in_maps = []
    for core in range(NCORES):
        m = dict(shared)
        m.update(per_core[core])
        in_maps.append(m)
    res = run_bass_kernel_spmd(nc, in_maps, core_ids=list(range(NCORES)))
    return unpack_output(res.results)


# version 09 (final)
# speedup vs baseline: 1.1620x; 1.1620x over previous
import numpy as np
import ml_dtypes
import concourse.bass as bass
import concourse.mybir as mybir
from concourse.bass_utils import run_bass_kernel_spmd

F32 = mybir.dt.float32
F32R = mybir.dt.float32r
BF16 = mybir.dt.bfloat16
I8 = mybir.dt.int8
AF = mybir.ActivationFunctionType
ALU = mybir.AluOpType

B = 2
L = 8192
DM = 2048
DI = 4096
DXB = 1024
DST = 128
HD = 64
NH = 64
NG = 32
CHUNK = 256
DCONV = 4
CONV_DIM = DI + 2 * DXB
DIP = DI + CONV_DIM + NH
INTER = 8192
EPS = 1e-5

NCORES = 8
NCS = 4
NT = 4
TT = 512
NCHUNK = 2
KC = DM // 128
ZC = DI // 128
XBC_C = CONV_DIM // 128
IC = INTER // 128
TOKC = TT // 128
NBG = 8

MASK_NEG = -1e30



class Sem:

    def __init__(self, nc, name):
        self.h = nc.alloc_semaphore(name)
        self.n = 0

    def inc(self, k=1):
        self.n += k
        return self.n


class _Rec:
    _PASS = {"wait_ge", "nop"}

    _DMA = {"dma_start", "dma_start_transpose", "collective_compute"}

    def __init__(self, e):
        self.e = e
        self.last = None
        self.last_dma = False

    def __getattr__(self, n):
        f = getattr(self.e, n)
        if not callable(f) or n in _Rec._PASS or n.startswith("_"):
            return f
        def g(*a, **k):
            r = f(*a, **k)
            if r is not None and hasattr(r, "then_inc"):
                self.last = r
                self.last_dma = n in _Rec._DMA
            return r
        return g


def _wrap(fn):
    return lambda e: fn(_Rec(e))


def _fin(e, h, k=1):
    li = e.last if isinstance(e, _Rec) else None
    if li is None or (isinstance(e, _Rec) and e.last_dma):
        li = (e.e if isinstance(e, _Rec) else e).drain()
    li.then_inc(h, k)



def _bf(x):
    return np.ascontiguousarray(x.astype(ml_dtypes.bfloat16))



def prep_inputs(inp):
    u = np.asarray(inp["L0_attn_u"], np.float32)
    W_in = np.asarray(inp["L0_attn_W_in"], np.float32)
    conv_w = np.asarray(inp["L0_attn_conv_w"], np.float32)
    conv_b = np.asarray(inp["L0_attn_conv_b"], np.float32)
    dt_bias = np.asarray(inp["L0_attn_dt_bias"], np.float32)
    A_log = np.asarray(inp["L0_attn_A_log"], np.float32)
    Dv = np.asarray(inp["L0_attn_D"], np.float32)
    nw = np.asarray(inp["L0_attn_norm_weight"], np.float32)
    W_out = np.asarray(inp["L0_attn_W_out"], np.float32)
    wg = np.asarray(inp["L0_ffn_w_gate"])
    wu = np.asarray(inp["L0_ffn_w_up"])
    wd = np.asarray(inp["L0_ffn_w_down"])
    sg = np.asarray(inp["L0_ffn_s_gate"], np.float32)
    su = np.asarray(inp["L0_ffn_s_up"], np.float32)
    sd = np.asarray(inp["L0_ffn_s_down"], np.float32)

    WinT = np.ascontiguousarray(W_in.T)
    Wdt = np.ascontiguousarray(
        WinT[:, DI + CONV_DIM:].reshape(KC, 128, NH))
    WxBC = np.ascontiguousarray(
        WinT[:, DI:DI + CONV_DIM].reshape(KC, 128, XBC_C, 128)
        .transpose(2, 0, 1, 3))
    Wz = np.ascontiguousarray(
        WinT[:, :DI].reshape(KC, 128, ZC, 128)
        .transpose(2, 0, 1, 3))
    WoT = np.ascontiguousarray(W_out.T.reshape(DI // 128, 128, DM))
    Wg = np.ascontiguousarray(
        wg.reshape(KC, 128, IC, 128).transpose(2, 0, 1, 3))
    Wu = np.ascontiguousarray(
        wu.reshape(KC, 128, IC, 128).transpose(2, 0, 1, 3))
    Wd = np.ascontiguousarray(wd.reshape(IC, 128, DM))

    cwp = np.ascontiguousarray(
        conv_w.reshape(XBC_C, 128, DCONV).transpose(1, 0, 2)
        .reshape(128, XBC_C * DCONV))
    cbp = np.ascontiguousarray(conv_b.reshape(XBC_C, 128).T)
    nwp = np.ascontiguousarray(nw.reshape(NG, 128).T)
    p = np.arange(128)
    dpp = np.ascontiguousarray(
        Dv[2 * np.arange(NG)[None, :] + (p[:, None] // 64)].astype(np.float32))
    sgp = np.ascontiguousarray(sg.reshape(IC, 128).T)
    sup = np.ascontiguousarray(su.reshape(IC, 128).T)
    sdp = np.ascontiguousarray(sd.reshape(KC, 128).T)

    mneg = np.zeros((128, 2 * CHUNK), np.float32)
    ll = np.arange(CHUNK)
    for st in range(2):
        s = np.arange(128) + st * 128
        mneg[:, st * CHUNK:(st + 1) * CHUNK] = np.where(
            ll[None, :] < s[:, None], MASK_NEG, 0.0)
    mneg = mneg.astype(ml_dtypes.bfloat16)
    ident_bf = np.eye(128, dtype=ml_dtypes.bfloat16)
    ident_f = np.eye(128, dtype=np.float32)
    sel64 = np.eye(64, dtype=np.float32)

    shared = dict(
        Wdt=Wdt, WxBC=WxBC, Wz=Wz, WoT=WoT, Wg=Wg, Wu=Wu, Wd=Wd,
        cwp=cwp, cbp=cbp, nwp=nwp, sgp=sgp, sup=sup, sdp=sdp,
        dtb=dt_bias.reshape(NH, 1), alog=A_log.reshape(NH, 1),
        dvec=Dv.reshape(NH, 1), dpp=dpp,
        mneg=mneg, ident=ident_bf, identf=ident_f, sel64=sel64,
    )

    per_core = []
    for core in range(NCORES):
        seq = core // NCS
        ci = core % NCS
        ut = np.empty((NT, TT, DM), np.float32)
        uh = np.zeros((NT, 3, DM), np.float32)
        for t in range(NT):
            b = 4 * t + ci
            ut[t] = u[seq, TT * b:TT * (b + 1)]
            if b > 0:
                uh[t] = u[seq, TT * b - 3:TT * b]
        msel = np.zeros((NH, NCS), np.float32)
        msel[:, :ci] = 1.0
        hsel = np.zeros((NH, NCS), np.float32)
        if ci > 0:
            hsel[:, ci - 1] = 1.0
        per_core.append(dict(ut=ut, uh=np.ascontiguousarray(uh),
                             msel=msel, hsel=hsel))
    return shared, per_core


def unpack_output(res_list):
    out = np.empty((B, L, DM), np.float32)
    for core in range(NCORES):
        seq = core // NCS
        ci = core % NCS
        o = res_list[core]["out"]
        for t in range(NT):
            b = 4 * t + ci
            out[seq, TT * b:TT * (b + 1), :] = o[t]
    return out



AR_ALIGN = 64


def _pad(n, a=AR_ALIGN):
    return (n + a - 1) // a * a


class Prog:

    def __init__(self, nc):
        self.nc = nc
        self._mk_dram()
        self._mk_sbuf()
        self._mk_psum()
        self._mk_sems()

    def _mk_dram(self):
        nc = self.nc
        d = {}
        d["ut"] = nc.dram_tensor("ut", [NT, TT, DM], F32, kind="ExternalInput")
        d["uh"] = nc.dram_tensor("uh", [NT, 3, DM], F32, kind="ExternalInput")
        d["Wdt"] = nc.dram_tensor("Wdt", [KC, 128, NH], F32, kind="ExternalInput")
        d["WxBC"] = nc.dram_tensor("WxBC", [XBC_C, KC, 128, 128], F32, kind="ExternalInput")
        d["Wz"] = nc.dram_tensor("Wz", [ZC, KC, 128, 128], F32, kind="ExternalInput")
        d["WoT"] = nc.dram_tensor("WoT", [DI // 128, 128, DM], F32, kind="ExternalInput")
        d["Wg"] = nc.dram_tensor("Wg", [IC, KC, 128, 128], I8, kind="ExternalInput")
        d["Wu"] = nc.dram_tensor("Wu", [IC, KC, 128, 128], I8, kind="ExternalInput")
        d["Wd"] = nc.dram_tensor("Wd", [IC, 128, DM], I8, kind="ExternalInput")
        d["cwp"] = nc.dram_tensor("cwp", [128, XBC_C * DCONV], F32, kind="ExternalInput")
        d["cbp"] = nc.dram_tensor("cbp", [128, XBC_C], F32, kind="ExternalInput")
        d["nwp"] = nc.dram_tensor("nwp", [128, NG], F32, kind="ExternalInput")
        d["sgp"] = nc.dram_tensor("sgp", [128, IC], F32, kind="ExternalInput")
        d["sup"] = nc.dram_tensor("sup", [128, IC], F32, kind="ExternalInput")
        d["sdp"] = nc.dram_tensor("sdp", [128, KC], F32, kind="ExternalInput")
        d["dtb"] = nc.dram_tensor("dtb", [NH, 1], F32, kind="ExternalInput")
        d["alog"] = nc.dram_tensor("alog", [NH, 1], F32, kind="ExternalInput")
        d["dvec"] = nc.dram_tensor("dvec", [NH, 1], F32, kind="ExternalInput")
        d["dpp"] = nc.dram_tensor("dpp", [128, NG], F32, kind="ExternalInput")
        d["mneg"] = nc.dram_tensor("mneg", [128, 2 * CHUNK], BF16, kind="ExternalInput")
        d["ident"] = nc.dram_tensor("ident", [128, 128], BF16, kind="ExternalInput")
        d["identf"] = nc.dram_tensor("identf", [128, 128], F32, kind="ExternalInput")
        d["sel64"] = nc.dram_tensor("sel64", [NH, NH], F32, kind="ExternalInput")
        d["msel"] = nc.dram_tensor("msel", [NH, NCS], F32, kind="ExternalInput")
        d["hsel"] = nc.dram_tensor("hsel", [NH, NCS], F32, kind="ExternalInput")
        d["out"] = nc.dram_tensor("out", [NT, TT, DM], F32, kind="ExternalOutput")
        d["h_sp"] = nc.dram_tensor("h_sp", [IC, 128, 2 * TT], BF16)
        d["Wdtb"] = nc.dram_tensor("Wdtb", [KC, 128, NH], BF16)
        d["WxBCb"] = nc.dram_tensor("WxBCb", [XBC_C, KC, 128, 128], BF16)
        d["Wzb"] = nc.dram_tensor("Wzb", [ZC, KC, 128, 128], BF16)
        d["WoTb"] = nc.dram_tensor("WoTb", [DI // 128, 128, DM], BF16)
        d["r1_sp"] = nc.dram_tensor("r1_sp", [2, TOKC, 128, DM], BF16)
        for t in range(NT):
            d[f"Sin{t}"] = nc.dram_tensor(f"Sin{t}", [128, DI], BF16)
            d[f"Sout{t}"] = nc.dram_tensor(f"Sout{t}", [NCS * 128, DI], BF16)
            d[f"CEin{t}"] = nc.dram_tensor(f"CEin{t}", [128, DI], BF16)
            d[f"CEout{t}"] = nc.dram_tensor(f"CEout{t}", [NCS * 128, DI], BF16)
            d[f"Tin{t}"] = nc.dram_tensor(f"Tin{t}", [1, NH], F32)
            d[f"Tout{t}"] = nc.dram_tensor(f"Tout{t}", [NCS, NH], F32)
        self.d = d

    def _mk_sbuf(self):
        nc = self.nc
        s = {}
        s["xl2"] = nc.alloc_sbuf_tensor("sb_xl2", [128, 2 * KC * TT], BF16)
        s["cw"] = nc.alloc_sbuf_tensor("sb_cw", [128, XBC_C * DCONV], F32)
        s["cb"] = nc.alloc_sbuf_tensor("sb_cb", [128, XBC_C], F32)
        s["nw"] = nc.alloc_sbuf_tensor("sb_nw", [128, NG], F32)
        s["sg"] = nc.alloc_sbuf_tensor("sb_sg", [128, IC], F32)
        s["su"] = nc.alloc_sbuf_tensor("sb_su", [128, IC], F32)
        s["sd"] = nc.alloc_sbuf_tensor("sb_sd", [128, KC], F32)
        s["dtb"] = nc.alloc_sbuf_tensor("sb_dtb", [NH, 1], F32)
        s["alog"] = nc.alloc_sbuf_tensor("sb_alog", [NH, 1], F32)
        s["negA"] = nc.alloc_sbuf_tensor("sb_negA", [NH, 1], F32)
        s["dvec"] = nc.alloc_sbuf_tensor("sb_dvec", [NH, 1], F32)
        s["dpair"] = nc.alloc_sbuf_tensor("sb_dpair", [128, NG], F32)
        s["mneg"] = nc.alloc_sbuf_tensor("sb_mneg", [128, 2 * CHUNK], BF16)
        s["ident"] = nc.alloc_sbuf_tensor("sb_ident", [128, 128], BF16)
        s["identf"] = nc.alloc_sbuf_tensor("sb_identf", [128, 128], F32)
        s["sel64"] = nc.alloc_sbuf_tensor("sb_sel64", [NH, NH], F32)
        s["sel64b"] = nc.alloc_sbuf_tensor("sb_sel64b", [NH, NH], BF16)
        s["msel"] = nc.alloc_sbuf_tensor("sb_msel", [NH, NCS], F32)
        s["hsel"] = nc.alloc_sbuf_tensor("sb_hsel", [NH, NCS], F32)
        s["epsc"] = nc.alloc_sbuf_tensor("sb_epsc", [128, 1], F32)
        s["onec"] = nc.alloc_sbuf_tensor("sb_onec", [128, 1], F32)
        s["negc"] = nc.alloc_sbuf_tensor("sb_negc", [128, 1], F32)
        s["nhc"] = nc.alloc_sbuf_tensor("sb_nhc", [128, 1], F32)
        s["dtraw"] = nc.alloc_sbuf_tensor("sb_dtraw", [NH, TT], F32)
        s["dt"] = nc.alloc_sbuf_tensor("sb_dt", [NH, TT], F32)
        s["dA"] = nc.alloc_sbuf_tensor("sb_dA", [NH, TT], F32)
        s["cst"] = nc.alloc_sbuf_tensor("sb_cst", [NH, TT], F32)
        s["csc"] = nc.alloc_sbuf_tensor("sb_csc", [NH, TT], F32)
        s["ones64"] = nc.alloc_sbuf_tensor("sb_ones64", [NH, TT], F32)
        s["dtds"] = s["dtraw"]
        s["expda"] = nc.alloc_sbuf_tensor("sb_expda", [NH, CHUNK], BF16)
        s["hcol"] = nc.alloc_sbuf_tensor("sb_hcol", [NH, 8], F32)
        s["tokc"] = nc.alloc_sbuf_tensor("sb_tokc", [128, 8 * NH], F32)
        s["bnst"] = nc.alloc_sbuf_tensor("sb_bnst", [128, 4 * 6], F32)
        s["bnag"] = nc.alloc_sbuf_tensor("sb_bnag", [128, 2], F32)
        s["mcol"] = nc.alloc_sbuf_tensor("sb_mcol", [128, 1], F32)
        s["rcol"] = nc.alloc_sbuf_tensor("sb_rcol", [128, 1], F32)
        s["Tg"] = nc.alloc_sbuf_tensor("sb_Tg", [NH, NCS], F32)
        s["Tcum"] = nc.alloc_sbuf_tensor("sb_Tcum", [NH, NCS], F32)
        s["ones4"] = nc.alloc_sbuf_tensor("sb_ones4", [NH, NCS], F32)
        s["fcol"] = nc.alloc_sbuf_tensor("sb_fcol", [NH, 8], F32)
        s["fbf"] = nc.alloc_sbuf_tensor("sb_fbf", [NH, 8], BF16)
        s["pscol"] = nc.alloc_sbuf_tensor("sb_pscol", [NH, 1], F32)
        s["rstd"] = nc.alloc_sbuf_tensor("sb_rstd", [128, 64], F32)
        s["selp"] = nc.alloc_sbuf_tensor("sb_selp", [NH, 128], F32)
        s["cdp"] = nc.alloc_sbuf_tensor("sb_cdp", [128, NH], BF16)
        s["r64"] = nc.alloc_sbuf_tensor("sb_r64", [NH, NH], BF16)
        s["onesbf"] = nc.alloc_sbuf_tensor("sb_onesbf", [128, 1], BF16)
        s["carry"] = nc.alloc_sbuf_tensor("sb_carry", [128, DI], BF16)
        self.s = s

        off = {}
        o = 0
        def put(name, nbytes):
            nonlocal o
            off[name] = o
            o += _pad(nbytes)
        put("xfm", NBG * TT * 2)
        put("bfm", NBG * TT * 2)
        put("cfm", NG * TT * 2)
        put("zy", ZC * TT * 2)
        put("xln", KC * (TT + 3) * 2)
        put("pc", 3 * (TT + 3) * 2)
        put("xtok", DI * 2)
        put("xds", DI * 2)
        put("btok", DXB * 2)
        put("gl", 4 * CHUNK * 2)
        put("wbuf", 3 * KC * 128 * 2)
        put("ust", DM * 2)
        put("r1", DM * 2)
        off["ln1"] = off["r1"]
        put("scr", 6 * TT * 2)
        m_end = o
        f_o = 0
        def putf(name, nbytes):
            nonlocal f_o
            off[name] = f_o
            f_o += _pad(nbytes)
        putf("fh", 4 * IC * 128 * 2)
        putf("fw", 6 * KC * 128 * 2)
        putf("fwd", 6 * 1024 * 2)
        putf("fr1", 4 * DM * 2)
        putf("fo", 4 * 1024 * 4)
        putf("fg", 2 * 1024 * 2)
        putf("fu", 2 * 1024 * 2)
        putf("fho", 2 * 1024 * 2)
        f_end = f_o
        ar_bytes = _pad(max(m_end, f_end))
        self.off = off
        self.ar_bytes = ar_bytes
        self.arena = nc.alloc_sbuf_tensor("arena", [128, ar_bytes], mybir.dt.uint8)

    def pv(self, pool, off_f32, P, E, dtype):
        if dtype == F32:
            return pool[0:P, off_f32:off_f32 + E]
        ap = pool[0:P, off_f32:off_f32 + (E + 1) // 2].bitcast(BF16)
        return bass.AP(ap.tensor, ap.offset, [[ap.ap[0][0], P], [1, E]])

    def av(self, name, eoff, P, E, dtype, poff=0):
        esz = {F32: 4, BF16: 2, F32R: 4}[dtype]
        b0 = self.off[name] + eoff * esz
        return self.arena[poff:poff + P, b0:b0 + E * esz].bitcast(dtype)

    def _mk_psum(self):
        nc = self.nc
        self.pA = nc.alloc_psum_tensor("pA", [128, 2048], F32)
        self.pB = nc.alloc_psum_tensor("pB", [128, 2048], F32)

    def _mk_sems(self):
        nc = self.nc
        self.sm = {k: Sem(nc, k) for k in
                   ["dg", "dg0", "dg1", "dh0", "dh1", "ds", "du", "dtn",
                    "dr1", "df0", "df1", "do", "t", "v", "sc", "gp", "cc",
                    "fw", "fv", "dg2", "dsh", "dsr"] +
                   [f"do{i}" for i in range(4)] +
                   ["dfg0", "dfg1"] +
                   [f"dgw{i}" for i in range(6)] +
                   [f"dcast{i}" for i in range(7)]}


    def barrier(self, block, engines=("tensor", "vector", "scalar", "gpsimd", "sync")):
        sm = self.sm["fv"]
        base = sm.n
        tgt = base + len(engines)
        sm.n = tgt
        for e in engines:
            def f(eng, _t=tgt):
                _fin(eng, sm.h, 1)
                eng.wait_ge(sm.h, _t)
            getattr(block, e)(f)

    def tap(self, block, name, ap, wait=None):
        if name not in self.taps:
            return
        nc = self.nc
        t = nc.dram_tensor("tap_" + name, list(ap.shape), ap.dtype,
                           kind="ExternalOutput")
        sm = self.sm["ds"]
        tgt = sm.inc(16)
        def f(eng):
            if wait is not None:
                eng.wait_ge(wait[0].h, wait[1])
            eng.dma_start(out=t[tuple(slice(None) for _ in ap.shape)],
                          in_=ap).then_inc(sm.h, 16)
            eng.wait_ge(sm.h, tgt)
        block.sync(_wrap(f))

    def emit_init(self, block):
        nc, s, d = self.nc, self.s, self.d
        dg = self.sm["dg"]
        gp = self.sm["gp"]
        sc = self.sm["sc"]

        def g_fn(g):
            for name in ["cwp", "cbp", "nwp", "sgp", "sup", "sdp", "dtb",
                         "alog", "dvec", "dpp", "mneg", "ident", "identf",
                         "sel64", "msel", "hsel"]:
                dst = {"cwp": s["cw"], "cbp": s["cb"], "nwp": s["nw"],
                       "sgp": s["sg"], "sup": s["su"], "sdp": s["sd"],
                       "dtb": s["dtb"], "alog": s["alog"], "dvec": s["dvec"],
                       "dpp": s["dpair"], "mneg": s["mneg"],
                       "ident": s["ident"], "identf": s["identf"],
                       "sel64": s["sel64"], "msel": s["msel"],
                       "hsel": s["hsel"]}[name]
                sh = dst.shape
                g.dma_start(out=dst[:sh[0], :sh[1]],
                            in_=d[name][:sh[0], :sh[1]]).then_inc(dg.h, 16)
            dg.inc(16 * 16)
            g.memset(s["epsc"][:, :], EPS)
            g.memset(s["onec"][:, :], 1.0)
            g.memset(s["negc"][:, :], -1.0)
            g.memset(s["nhc"][:, :], -0.5)
            g.memset(s["ones64"][:NH, :], 1.0)
            g.memset(s["ones4"][:NH, :], 1.0)
            g.memset(s["onesbf"][:, :], 1.0)
            zv = self.av("xds", 0, 128, DI, BF16)
            g.memset(zv, 0.0)
            g.drain()
            g.dma_start(out=d["CEin0"][:, :], in_=zv).then_inc(dg.h, 16)
            dg.inc(16)
            self.cast_rec = {}
            self._ncast = [0]
            def cast(srcs, dsts, n, name):
                dc = self.sm[f"dcast{self._ncast[0]}"]
                self._ncast[0] += 1
                for i in range(n):
                    g.dma_start(out=dsts[i], in_=srcs[i]).then_inc(dc.h, 16)
                    dc.inc(16)
                self.cast_rec[name] = (dc, dc.n)
            cast([d["Wdt"][:, :, :]], [d["Wdtb"][:, :, :]], 1, "Wdt")
            cast([d["WxBC"][o, :, :, :] for o in range(XBC_C)],
                 [d["WxBCb"][o, :, :, :] for o in range(XBC_C)], XBC_C, "WxBC")
            cast([d["Wz"][o, :, :, :] for o in range(ZC)],
                 [d["Wzb"][o, :, :, :] for o in range(ZC)], ZC, "Wz")
            cast([d["WoT"][k, :, :] for k in range(DI // 128)],
                 [d["WoTb"][k, :, :] for k in range(DI // 128)], DI // 128, "WoT")
            _fin(g, gp.h, 1)
        gp.inc()
        block.gpsimd(_wrap(g_fn))

        def s_fn(scl):
            scl.wait_ge(dg.h, dg.n)
            scl.wait_ge(gp.h, self.sm["gp"].n)
            scl.activation(s["negA"][:NH, :], s["alog"][:NH, :], AF.Exp)
            scl.drain()
            scl.mul(s["negA"][:NH, :], s["negA"][:NH, :], s["negc"][:NH, 0:1])
            _fin(scl, sc.h, 1)
        sc.inc()
        block.scalar(_wrap(s_fn))

        v = self.sm["v"]
        def v_fn(vec):
            vec.wait_ge(dg.h, dg.n)
            vec.tensor_copy(s["sel64b"][:NH, :], s["sel64"][:NH, :])
            _fin(vec, v.h, 1)
        v.inc()
        block.vector(_wrap(v_fn))
        self.barrier(block)


    def xln_v(self, k):
        return self.av("xln", k * (TT + 3), 128, TT + 3, BF16)

    def wb_v(self, buf, k, M=128):
        return self.av("wbuf", buf * (KC * 128) + k * 128, 128, M, BF16)

    def dma_slab(self, g, dram, o, dstname, dstoff, M=128, kc=KC):
        src = bass.AP(dram, o * kc * 128 * M, [[M, 128], [128 * M, kc], [1, M]])
        dv = self.av(dstname, dstoff, 128, kc * M, BF16)
        dst = bass.AP(dv.tensor, dv.offset, [[dv.ap[0][0], 128], [M, kc], [1, M]])
        return g.dma_start(out=dst, in_=src)

    def emit_ln1(self, block, t):
        s, d = self.s, self.d
        ds, v, sc, tt = self.sm["ds"], self.sm["v"], self.sm["sc"], self.sm["t"]
        ust = self.av("ust", 0, 128, DM, BF16)
        du = self.sm["du"]
        v_snap = self.sm["v"].n
        apply_rec = {}
        trans_rec = {}
        copy_rec = {}
        chunks = list(range(-1, TOKC))
        for ci, tc in enumerate(chunks):
            P = 3 if tc < 0 else 128
            par = ci % 2
            ln1 = self.av("ln1", par * DM, P, DM, BF16)
            r_dma = du.inc(16)
            def sy_fn(sy, tc=tc, prev=chunks[ci - 1] if ci >= 1 else None):
                if prev is not None:
                    sy.wait_ge(v.h, apply_rec[prev])
                else:
                    sy.wait_ge(v.h, v_snap)
                if tc < 0:
                    sy.dma_start(out=ust[0:3, :], in_=d["uh"][t, :, :]).then_inc(du.h, 16)
                else:
                    sy.dma_start(out=ust, in_=d["ut"][t, 128 * tc:128 * (tc + 1), :]).then_inc(du.h, 16)
            block.gpsimd(_wrap(sy_fn))

            r_stats = v.inc()
            def v_fn(vec, P=P, r_dma=r_dma):
                vec.wait_ge(du.h, r_dma)
                for q in range(4):
                    vec.bn_stats(s["bnst"][:P, 6 * q:6 * (q + 1)],
                                 ust[:P, 512 * q:512 * (q + 1)])
                vec.drain()
                vec.bn_aggr(s["bnag"][:P, :], s["bnst"][:P, :])
                _fin(vec, v.h, 1)
            block.vector(_wrap(v_fn))

            r_sqrt = sc.inc()
            def s_fn(scl, P=P, r_stats=r_stats):
                scl.wait_ge(v.h, r_stats)
                scl.activation(s["rcol"][:P, :], s["bnag"][:P, 1:2], AF.Ln,
                               bias=s["epsc"][:P, 0:1])
                scl.drain()
                scl.activation(s["rcol"][:P, :], s["rcol"][:P, :], AF.Exp,
                               scale=s["nhc"][:P, 0:1])
                _fin(scl, sc.h, 1)
            block.scalar(_wrap(s_fn))

            r_apply = v.inc()
            apply_rec[tc] = r_apply
            def v_fn2(vec, P=P, ln1=ln1, r_sqrt=r_sqrt,
                      guard=chunks[ci - 2] if ci >= 2 else None):
                vec.wait_ge(sc.h, r_sqrt)
                if guard is not None:
                    vec.wait_ge(tt.h, trans_rec[guard])
                vec.tensor_scalar(out=ln1[:P, :], in0=ust[:P, :],
                                  scalar1=s["bnag"][:P, 0:1],
                                  scalar2=s["rcol"][:P, 0:1],
                                  op0=ALU.subtract, op1=ALU.mult)
                _fin(vec, v.h, 1)
            block.vector(_wrap(v_fn2))

            for g in range(4):
                r_tr = tt.inc()
                def t_fn(ten, g=g, tc=tc, P=P, ln1=ln1, r_apply=r_apply,
                         prevkey=(tc, g) if False else None, ci=ci):
                    ten.wait_ge(v.h, r_apply)
                    if g >= 1:
                        ten.wait_ge(v.h, copy_rec[(tc, g - 1)])
                    elif ci >= 1:
                        ten.wait_ge(v.h, copy_rec[(chunks[ci - 1], 3)])
                    for j in range(4):
                        k = 4 * g + j
                        ps = self.pv(self.pB, 1024 + 64 * j, 128, P, BF16)
                        ten.transpose(ps, ln1[:P, 128 * k:128 * (k + 1)],
                                      s["ident"][:P, :P])
                    _fin(ten, tt.h, 1)
                block.tensor(_wrap(t_fn))
                if g == 3:
                    trans_rec[tc] = r_tr

                r_cp = v.inc()
                copy_rec[(tc, g)] = r_cp
                def v_fn3(vec, g=g, tc=tc, P=P, r_tr=r_tr):
                    vec.wait_ge(tt.h, r_tr)
                    col = 0 if tc < 0 else 3 + 128 * tc
                    for j in range(4):
                        k = 4 * g + j
                        ps = self.pv(self.pB, 1024 + 64 * j, 128, P, BF16)
                        vec.tensor_copy(self.xln_v(k)[:, col:col + P], ps)
                    _fin(vec, v.h, 1)
                block.vector(_wrap(v_fn3))
        self.ln1_done = (v, v.n)

    def emit_dt(self, block, t):
        s, d = self.s, self.d
        dg, v, sc, tt = self.sm["dg"], self.sm["v"], self.sm["sc"], self.sm["t"]
        dg0 = self.sm["dg0"]
        r_w = dg0.inc(16)
        tt_snap = tt.n
        def g_fn(g):
            g.wait_ge(tt.h, tt_snap)
            g.wait_ge(self.cast_rec["Wdt"][0].h, self.cast_rec["Wdt"][1])
            self.dma_slab(g, d["Wdtb"], 0, "wbuf", 0, M=NH).then_inc(dg0.h, 16)
        block.sync(_wrap(g_fn))

        r_mm = tt.inc()
        self.dt_mm = r_mm
        def t_fn(ten):
            ten.wait_ge(self.sm["dg0"].h, r_w)
            ten.wait_ge(self.ln1_done[0].h, self.ln1_done[1])
            for k in range(KC):
                wv = self.av("wbuf", k * NH, 128, NH, BF16)
                ten.matmul(self.pA[0:NH, 0:TT], wv, self.xln_v(k)[:, 3:3 + TT],
                           start=(k == 0), stop=(k == KC - 1))
            _fin(ten, tt.h, 1)
        block.tensor(_wrap(t_fn))

        r_dt = sc.inc()
        r_dtv = v.inc()
        self.dt_evac = r_dtv
        def v_fn(vec):
            vec.wait_ge(tt.h, r_mm)
            vec.tensor_copy(s["dtraw"][:NH, :], self.pA[0:NH, 0:TT])
            _fin(vec, v.h, 1)
        block.vector(_wrap(v_fn))

        def s_fn(scl):
            scl.wait_ge(v.h, r_dtv)
            scl.activation(s["dt"][:NH, :], s["dtraw"][:NH, :], AF.Identity,
                           bias=s["dtb"][:NH, 0:1])
            scl.drain()
            scl.activation(s["dtds"][:NH, :], s["dt"][:NH, :], AF.Abs)
            scl.drain()
            scl.activation(s["dtds"][:NH, :], s["dtds"][:NH, :], AF.Exp,
                           scale=s["negc"][:NH, 0:1])
            scl.drain()
            scl.activation(s["dtds"][:NH, :], s["dtds"][:NH, :], AF.Ln,
                           bias=s["onec"][:NH, 0:1])
            scl.drain()
            scl.activation(s["dt"][:NH, :], s["dt"][:NH, :], AF.Relu)
            _fin(scl, sc.h, 1)
        block.scalar(_wrap(s_fn))

        r_cs = v.inc()
        def v_fn2(vec):
            vec.wait_ge(sc.h, r_dt)
            vec.tensor_add(s["dt"][:NH, :], s["dt"][:NH, :], s["dtds"][:NH, :])
            vec.drain()
            vec.tensor_scalar(out=s["dA"][:NH, :], in0=s["dt"][:NH, :],
                              scalar1=s["negA"][:NH, 0:1], scalar2=None,
                              op0=ALU.mult)
            vec.drain()
            vec.tensor_tensor_scan(s["cst"][:NH, :], s["ones64"][:NH, :],
                                   s["dA"][:NH, :], 0.0, ALU.mult, ALU.add)
            for c in range(NCHUNK):
                vec.tensor_tensor_scan(
                    s["csc"][:NH, CHUNK * c:CHUNK * (c + 1)],
                    s["ones64"][:NH, CHUNK * c:CHUNK * (c + 1)],
                    s["dA"][:NH, CHUNK * c:CHUNK * (c + 1)],
                    0.0, ALU.mult, ALU.add)
            _fin(vec, v.h, 1)
        block.vector(_wrap(v_fn2))

        r_dtds = sc.inc()
        def s_fn2(scl):
            scl.wait_ge(v.h, r_cs)
            scl.activation(s["dtds"][:NH, :], s["cst"][:NH, :], AF.Exp,
                           scale=s["negc"][:NH, 0:1],
                           bias=s["cst"][:NH, TT - 1:TT])
            _fin(scl, sc.h, 1)
        block.scalar(_wrap(s_fn2))

        r_dtds2 = v.inc()
        def v_fn3(vec):
            vec.wait_ge(sc.h, r_dtds)
            vec.tensor_mul(s["dtds"][:NH, :], s["dtds"][:NH, :], s["dt"][:NH, :])
            _fin(vec, v.h, 1)
        block.vector(_wrap(v_fn3))

        dtn = self.sm["dtn"]
        r_ts = dtn.inc(16)
        def sy_fn(sy):
            sy.wait_ge(v.h, r_dtds2)
            sy.dma_start(out=d[f"Tin{t}"][0:1, 0:NH],
                         in_=s["cst"][0:NH, TT - 1:TT]).then_inc(dtn.h, 16)
        block.sync(_wrap(sy_fn))
        self.tin_rec = r_ts
        self.dt_done = (v, r_dtds2)

    def emit_inproj_xbc(self, block, t):
        s = self.s
        dg, v, sc, tt = self.sm["dg"], self.sm["v"], self.sm["sc"], self.sm["t"]
        NO = XBC_C
        GRP = 3
        ngrp = NO // GRP
        w_rec, mm_rec, ev_rec, cv_rec, si_rec = {}, {}, {}, {}, {}
        stp = TT + 3

        for gi in range(ngrp):
            ppool = self.pA if gi % 2 == 0 else self.pB
            for j in range(GRP):
                o = gi * GRP + j

                def g_fn(g, o=o, t=t):
                    dsl = self.sm[f"dg{o % 3}"]
                    if o < 3:
                        g.wait_ge(tt.h, self.dt_mm)
                        if t == 0 and o == 0:
                            g.wait_ge(self.cast_rec["WxBC"][0].h, self.cast_rec["WxBC"][1])
                    if o - 3 in mm_rec:
                        g.wait_ge(tt.h, mm_rec[o - 3])
                    self.dma_slab(g, self.d["WxBCb"], o, "wbuf",
                                  (o % 3) * KC * 128).then_inc(dsl.h, 16)
                    w_rec[o] = dsl.inc(16)
                block.sync(_wrap(g_fn))

                def t_fn(ten, gi=gi, j=j, o=o, ppool=ppool):
                    dsl = self.sm[f"dg{o % 3}"]
                    if o == 0:
                        ten.wait_ge(v.h, self.dt_evac)
                    ten.wait_ge(dsl.h, w_rec[o])
                    if j == 0 and gi >= 2:
                        ten.wait_ge(sc.h, ev_rec[gi - 2])
                    main = ppool[:, 512 * j:512 * (j + 1)]
                    halo = ppool[:, 1536 + 128 * j:1536 + 128 * j + 3]
                    for k in range(KC):
                        wv = self.wb_v(o % 3, k)
                        ten.matmul(main, wv, self.xln_v(k)[:, 3:3 + TT],
                                   start=(k == 0), stop=(k == KC - 1))
                        ten.matmul(halo, wv, self.xln_v(k)[:, 0:3],
                                   start=(k == 0), stop=(k == KC - 1))
                    mm_rec[o] = tt.inc()
                    _fin(ten, tt.h, 1)
                block.tensor(_wrap(t_fn))

            def s_fn(scl, gi=gi, ppool=ppool):
                scl.wait_ge(tt.h, mm_rec[gi * GRP + GRP - 1])
                if gi >= 1:
                    scl.wait_ge(sc.h, si_rec[gi - 1])
                for j in range(GRP):
                    main = ppool[:, 512 * j:512 * (j + 1)]
                    halo = ppool[:, 1536 + 128 * j:1536 + 128 * j + 3]
                    pc = self.av("pc", j * stp, 128, stp, BF16)
                    scl.copy(pc[:, 0:3], halo)
                    scl.copy(pc[:, 3:3 + TT], main)
                ev_rec[gi] = sc.inc()
                _fin(scl, sc.h, 1)
            block.scalar(_wrap(s_fn))

            def v_fn(vec, gi=gi):
                vec.wait_ge(sc.h, ev_rec[gi])
                o0 = gi * GRP
                cw = s["cw"]
                for j in range(GRP):
                    o = o0 + j
                    pcj = self.av("pc", j * stp, 128, stp, BF16)
                    mt = [self.av("scr", q * TT, 128, TT, BF16) for q in range(4)]
                    cvt = self.av("scr", 4 * TT, 128, 2 * TT, BF16)
                    for kk in range(4):
                        vec.tensor_scalar(out=mt[kk][:, :], in0=pcj[:, kk:kk + TT],
                                          scalar1=cw[:, 4 * o + kk:4 * o + kk + 1],
                                          scalar2=None, op0=ALU.mult)
                    vec.drain()
                    vec.tensor_add(cvt[:, 0:TT], mt[0][:, :], mt[1][:, :])
                    vec.tensor_add(cvt[:, TT:2 * TT], mt[2][:, :], mt[3][:, :])
                    vec.drain()
                    vec.tensor_add(pcj[:, 0:TT], cvt[:, 0:TT], cvt[:, TT:2 * TT])
                    vec.drain()
                cv_rec[gi] = v.inc()
                _fin(vec, v.h, 1)
            block.vector(_wrap(v_fn))

            def s_fn2(scl, gi=gi):
                scl.wait_ge(v.h, cv_rec[gi])
                for j in range(GRP):
                    o = gi * GRP + j
                    pcj = self.av("pc", j * stp, 128, stp, BF16)
                    if o < 8:
                        dst = self.av("xfm", o * TT, 128, TT, BF16)
                    elif o < 16:
                        dst = self.av("bfm", (o - 8) * TT, 128, TT, BF16)
                    else:
                        dst = self.av("cfm", (o - 16) * TT, 128, TT, BF16)
                    scl.activation(dst, pcj[:, 0:TT], AF.Silu,
                                   bias=s["cb"][:, o:o + 1])
                si_rec[gi] = sc.inc()
                _fin(scl, sc.h, 1)
            block.scalar(_wrap(s_fn2))
        self.xbc_done = (sc, si_rec[ngrp - 1])
        self.xbc_mm_done = (tt, mm_rec[NO - 1])

    def emit_tapbar(self, block, specs):
        self.barrier(block)
        ds = self.sm["ds"]
        def sy_fn(sy):
            for name, ap in specs:
                tt_ = self.nc.dram_tensor("tap_" + name, list(ap.shape),
                                          ap.dtype, kind="ExternalOutput")
                idx = tuple(slice(None) for _ in ap.shape)
                sy.dma_start(out=tt_[idx], in_=ap).then_inc(ds.h, 16)
                ds.inc(16)
            sy.wait_ge(ds.h, ds.n)
        block.sync(_wrap(sy_fn))

    def _transpose_batch(self, block, items, pbank_off=0, pool=None):
        tt, v = self.sm["t"], self.sm["v"]
        pool = pool or self.pA
        for g0 in range(0, len(items), 4):
            grp = items[g0:g0 + 4]
            def t_fn(ten, grp=grp, g0=g0):
                if g0 >= 4:
                    ten.wait_ge(v.h, self._tb_cp)
                for j, (ina, outa, dt_) in enumerate(grp):
                    ps = self.pv(pool, pbank_off + 64 * j, ina.shape[1],
                                 ina.shape[0], dt_)
                    idn = self.s["ident"] if dt_ == BF16 else self.s["identf"]
                    ten.transpose(ps, ina, idn[:ina.shape[0], :ina.shape[0]])
                _fin(ten, tt.h, 1)
            self._tb_tr = tt.inc()
            block.tensor(_wrap(t_fn))
            r_tr = self._tb_tr
            def v_fn(vec, grp=grp, r_tr=r_tr):
                vec.wait_ge(tt.h, r_tr)
                for j, (ina, outa, dt_) in enumerate(grp):
                    ps = self.pv(pool, pbank_off + 64 * j, ina.shape[1],
                                 ina.shape[0], dt_)
                    vec.tensor_copy(outa, ps)
                _fin(vec, v.h, 1)
            self._tb_cp = v.inc()
            block.vector(_wrap(v_fn))

    def emit_phaseA(self, block, t):
        s, d = self.s, self.d
        dg, ds, v, sc, tt = (self.sm[k] for k in ["dg", "ds", "v", "sc", "t"])
        xtok = lambda st: self.av("xtok", st * DXB, 128, DXB, BF16)
        btokf = lambda st: self.av("zy", st * DXB, 128, DXB, BF16)
        xds = self.av("xds", 0, 128, DI, BF16)

        if t > 0:
            r_ce = dg.inc(16)
            def g_ce(g):
                g.dma_start(out=d[f"CEin{t}"][:, :],
                            in_=s["carry"][:, :]).then_inc(dg.h, 16)
            block.gpsimd(_wrap(g_ce))
            self.ce_sent = (dg, r_ce)
        else:
            self.ce_sent = (dg, dg.n)

        def t_gate(ten):
            ten.wait_ge(self.xbc_done[0].h, self.xbc_done[1])
        block.tensor(_wrap(t_gate))

        items = []
        for st in range(4):
            for g_ in range(NBG):
                items.append((self.av("xfm", g_ * TT + 128 * st, 128, 128, BF16),
                              bass.AP(xtok(st).tensor, xtok(st).offset + g_ * 128,
                                      [[xtok(st).ap[0][0], 128], [1, 128]]),
                              BF16))
        for st in range(4):
            for g_ in range(NBG):
                items.append((self.av("bfm", g_ * TT + 128 * st, 128, 128, BF16),
                              bass.AP(btokf(st).tensor, btokf(st).offset + g_ * 128,
                                      [[btokf(st).ap[0][0], 128], [1, 128]]),
                              BF16))
        for st in range(4):
            items.append((s["dtds"][0:NH, 128 * st:128 * (st + 1)],
                          s["tokc"][0:128, NH * st:NH * (st + 1)], F32))
        self._tb_cp = v.n
        self._transpose_batch(block, items)
        r_tp = self._tb_cp

        smm = None
        for st in range(4):
            r_xds = v.inc()
            def v_fn(vec, st=st, r_xds=r_xds, smm=smm):
                if st == 0:
                    vec.wait_ge(v.h, r_tp)
                else:
                    vec.wait_ge(tt.h, smm)
                for h in range(NH):
                    xc = (h // 8) * 128 + 64 * (h % 2)
                    vec.tensor_scalar(
                        out=bass.AP(xds.tensor, xds.offset + 64 * h,
                                    [[xds.ap[0][0], 128], [1, 64]]),
                        in0=bass.AP(xtok(st).tensor, xtok(st).offset + xc,
                                    [[xtok(st).ap[0][0], 128], [1, 64]]),
                        scalar1=s["tokc"][0:128, NH * st + h:NH * st + h + 1],
                        scalar2=None, op0=ALU.mult)
                _fin(vec, v.h, 1)
            block.vector(_wrap(v_fn))

            smm = tt.inc()
            def t_fn(ten, st=st, r_xds=r_xds):
                ten.wait_ge(v.h, r_xds)
                for bg in range(NBG):
                    ps = self.pA[:, 512 * bg:512 * (bg + 1)] if bg < 4 else \
                        self.pB[:, 512 * (bg - 4):512 * (bg - 3)]
                    lhs = bass.AP(btokf(st).tensor, btokf(st).offset + bg * 128,
                                  [[btokf(st).ap[0][0], 128], [1, 128]])
                    rhs = bass.AP(xds.tensor, xds.offset + 512 * bg,
                                  [[xds.ap[0][0], 128], [1, 512]])
                    ten.matmul(ps, lhs, rhs, start=(st == 0), stop=(st == 3))
                _fin(ten, tt.h, 1)
            block.tensor(_wrap(t_fn))

        r_sev = v.inc()
        def v_fn2(vec, smm=smm):
            vec.wait_ge(tt.h, smm)
            for bg in range(NBG):
                ps = self.pA[:, 512 * bg:512 * (bg + 1)] if bg < 4 else \
                    self.pB[:, 512 * (bg - 4):512 * (bg - 3)]
                vec.tensor_copy(bass.AP(xds.tensor, xds.offset + 512 * bg,
                                        [[xds.ap[0][0], 128], [1, 512]]), ps)
            _fin(vec, v.h, 1)
        block.vector(_wrap(v_fn2))

        r_ssend = dg.inc(16)
        def g_fn(g):
            g.wait_ge(v.h, r_sev)
            g.dma_start(out=d[f"Sin{t}"][:, :], in_=xds).then_inc(dg.h, 16)
        block.gpsimd(_wrap(g_fn))
        self.s_sent = (dg, r_ssend)
        self.sA_done = (v, r_sev)

    def emit_collective(self, block, t):
        d = self.d
        cc, dg, ds = self.sm["cc"], self.sm["dg"], self.sm["ds"]
        groups = [[0, 1, 2, 3], [4, 5, 6, 7]]
        r_cc = cc.inc(3)
        def g_fn(g):
            g.wait_ge(dg.h, max(self.s_sent[1], self.ce_sent[1]))
            g.wait_ge(self.sm["dtn"].h, self.tin_rec)
            for a, b in [(f"Sin{t}", f"Sout{t}"), (f"CEin{t}", f"CEout{t}"),
                         (f"Tin{t}", f"Tout{t}")]:
                g.collective_compute(
                    "AllGather", ALU.bypass, replica_groups=groups,
                    ins=[d[a][:, :].opt()], outs=[d[b][:, :].opt()],
                ).then_inc(cc.h, 1)
        block.gpsimd(_wrap(g_fn))
        self.cc_done = (cc, r_cc)

    def emit_inproj_z(self, block, t):
        dg0, dg1, v, sc, tt = (self.sm[k] for k in
                               ["dg0", "dg1", "v", "sc", "t"])
        w_rec, mm_rec, si_rec = {}, {}, {}
        for o in range(ZC):
            dsl = self.sm[f"dg{o % 3}"]
            def g_fn(g, o=o, dsl=dsl, t=t):
                if o < 3:
                    g.wait_ge(tt.h, self.xbc_mm_done[1])
                    if t == 0 and o == 0:
                        g.wait_ge(self.cast_rec["Wz"][0].h, self.cast_rec["Wz"][1])
                else:
                    g.wait_ge(tt.h, mm_rec[o - 3])
                self.dma_slab(g, self.d["Wzb"], o, "wbuf",
                              (o % 3) * KC * 128).then_inc(dsl.h, 16)
                w_rec[o] = dsl.inc(16)
            block.sync(_wrap(g_fn))

            def t_fn(ten, o=o, dsl=dsl):
                ten.wait_ge(dsl.h, w_rec[o])
                if o < 2:
                    ten.wait_ge(self.sA_done[0].h, self.sA_done[1])
                else:
                    ten.wait_ge(self.sm["sc"].h, si_rec[o - 2])
                ps = self.pA[:, 512 * (o % 2):512 * (o % 2) + 512]
                for k in range(KC):
                    ten.matmul(ps, self.wb_v(o % 3, k),
                               self.xln_v(k)[:, 3:3 + TT],
                               start=(k == 0), stop=(k == KC - 1))
                mm_rec[o] = tt.inc()
                _fin(ten, tt.h, 1)
            block.tensor(_wrap(t_fn))

            def s_fn(scl, o=o):
                scl.wait_ge(tt.h, mm_rec[o])
                ps = self.pA[:, 512 * (o % 2):512 * (o % 2) + 512]
                scl.activation(self.av("zy", o * TT, 128, TT, BF16), ps,
                               AF.Silu)
                si_rec[o] = self.sm["sc"].inc()
                _fin(scl, self.sm["sc"].h, 1)
            block.scalar(_wrap(s_fn))
        self.z_done = (self.sm["sc"], si_rec[ZC - 1])

    def emit_kasm(self, block, t):
        s, d = self.s, self.d
        dg, ds, v, sc, tt, cc = (self.sm[k] for k in
                                 ["dg", "ds", "v", "sc", "t", "cc"])
        recv = self.av("xds", 0, 128, DI, BF16)
        tmp = self.av("scr", 0, 128, 512, BF16)

        dtn = self.sm["dtn"]
        r_tg = dtn.inc(16)
        def sy_fn(sy):
            sy.wait_ge(cc.h, self.cc_done[1])
            src = bass.AP(d[f"Tout{t}"], 0, [[1, NH], [NH, NCS]])
            sy.dma_start(out=s["Tg"][0:NH, 0:NCS], in_=src,
                         allow_slow_non_contiguous=True).then_inc(dtn.h, 16)
        block.sync(_wrap(sy_fn))

        r_f = v.inc()
        def v_fn(vec):
            vec.wait_ge(dtn.h, r_tg)
            vec.tensor_tensor_scan(s["Tcum"][:NH, :], s["ones4"][:NH, :NCS],
                                   s["Tg"][:NH, :], 0.0, ALU.mult, ALU.add)
            vec.drain()
            vec.tensor_mul(s["fcol"][:NH, 0:NCS], s["hsel"][:NH, :],
                           s["Tcum"][:NH, :])
            vec.drain()
            vec.tensor_reduce(s["pscol"][:NH, 0:1], s["fcol"][:NH, 0:NCS],
                              axis=mybir.AxisListType.X, op=ALU.add)
            _fin(vec, v.h, 1)
        block.vector(_wrap(v_fn))

        r_cl = v.inc()
        def v_cl(vec):
            vec.wait_ge(v.h, r_f)
            for j in range(3):
                vec.tensor_scalar(out=s["fcol"][:NH, 1 + j:2 + j],
                                  in0=s["Tcum"][:NH, j:j + 1],
                                  scalar1=s["pscol"][:NH, 0:1], scalar2=0.0,
                                  op0=ALU.subtract, op1=ALU.max)
            _fin(vec, v.h, 1)
        block.vector(_wrap(v_cl))
        r_fs = sc.inc()
        def s_fn(scl):
            scl.wait_ge(v.h, r_cl)
            scl.activation(s["fcol"][:NH, 0:1], s["pscol"][:NH, 0:1], AF.Exp)
            for j in range(3):
                scl.drain()
                scl.activation(s["fcol"][:NH, 1 + j:2 + j],
                               s["fcol"][:NH, 1 + j:2 + j], AF.Exp,
                               scale=s["negc"][:NH, 0:1])
            _fin(scl, sc.h, 1)
        block.scalar(_wrap(s_fn))

        r_fb = v.inc()
        def v_fn2(vec):
            vec.wait_ge(sc.h, r_fs)
            for j in range(3):
                vec.tensor_mul(s["fcol"][:NH, 1 + j:2 + j],
                               s["fcol"][:NH, 1 + j:2 + j],
                               s["msel"][:NH, j:j + 1])
            vec.drain()
            vec.tensor_copy(s["fbf"][:NH, 0:4], s["fcol"][:NH, 0:4])
            _fin(vec, v.h, 1)
        block.vector(_wrap(v_fn2))

        last_v = r_fb
        for q in range(4):
            src = d[f"CEout{t}"] if q == 0 else d[f"Sout{t}"]
            row0 = 3 * 128 if q == 0 else (q - 1) * 128
            r_recv = dg.inc(16)
            def g_fn(g, src=src, row0=row0, last_v=last_v, q=q):
                if q == 0:
                    g.wait_ge(cc.h, self.cc_done[1])
                    g.wait_ge(dg.h, self.s_sent[1])
                g.wait_ge(v.h, last_v)
                g.dma_start(out=recv,
                            in_=src[row0:row0 + 128, :]).then_inc(dg.h, 16)
            block.gpsimd(_wrap(g_fn))

            for slab in range(8):
                ps = self.pA[:, 512 * (slab % 2):512 * (slab % 2) + 512]
                r_mm = tt.inc()
                def t_fn(ten, q=q, slab=slab, ps=ps, r_recv=r_recv):
                    ten.wait_ge(v.h, r_fb)
                    if q == 0 and slab < 2:
                        ten.wait_ge(self.z_done[0].h, self.z_done[1])
                    if not (q == 0 and slab < 2):
                        ten.wait_ge(v.h, self._ka_v)
                    lhs = s["fbf"][0:NH, q:q + 1].to_broadcast((NH, 128))
                    rhs = bass.AP(s["sel64b"], 8 * slab,
                                  [[NH, NH], [1, 8], [0, 64]])
                    ten.matmul(ps, lhs, rhs, start=True, stop=True)
                    _fin(ten, tt.h, 1)
                block.tensor(_wrap(t_fn))

                self._ka_v = v.inc()
                def v_fn3(vec, q=q, slab=slab, ps=ps, r_mm=r_mm, r_recv=r_recv):
                    vec.wait_ge(tt.h, r_mm)
                    vec.wait_ge(dg.h, r_recv)
                    cslab = s["carry"][:, 512 * slab:512 * (slab + 1)]
                    rslab = bass.AP(recv.tensor, recv.offset + 512 * slab,
                                    [[recv.ap[0][0], 128], [1, 512]])
                    if q == 0:
                        vec.tensor_mul(cslab, rslab, ps)
                    else:
                        vec.tensor_mul(tmp[:, :], rslab, ps)
                        vec.drain()
                        vec.tensor_add(cslab, cslab, tmp[:, :])
                    _fin(vec, v.h, 1)
                block.vector(_wrap(v_fn3))
            last_v = self._ka_v
        self.kasm_done = (v, last_v)

    def emit_chunk(self, block, t, c):
        s = self.s
        v, sc, tt = self.sm["v"], self.sm["sc"], self.sm["t"]
        CC = CHUNK * c
        csc_c = s["csc"][0:NH, CC:CC + CHUNK]
        xds = self.av("xds", 0, 128, DI, BF16)
        xtok = self.av("xtok", 0, 128, 4096, BF16)
        first_gate = self.kasm_done if c == 0 else self.chunk_done

        def xtv(col, w=64):
            return bass.AP(xtok.tensor, xtok.offset + col,
                           [[xtok.ap[0][0], 128], [1, w]])

        def xdv(col, w):
            return bass.AP(xds.tensor, xds.offset + col,
                           [[xds.ap[0][0], 128], [1, w]])

        r_prep_s = sc.inc()
        def s_fn(scl):
            scl.wait_ge(first_gate[0].h, first_gate[1])
            scl.activation(s["expda"][:NH, :], csc_c, AF.Exp)
            scl.activation(s["dtds"][:NH, CC:CC + CHUNK], csc_c, AF.Exp,
                           scale=s["negc"][:NH, 0:1],
                           bias=s["csc"][:NH, CC + CHUNK - 1:CC + CHUNK])
            scl.activation(s["hcol"][:NH, 0:1],
                           s["csc"][:NH, CC + CHUNK - 1:CC + CHUNK], AF.Exp)
            _fin(scl, sc.h, 1)
        block.scalar(_wrap(s_fn))

        r_prep_v = v.inc()
        def v_fn(vec):
            vec.wait_ge(sc.h, r_prep_s)
            vec.tensor_mul(s["dtds"][:NH, CC:CC + CHUNK],
                           s["dtds"][:NH, CC:CC + CHUNK],
                           s["dt"][:NH, CC:CC + CHUNK])
            vec.tensor_scalar(out=s["r64"][:NH, :], in0=s["sel64b"][:NH, :],
                              scalar1=s["hcol"][:NH, 0:1], scalar2=None,
                              op0=ALU.mult)
            _fin(vec, v.h, 1)
        block.vector(_wrap(v_fn))

        r_cdp_t = tt.inc()
        def t_fn(ten):
            ten.wait_ge(v.h, r_prep_v)
            ten.matmul(self.pA[:, 0:64],
                       s["onesbf"][0:NH, 0:1].to_broadcast((NH, 128)),
                       s["r64"][0:NH, :], start=True, stop=True)
            _fin(ten, tt.h, 1)
        block.tensor(_wrap(t_fn))
        r_cdp_v = v.inc()
        def v_fn2(vec):
            vec.wait_ge(tt.h, r_cdp_t)
            vec.tensor_copy(s["cdp"][:, 0:NH], self.pA[:, 0:64])
            _fin(vec, v.h, 1)
        block.vector(_wrap(v_fn2))

        items = []
        for st in range(2):
            sl = slice(CC + 128 * st, CC + 128 * (st + 1))
            items.append((s["csc"][0:NH, sl], s["tokc"][0:128, 64 * st:64 * (st + 1)], F32))
            items.append((s["dt"][0:NH, sl], s["tokc"][0:128, 128 + 64 * st:192 + 64 * st], F32))
            items.append((s["dtds"][0:NH, sl], s["tokc"][0:128, 256 + 64 * st:320 + 64 * st], F32))
        for st in range(2):
            for g_ in range(NBG):
                items.append((self.av("xfm", g_ * TT + CC + 128 * st, 128, 128, BF16),
                              xtv(1024 * st + g_ * 128, 128), BF16))
        self._tb_cp = v.n
        self._transpose_batch(block, items, pbank_off=512, pool=self.pA)
        r_tp = self._tb_cp

        r_neg = v.inc()
        def v_fn3(vec):
            vec.wait_ge(v.h, r_tp)
            vec.tensor_scalar(out=s["tokc"][0:128, 384:512],
                              in0=s["tokc"][0:128, 0:128], scalar1=-1.0,
                              scalar2=None, op0=ALU.mult)
            _fin(vec, v.h, 1)
        block.vector(_wrap(v_fn3))

        btok = self.av("btok", 0, 128, DXB, BF16)
        r_cupd = {}
        for ph in range(2):
            pacc = self.pA if ph == 0 else self.pB
            ptr = self.pB if ph == 0 else self.pA
            smm = None
            for st in range(2):
                bitems = [(self.av("bfm", g_ * TT + CC + 128 * st, 128, 128, BF16),
                           bass.AP(btok.tensor, btok.offset + g_ * 128,
                                   [[btok.ap[0][0], 128], [1, 128]]), BF16)
                          for g_ in range(NBG)]
                def tg(ten, ph=ph, st=st, smm=smm):
                    if ph == 0 and st == 0:
                        ten.wait_ge(v.h, r_cdp_v)
                    if ph == 1 and st == 0:
                        ten.wait_ge(v.h, r_cupd[0])
                    if smm is not None:
                        ten.wait_ge(tt.h, smm)
                block.tensor(_wrap(tg))
                self._tb_cp = v.n
                self._transpose_batch(block, bitems, pbank_off=512 + 1024,
                                      pool=ptr)
                r_bt = self._tb_cp

                r_xds = v.inc()
                def v_fn4(vec, ph=ph, st=st, r_bt=r_bt, smm=smm):
                    vec.wait_ge(v.h, r_bt)
                    if st == 0 and ph == 0:
                        vec.wait_ge(v.h, r_neg)
                    if smm is not None:
                        vec.wait_ge(tt.h, smm)
                    for hh in range(32):
                        h = 32 * ph + hh
                        xc = 1024 * st + (h // 8) * 128 + 64 * (h % 2)
                        vec.tensor_scalar(
                            out=xdv(2048 * ph + 64 * hh, 64), in0=xtv(xc),
                            scalar1=s["tokc"][0:128, 256 + 64 * st + h:257 + 64 * st + h],
                            scalar2=None, op0=ALU.mult)
                    _fin(vec, v.h, 1)
                block.vector(_wrap(v_fn4))

                smm = tt.inc()
                def t_fn2(ten, ph=ph, st=st, r_xds=r_xds, pacc=pacc):
                    ten.wait_ge(v.h, r_xds)
                    for bgl in range(4):
                        bg = 4 * ph + bgl
                        ps = pacc[:, 512 * bgl:512 * (bgl + 1)]
                        lhs = bass.AP(btok.tensor, btok.offset + bg * 128,
                                      [[btok.ap[0][0], 128], [1, 128]])
                        rhs = xdv(2048 * ph + 512 * bgl, 512)
                        ten.matmul(ps, lhs, rhs, start=(st == 0), stop=(st == 1))
                    _fin(ten, tt.h, 1)
                block.tensor(_wrap(t_fn2))

            r_cu = v.inc()
            r_cupd[ph] = r_cu
            def v_fn5(vec, ph=ph, smm=smm, pacc=pacc):
                vec.wait_ge(tt.h, smm)
                vec.tensor_copy(xdv(2048 * ph, 2048),
                                s["carry"][:, 2048 * ph:2048 * (ph + 1)])
                vec.drain()
                for bgl in range(4):
                    bg = 4 * ph + bgl
                    ps = pacc[:, 512 * bgl:512 * (bgl + 1)]
                    csl = s["carry"][:, 512 * bg:512 * (bg + 1)]
                    cdap = bass.AP(s["cdp"], 8 * bg, [[NH, 128], [1, 8], [0, 64]])
                    vec.tensor_tensor(csl, csl, cdap, ALU.mult)
                    vec.drain()
                    vec.tensor_add(csl, csl, ps)
                    vec.drain()
                _fin(vec, v.h, 1)
            block.vector(_wrap(v_fn5))
        r_cup = r_cupd[1]

        ydone, exps_rec, yd_rec, gl_rec = {}, {}, {}, {}

        def pools(j):
            pp = self.pA if j % 2 == 0 else self.pB
            return pp, (pp, 512), (pp, 1024), (pp, 1536)

        def emit_t3(j):
            pG, pe, pYd, pYo = pools(j)
            r = tt.inc()
            def t_fn3(ten, j=j, pG=pG, pe=pe, pYd=pYd):
                if j < 2:
                    ten.wait_ge(v.h, r_cup)
                else:
                    ten.wait_ge(v.h, ydone[j - 2])
                crhs = self.av("cfm", j * TT + CC, 128, CHUNK, BF16)
                crhs1 = self.av("cfm", j * TT + CC + 128, 128, 128, BF16)
                for st in range(2):
                    bl = self.av("bfm", (j // 4) * TT + CC + 128 * st, 128, 128, BF16)
                    if st == 0:
                        ten.matmul(pG[:, 0:CHUNK], bl, crhs, start=True, stop=True)
                    else:
                        ten.matmul(pG[:, CHUNK:CHUNK + 128], bl, crhs1,
                                   start=True, stop=True)
                for hh in range(2):
                    lhs = s["sel64"][0:NH, 2 * j + hh:2 * j + hh + 1] \
                        .to_broadcast((NH, 128))
                    ten.matmul(pe[0][:, pe[1] + 256 * hh:pe[1] + 256 * hh + 256],
                               lhs, csc_c, start=True, stop=True)
                    ten.matmul(pYd[0][:, pYd[1] + 256 + 128 * hh:pYd[1] + 256 + 128 * hh + 128],
                               lhs, s["csc"][0:NH, CC + 128:CC + CHUNK],
                               start=True, stop=True)
                _fin(ten, tt.h, 1)
            block.tensor(_wrap(t_fn3))
            return r

        t3_rec = {0: emit_t3(0), 1: emit_t3(1)}
        for j in range(NG):
            pG, pe, pYd, pYo = pools(j)

            r_xb = v.inc()
            def v_fn6a(vec, j=j):
                if j == 0:
                    vec.wait_ge(v.h, r_neg)
                if j >= 4:
                    vec.wait_ge(tt.h, yd_rec[j - 4])
                so = 2048 + 256 * (j % 4)
                for st in range(2):
                    for hh in range(2):
                        h = 2 * j + hh
                        xc = 1024 * st + (h // 8) * 128 + 64 * (h % 2)
                        vec.tensor_scalar(
                            out=xtv(so + 64 * (2 * st + hh)), in0=xtv(xc),
                            scalar1=s["tokc"][0:128, 128 + 64 * st + h:129 + 64 * st + h],
                            scalar2=None, op0=ALU.mult)
                _fin(vec, v.h, 1)
            block.vector(_wrap(v_fn6a))

            r_vm = v.inc()
            def v_fn6b(vec, j=j, pe=pe, pYd=pYd, r_tj=t3_rec[j]):
                vec.wait_ge(tt.h, r_tj)
                ap_e = bass.AP(pe[0], pe[1], [[2048, 128], [256, 2], [1, 128]])
                mm0 = bass.AP(s["mneg"], 0, [[512, 128], [0, 2], [1, 128]])
                vec.tensor_tensor(ap_e, ap_e, mm0, ALU.add)
                ap_e1 = bass.AP(pYd[0], pYd[1] + 256, [[2048, 128], [128, 2], [1, 128]])
                mm1 = bass.AP(s["mneg"], CHUNK + 128, [[512, 128], [0, 2], [1, 128]])
                vec.tensor_tensor(ap_e1, ap_e1, mm1, ALU.add)
                _fin(vec, v.h, 1)
            block.vector(_wrap(v_fn6b))

            r_exps = sc.inc()
            exps_rec[j] = r_exps
            def s_fn2(scl, j=j, pe=pe, pYd=pYd, r_vm=r_vm):
                scl.wait_ge(v.h, r_vm)
                for hh in range(2):
                    h = 2 * j + hh
                    gl0 = self.av("gl", hh * 2 * CHUNK, 128, CHUNK, BF16)
                    scl.activation(gl0,
                                   pe[0][:, pe[1] + 256 * hh:pe[1] + 256 * hh + 256],
                                   AF.Exp, bias=s["tokc"][0:128, 384 + h:385 + h])
                    gl1 = self.av("gl", hh * 2 * CHUNK + CHUNK, 128, 128, BF16)
                    scl.activation(gl1,
                                   pYd[0][:, pYd[1] + 256 + 128 * hh:pYd[1] + 256 + 128 * hh + 128],
                                   AF.Exp, bias=s["tokc"][0:128, 448 + h:449 + h])
                _fin(scl, sc.h, 1)
            block.scalar(_wrap(s_fn2))

            r_em = tt.inc()
            def t_fn4(ten, j=j, pYd=pYd, r_exps=r_exps):
                ten.wait_ge(sc.h, r_exps)
                if j >= 2:
                    ten.wait_ge(v.h, ydone[j - 2])
                for hh in range(2):
                    lhs = s["sel64b"][0:NH, 2 * j + hh:2 * j + hh + 1] \
                        .to_broadcast((NH, 64))
                    ten.matmul(pYd[0][64 * hh:64 * (hh + 1), pYd[1]:pYd[1] + 256],
                               lhs, s["expda"][0:NH, :],
                               start=True, stop=True)
                _fin(ten, tt.h, 1)
            block.tensor(_wrap(t_fn4))

            r_ee = v.inc()
            def s_fn3(vec, j=j, pYd=pYd, r_em=r_em):
                vec.wait_ge(tt.h, r_em)
                eda = self.av("scr", CHUNK * (j % 2), 128, CHUNK, BF16)
                vec.tensor_copy(eda, pYd[0][:, pYd[1]:pYd[1] + 256])
                _fin(vec, v.h, 1)
            block.vector(_wrap(s_fn3))

            r_gl = v.inc()
            gl_rec[j] = r_gl
            def v_fn7(vec, j=j, pG=pG, r_exps=r_exps):
                vec.wait_ge(sc.h, r_exps)
                for hh in range(2):
                    gl0 = self.av("gl", hh * 2 * CHUNK, 128, CHUNK, BF16)
                    vec.tensor_mul(gl0, gl0, pG[:, 0:CHUNK])
                    gl1 = self.av("gl", hh * 2 * CHUNK + CHUNK, 128, 128, BF16)
                    vec.tensor_mul(gl1, gl1, pG[:, CHUNK:CHUNK + 128])
                _fin(vec, v.h, 1)
            block.vector(_wrap(v_fn7))

            r_ym = tt.inc()
            yd_rec[j] = r_ym
            def t_fn5(ten, j=j, pYd=pYd, pYo=pYo, r_gl=r_gl, r_ee=r_ee):
                ten.wait_ge(v.h, max(r_gl, r_ee))
                so = 2048 + 256 * (j % 4)
                for hh in range(2):
                    for st in range(2):
                        lhsT = xtv(so + 64 * (2 * st + hh))
                        gl0 = self.av("gl", hh * 2 * CHUNK + st * CHUNK, 128,
                                      CHUNK if st == 0 else 128, BF16)
                        out = pYd[0][64 * hh:64 * hh + 64,
                                     pYd[1] + (0 if st == 0 else 128):pYd[1] + 256]
                        ten.matmul(out, lhsT, gl0, start=(st == 0), stop=(st == 1))
                crhs = self.av("cfm", j * TT + CC, 128, CHUNK, BF16)
                ten.matmul(pYo[0][:, pYo[1]:pYo[1] + 256], xdv(128 * j, 128),
                           crhs, start=True, stop=True)
                _fin(ten, tt.h, 1)
            block.tensor(_wrap(t_fn5))

            r_ya = v.inc()
            ydone[j] = r_ya
            def v_fn8(vec, j=j, pYd=pYd, pYo=pYo, r_ym=r_ym):
                vec.wait_ge(tt.h, r_ym)
                eda = self.av("scr", CHUNK * (j % 2), 128, CHUNK, BF16)
                ysb = self.av("scr", 2 * CHUNK, 128, CHUNK, BF16)
                vec.tensor_mul(ysb, eda, pYo[0][:, pYo[1]:pYo[1] + 256])
                vec.drain()
                vec.tensor_add(ysb, ysb, pYd[0][:, pYd[1]:pYd[1] + 256])
                vec.drain()
                xsl = self.av("xfm", (j // 4) * TT + CC, 128, CHUNK, BF16)
                vec.scalar_tensor_tensor(out=ysb, in0=xsl,
                                         scalar=s["dpair"][:, j:j + 1],
                                         in1=ysb, op0=ALU.mult, op1=ALU.add)
                vec.drain()
                zsl = self.av("zy", j * TT + CC, 128, CHUNK, BF16)
                vec.tensor_mul(zsl, ysb, zsl)
                _fin(vec, v.h, 1)
            block.vector(_wrap(v_fn8))

            if j + 2 < NG:
                t3_rec[j + 2] = emit_t3(j + 2)

        sq_rec = {}
        for j in range(NG):
            r_sq = v.inc()
            def v_fn9(vec, j=j):
                vec.wait_ge(v.h, ydone[j])
                if j >= 1:
                    vec.wait_ge(tt.h, sq_rec[j - 1])
                vec.drain()
                ysq = self.av("scr", 2 * CHUNK, 128, CHUNK, BF16)
                zsl = self.av("zy", j * TT + CC, 128, CHUNK, BF16)
                vec.tensor_mul(ysq, zsl, zsl)
                _fin(vec, v.h, 1)
            block.vector(_wrap(v_fn9))
            r_sm = tt.inc()
            sq_rec[j] = r_sm
            def t_fn6(ten, j=j, r_sq=r_sq):
                ten.wait_ge(v.h, r_sq)
                if j == 0:
                    ten.wait_ge(v.h, ydone[NG - 1])
                ysq = self.av("scr", 2 * CHUNK, 128, CHUNK, BF16)
                for half in range(2):
                    lhsT = bass.AP(ysq.tensor, ysq.offset + 128 * half,
                                   [[ysq.ap[0][0], 128], [1, 128]])
                    ten.matmul(self.pA[:, 2 * j + half:2 * j + half + 1],
                               lhsT, s["onesbf"][:, 0:1].to_broadcast((128, 1)),
                               start=True, stop=True)
                _fin(ten, tt.h, 1)
            block.tensor(_wrap(t_fn6))
        r_rs = v.inc()
        def v_fn10(vec):
            vec.wait_ge(tt.h, sq_rec[NG - 1])
            vec.tensor_scalar(out=s["rstd"][:, 0:NH], in0=self.pA[:, 0:64],
                              scalar1=1.0 / 128, scalar2=None, op0=ALU.mult)
            _fin(vec, v.h, 1)
        block.vector(_wrap(v_fn10))
        r_rsq = sc.inc()
        def s_fn4(scl):
            scl.wait_ge(v.h, r_rs)
            scl.activation(s["rstd"][:, 0:NH], s["rstd"][:, 0:NH], AF.Ln,
                           bias=s["epsc"][:, 0:1])
            scl.drain()
            scl.activation(s["rstd"][:, 0:NH], s["rstd"][:, 0:NH], AF.Exp,
                           scale=s["nhc"][:, 0:1])
            _fin(scl, sc.h, 1)
        block.scalar(_wrap(s_fn4))
        r_rrec = r_rsq

        ap_rec = {}
        for j in range(NG):
            r_rb = v.inc()
            def v_fn12(vec, j=j):
                vec.wait_ge(sc.h, r_rrec)
                if j >= 2:
                    vec.wait_ge(v.h, ap_rec[j - 2])
                for half in range(2):
                    rh = self.av("scr", 3 * CHUNK + 128 * half, 128, 128, BF16)
                    vec.tensor_scalar(out=rh, in0=s["ident"][:, :],
                                      scalar1=s["rstd"][:, 2 * j + half:2 * j + half + 1],
                                      scalar2=None, op0=ALU.mult)
                _fin(vec, v.h, 1)
            block.vector(_wrap(v_fn12))
            r_rm = tt.inc()
            def t_fn7(ten, j=j, r_rb=r_rb):
                ten.wait_ge(v.h, r_rb)
                pR = self.pB if j % 2 == 0 else self.pA
                for half in range(2):
                    rh = self.av("scr", 3 * CHUNK + 128 * half, 128, 128, BF16)
                    ten.matmul(pR[:, 128 * half:128 * (half + 1)],
                               s["onesbf"][:, 0:1].to_broadcast((128, 128)),
                               rh, start=True, stop=True)
                _fin(ten, tt.h, 1)
            block.tensor(_wrap(t_fn7))
            r_ap = v.inc()
            ap_rec[j] = r_ap
            def v_fn13(vec, j=j, r_rm=r_rm):
                vec.wait_ge(tt.h, r_rm)
                pR = self.pB if j % 2 == 0 else self.pA
                for half in range(2):
                    zsl = self.av("zy", j * TT + CC + 128 * half, 128, 128, BF16)
                    vec.scalar_tensor_tensor(
                        out=zsl, in0=zsl, scalar=s["nw"][:, j:j + 1],
                        in1=pR[:, 128 * half:128 * (half + 1)],
                        op0=ALU.mult, op1=ALU.mult)
                _fin(vec, v.h, 1)
            block.vector(_wrap(v_fn13))
        self.chunk_done = (v, ap_rec[NG - 1])

    def emit_outproj(self, block, t):
        s, d = self.s, self.d
        dg, ds, v, sc, tt = (self.sm[k] for k in ["dg", "ds", "v", "sc", "t"])
        dg0, dg1 = self.sm["dg0"], self.sm["dg1"]
        ust = self.av("ust", 0, 128, DM, BF16)
        r1 = self.av("r1", 0, 128, DM, BF16)
        w = t % 2
        w_rec, mm_rec, ev_rec, tr_rec = {}, {}, {}, {}
        u_rec, ln_rec = {}, {}
        for tp in range(2):
            for k in range(KC + KC):
                dsl = self.sm[f"dg{k % 3}"]
                def g_fn(g, tp=tp, k=k, dsl=dsl, t=t):
                    if tp == 0 and k < 3:
                        g.wait_ge(self.z_done[0].h, self.z_done[1])
                        if t == 0 and k == 0:
                            g.wait_ge(self.cast_rec["WoT"][0].h, self.cast_rec["WoT"][1])
                    if k - 3 >= 0 or tp > 0:
                        pk = (tp, k - 3) if k >= 3 else (
                            tp - 1, 30 + k if k < 2 else 29)
                        g.wait_ge(tt.h, mm_rec[pk])
                    self.dma_slab(g, d["WoTb"], k, "wbuf",
                                  (k % 3) * KC * 128, M=DM, kc=1
                                  ).then_inc(dsl.h, 16)
                block.sync(_wrap(g_fn))
                w_rec[(tp, k)] = dsl.inc(16)

                def t_fn(ten, tp=tp, k=k, dsl=dsl,
                         cpw=(self._ol_cp if tp == 1 else None)):
                    ten.wait_ge(dsl.h, w_rec[(tp, k)])
                    if k == 0:
                        ten.wait_ge(v.h, self.chunk_done[1])
                        if tp == 1:
                            ten.wait_ge(v.h, cpw)
                    wv = self.av("wbuf", (k % 3) * KC * 128, 128, DM, BF16)
                    for tl in range(2):
                        tok = 2 * tp + tl
                        pp = self.pA if tl == 0 else self.pB
                        lhsT = self.av("zy", k * TT + 128 * tok, 128, 128, BF16)
                        for q in range(4):
                            ten.matmul(pp[:, 512 * q:512 * (q + 1)], lhsT,
                                       bass.AP(wv.tensor, wv.offset + 512 * q,
                                               [[wv.ap[0][0], 128], [1, 512]]),
                                       start=(k == 0), stop=(k == KC + KC - 1))
                    mm_rec[(tp, k)] = tt.inc()
                    _fin(ten, tt.h, 1)
                block.tensor(_wrap(t_fn))

            for tl in range(2):
                tok = 2 * tp + tl
                pp = self.pA if tl == 0 else self.pB
                du = self.sm["du"]
                r_u = du.inc(16)
                u_rec[tok] = r_u
                def sy_fn(sy, t=t, tok=tok, tp=tp, tl=tl):
                    if tok > 0:
                        sy.wait_ge(v.h, ev_rec[tok - 1])
                    sy.dma_start(out=ust, in_=d["ut"][t, 128 * tok:128 * (tok + 1), :]
                                 ).then_inc(du.h, 16)
                block.gpsimd(_wrap(sy_fn))

                r_ev = v.inc()
                ev_rec[tok] = r_ev
                def v_fn(vec, tok=tok, tp=tp, pp=pp, r_u=r_u):
                    vec.wait_ge(tt.h, mm_rec[(tp, KC + KC - 1)])
                    vec.wait_ge(self.sm["du"].h, r_u)
                    if tok > 0:
                        vec.wait_ge(self.sm["dr1"].h, self.r1_rec)
                        vec.wait_ge(tt.h, tr_rec[tok - 1])
                    vec.tensor_add(r1, pp[:, 0:DM], ust)
                    vec.drain()
                    for q in range(4):
                        vec.bn_stats(s["bnst"][:, 6 * q:6 * (q + 1)],
                                     r1[:, 512 * q:512 * (q + 1)])
                    vec.drain()
                    vec.bn_aggr(s["bnag"][:, :], s["bnst"][:, :])
                    _fin(vec, v.h, 1)
                block.vector(_wrap(v_fn))

                self.r1_rec = self.sm["dr1"].inc(16)
                def sy_fn2(sy, t=t, tok=tok, r_ev=r_ev):
                    sy.wait_ge(v.h, r_ev)
                    sy.dma_start(out=d["r1_sp"][t % 2, tok, :, :],
                                 in_=r1).then_inc(self.sm["dr1"].h, 16)
                block.sync(_wrap(sy_fn2))

                r_sq = sc.inc()
                def s_fn(scl, r_ev=r_ev):
                    scl.wait_ge(v.h, r_ev)
                    scl.activation(s["rcol"][:, :], s["bnag"][:, 1:2], AF.Ln,
                                   bias=s["epsc"][:, 0:1])
                    scl.drain()
                    scl.activation(s["rcol"][:, :], s["rcol"][:, :], AF.Exp,
                                   scale=s["nhc"][:, 0:1])
                    _fin(scl, sc.h, 1)
                block.scalar(_wrap(s_fn))

                r_ln = v.inc()
                ln_rec[tok] = r_ln
                def v_fn2(vec, r_sq=r_sq):
                    vec.wait_ge(sc.h, r_sq)
                    vec.wait_ge(self.sm["dr1"].h, self.r1_rec)
                    vec.tensor_scalar(out=r1, in0=r1,
                                      scalar1=s["bnag"][:, 0:1],
                                      scalar2=s["rcol"][:, 0:1],
                                      op0=ALU.subtract, op1=ALU.mult)
                    _fin(vec, v.h, 1)
                block.vector(_wrap(v_fn2))

                for g_ in range(4):
                    r_tr = tt.inc()
                    def t_fn2(ten, g_=g_, tok=tok, r_ln=r_ln):
                        ten.wait_ge(v.h, r_ln)
                        if g_ >= 1:
                            ten.wait_ge(v.h, self._ol_cp)
                        for jj in range(4):
                            k = 4 * g_ + jj
                            ps = self.pv(self.pA, 1024 + 64 * jj, 128, 128, BF16)
                            ten.transpose(ps, r1[:, 128 * k:128 * (k + 1)],
                                          s["ident"][:, :])
                        _fin(ten, tt.h, 1)
                    block.tensor(_wrap(t_fn2))
                    if g_ == 3:
                        tr_rec[tok] = r_tr
                    def v_fn3(vec, g_=g_, tok=tok, w=w, r_tr=r_tr):
                        vec.wait_ge(tt.h, r_tr)
                        for jj in range(4):
                            k = 4 * g_ + jj
                            ps = self.pv(self.pA, 1024 + 64 * jj, 128, 128, BF16)
                            vec.tensor_copy(
                                s["xl2"][:, (w * KC + k) * TT + 128 * tok:
                                         (w * KC + k) * TT + 128 * (tok + 1)], ps)
                        _fin(vec, v.h, 1)
                    self._ol_cp = v.inc()
                    block.vector(_wrap(v_fn3))
        self.op_done = (v, self._ol_cp)

    def emit_ffn(self, block, w):
        s, d = self.s, self.d
        dg, ds, v, sc, tt = (self.sm[k] for k in ["dg", "ds", "v", "sc", "t"])
        dg0, dg1 = self.sm["dg0"], self.sm["dg1"]
        w_rec, mm_rec, hev = {}, {}, {}
        h_rec = {}
        prev_out = self.out_rec.get(7, 0)
        for o in range(IC):
            dsl = self.sm["dfg0"] if o % 2 == 0 else self.sm["dfg1"]
            def g_fn(g, o=o, dsl=dsl):
                if o < 2:
                    g.wait_ge(self.op_done[0].h, self.op_done[1])
                if o >= 2:
                    g.wait_ge(tt.h, mm_rec[o - 2])
                self.dma_slab(g, d["Wg"], o, "fw", (o % 2) * KC * 128
                              ).then_inc(dsl.h, 16)
                w_rec[o] = dsl.inc(16)
            block.gpsimd(_wrap(g_fn))
            def g_fn2(g, o=o, dsl=dsl):
                if o >= 2:
                    g.wait_ge(tt.h, mm_rec[o - 2])
                self.dma_slab(g, d["Wu"], o, "fw", (2 + o % 2) * KC * 128
                              ).then_inc(dsl.h, 16)
                w_rec[(o, "u")] = dsl.inc(16)
            block.gpsimd(_wrap(g_fn2))

            def t_fn(ten, o=o, dsl=dsl, w=w, ldev=self.last_dev):
                ten.wait_ge(dsl.h, w_rec[(o, "u")])
                if o < 2 and ldev is not None:
                    ten.wait_ge(v.h, ldev)
                if o >= 2:
                    ten.wait_ge(v.h, hev[o - 2])
                pp = self.pA if o % 2 == 0 else self.pB
                for k in range(KC):
                    wg_ = self.av("fw", (o % 2) * KC * 128 + k * 128, 128, 128, BF16)
                    wu_ = self.av("fw", (2 + o % 2) * KC * 128 + k * 128, 128, 128, BF16)
                    for half in range(2):
                        rhs = s["xl2"][:, (half * KC + k) * TT:(half * KC + k) * TT + TT]
                        ten.matmul(pp[:, 512 * half:512 * (half + 1)], wg_, rhs,
                                   start=(k == 0), stop=(k == KC - 1))
                        ten.matmul(pp[:, 1024 + 512 * half:1536 + 512 * half], wu_,
                                   rhs, start=(k == 0), stop=(k == KC - 1))
                mm_rec[o] = tt.inc()
                _fin(ten, tt.h, 1)
            block.tensor(_wrap(t_fn))

            r_si = sc.inc()
            def s_fn(scl, o=o):
                scl.wait_ge(tt.h, mm_rec[o])
                pp = self.pA if o % 2 == 0 else self.pB
                fg = self.av("fg", (o % 2) * 1024, 128, 1024, BF16)
                scl.activation(fg, pp[:, 0:1024], AF.Silu,
                               scale=s["sg"][:, o:o + 1])
                _fin(scl, sc.h, 1)
            block.scalar(_wrap(s_fn))

            r_hv = v.inc()
            hev[o] = r_hv
            def v_fn(vec, o=o, r_su=r_si, r_si=r_si):
                vec.wait_ge(sc.h, r_si)
                if o >= 2:
                    vec.wait_ge(self.sm["dh0" if o % 2 == 0 else "dh1"].h,
                                h_rec[o - 2])
                pp = self.pA if o % 2 == 0 else self.pB
                fu = self.av("fu", (o % 2) * 1024, 128, 1024, BF16)
                fg = self.av("fg", (o % 2) * 1024, 128, 1024, BF16)
                fho = self.av("fho", (o % 2) * 1024, 128, 1024, BF16)
                vec.tensor_scalar(out=fu, in0=pp[:, 1024:2048],
                                  scalar1=s["su"][:, o:o + 1], scalar2=None,
                                  op0=ALU.mult)
                vec.drain()
                vec.tensor_mul(fho, fg, fu)
                _fin(vec, v.h, 1)
            block.vector(_wrap(v_fn))

            dhp = self.sm["dh0" if o % 2 == 0 else "dh1"]
            r_hd = dhp.inc(16)
            h_rec[o] = r_hd
            def g_fn3(g, o=o, r_hv=r_hv, ldn=self.last_dn, dhp=dhp):
                g.wait_ge(v.h, r_hv)
                if o == 0 and ldn is not None:
                    g.wait_ge(tt.h, ldn)
                fho = self.av("fho", (o % 2) * 1024, 128, 1024, BF16)
                g.dma_start(out=d["h_sp"][o, :, :], in_=fho).then_inc(dhp.h, 16)
            block.sync(_wrap(g_fn3))

        sdbc = self.av("fw", 0, 128, DM, BF16)
        r_sdb = {}
        for m in range(KC):
            r_rb = v.inc()
            def v_fn2(vec, m=m):
                if m == 0:
                    vec.wait_ge(tt.h, mm_rec[IC - 1])
                if m >= 2:
                    vec.wait_ge(v.h, r_sdb[m - 2])
                rh = self.av("fg", 0, 128, 128, BF16)
                vec.tensor_scalar(out=rh, in0=s["ident"][:, :],
                                  scalar1=s["sd"][:, m:m + 1], scalar2=None,
                                  op0=ALU.mult)
                _fin(vec, v.h, 1)
            block.vector(_wrap(v_fn2))
            r_mm = tt.inc()
            def t_fn2(ten, m=m, r_rb=r_rb):
                ten.wait_ge(v.h, r_rb)
                if m == 0:
                    ten.wait_ge(v.h, hev[IC - 1])
                rh = self.av("fg", 0, 128, 128, BF16)
                pp = self.pA if m % 2 == 0 else self.pB
                ten.matmul(pp[:, 0:128],
                           s["onesbf"][:, 0:1].to_broadcast((128, 128)),
                           rh, start=True, stop=True)
                _fin(ten, tt.h, 1)
            block.tensor(_wrap(t_fn2))
            r_cp = v.inc()
            r_sdb[m] = r_cp
            def v_fn3(vec, m=m, r_mm=r_mm):
                vec.wait_ge(tt.h, r_mm)
                pp = self.pA if m % 2 == 0 else self.pB
                vec.tensor_copy(bass.AP(sdbc.tensor, sdbc.offset + 128 * m,
                                        [[sdbc.ap[0][0], 128], [1, 128]]),
                                pp[:, 0:128])
                _fin(vec, v.h, 1)
        
            block.vector(_wrap(v_fn3))

        dn_rec, dev_rec, hld_rec, wd_rec = {}, {}, {}, {}
        dsh, dsr = self.sm["dsh"], self.sm["dsr"]
        for q in range(2):
            r_hl = dsh.inc(16 * 4)
            def g_hl(g, q=q, w=w):
                if q == 0:
                    g.wait_ge(self.sm["dh0"].h, h_rec[IC - 2])
                    g.wait_ge(self.sm["dh1"].h, h_rec[IC - 1])
                else:
                    g.wait_ge(tt.h, dn_rec[(0, 1, IC - 1)])
                for tl in range(4):
                    tok = 4 * q + tl
                    fh = self.av("fh", tl * IC * 128, 128, IC * 128, BF16)
                    srcp = bass.AP(d["h_sp"], 128 * tok,
                                   [[2 * TT, 128], [128 * 2 * TT, IC], [1, 128]])
                    g.dma_start(out=bass.AP(fh.tensor, fh.offset,
                                            [[fh.ap[0][0], 128], [128, IC], [1, 128]]),
                                in_=srcp).then_inc(dsh.h, 16)
            block.sync(_wrap(g_hl))
            hld_rec[q] = r_hl

            r_r1 = self.sm["df0"].inc(16 * 4)
            def g_r1(g, q=q, w=w, pldev=self.last_dev):
                if q == 1:
                    for tl in range(4):
                        g.wait_ge(v.h, dev_rec[(0, 1, tl)])
                elif pldev is not None:
                    g.wait_ge(v.h, pldev)
                for tl in range(4):
                    tok = 4 * q + tl
                    tile = 2 * w + tok // 4
                    fr1 = self.av("fr1", tl * DM, 128, DM, BF16)
                    g.dma_start(out=fr1, in_=d["r1_sp"][tile % 2, tok % 4, :, :]
                                ).then_inc(self.sm["df0"].h, 16)
            block.sync(_wrap(g_r1))
            self._fr1w = r_r1

            for half in range(2):
                for k in range(IC):
                    dsl = self.sm[f"dgw{k % 6}"]
                    def g_wd(g, q=q, half=half, k=k, dsl=dsl):
                        if (q, half, k) == (0, 0, 0):
                            g.wait_ge(v.h, r_sdb[KC - 1])
                        pk = None
                        pw = 60 + k if k < 4 else 54 + k
                        if k >= 6:
                            pk = (q, half, k - 6)
                        elif half == 1:
                            pk = (q, 0, pw)
                        elif q == 1:
                            pk = (0, 1, pw)
                        if pk is not None and pk in dn_rec:
                            g.wait_ge(tt.h, dn_rec[pk])
                        srcp = bass.AP(d["Wd"], k * 128 * DM + 1024 * half,
                                      [[DM, 128], [1, 1024]])
                        g.dma_start(out=self.av("fwd", (k % 6) * 1024, 128, 1024, BF16),
                                    in_=srcp).then_inc(dsl.h, 16)
                        wd_rec[(q, half, k)] = dsl.inc(16)
                    block.gpsimd(_wrap(g_wd))

                    def t_dn(ten, q=q, half=half, k=k, dsl=dsl):
                        ten.wait_ge(dsl.h, wd_rec[(q, half, k)])
                        if k == 0:
                            ten.wait_ge(dsh.h, hld_rec[q])
                            if (q, half) == (0, 0):
                                ten.wait_ge(v.h, hev[IC - 1])
                            else:
                                ph = (q, half - 1) if half == 1 else (q - 1, 1)
                                for tl in range(4):
                                    ten.wait_ge(v.h, dev_rec[(ph[0], ph[1], tl)])
                        wv = self.av("fwd", (k % 6) * 1024, 128, 1024, BF16)
                        for tl in range(4):
                            fh = self.av("fh", tl * IC * 128, 128, IC * 128, BF16)
                            lhsT = bass.AP(fh.tensor, fh.offset + 128 * k,
                                           [[fh.ap[0][0], 128], [1, 128]])
                            pp = self.pA if tl < 2 else self.pB
                            po = 1024 * (tl % 2)
                            for sub in range(2):
                                ten.matmul(pp[:, po + 512 * sub:po + 512 * (sub + 1)],
                                           lhsT,
                                           bass.AP(wv.tensor, wv.offset + 512 * sub,
                                                   [[wv.ap[0][0], 128], [1, 512]]),
                                           start=(k == 0), stop=(k == IC - 1))
                        dn_rec[(q, half, k)] = tt.inc()
                        _fin(ten, tt.h, 1)
                    block.tensor(_wrap(t_dn))

                for tl in range(4):
                    tok = 4 * q + tl
                    tile = 2 * w + tok // 4
                    trow = (tok % 4) * 128
                    pp = self.pA if tl < 2 else self.pB
                    po = 1024 * (tl % 2)
                    r_dev = v.inc()
                    dev_rec[(q, half, tl)] = r_dev
                    def v_dn(vec, q=q, half=half, tl=tl, pp=pp, po=po):
                        vec.wait_ge(tt.h, dn_rec[(q, half, IC - 1)])
                        vec.wait_ge(self.sm["df0"].h, self._fr1w)
                        if self.out_rec.get(("p", tl)) is not None:
                            vec.wait_ge(self.sm[f"do{tl}"].h, self.out_rec[("p", tl)])
                        fr1 = self.av("fr1", tl * DM + 1024 * half, 128, 1024, BF16)
                        fo = self.av("fo", tl * 1024, 128, 1024, F32)
                        sdb = bass.AP(sdbc.tensor, sdbc.offset + 1024 * half,
                                      [[sdbc.ap[0][0], 128], [1, 1024]])
                        vec.tensor_mul(fo, pp[:, po:po + 1024], sdb)
                        vec.drain()
                        vec.tensor_add(fo, fo, fr1)
                        _fin(vec, v.h, 1)
                    block.vector(_wrap(v_dn))

                    self.out_rec[("p", tl)] = self.sm[f"do{tl}"].inc(16)
                    def sy_o(sy, tile=tile, trow=trow, half=half, tl=tl, r_dev=r_dev):
                        sy.wait_ge(v.h, r_dev)
                        fo = self.av("fo", tl * 1024, 128, 1024, F32)
                        sy.dma_start(out=d["out"][tile, trow:trow + 128,
                                                  1024 * half:1024 * (half + 1)],
                                     in_=fo).then_inc(self.sm[f"do{tl}"].h, 16)
                    block.sync(_wrap(sy_o))
            self._fr1w = r_r1
        self.ffn_done = (self.sm["do"], self.sm["do"].n)
        self.last_dev = dev_rec[(1, 1, 3)]
        self.last_dn = dn_rec[(1, 1, IC - 1)]

    def emit_all(self, block):
        self.out_rec = {-1: 0}
        self.last_dev = None
        self.last_dn = None
        self.emit_init(block)
        for t in range(NT):
            self.emit_ln1(block, t)
            self.emit_dt(block, t)
            self.emit_inproj_xbc(block, t)
            self.emit_phaseA(block, t)
            self.emit_collective(block, t)
            self.emit_inproj_z(block, t)
            self.emit_kasm(block, t)
            for c in range(NCHUNK):
                self.emit_chunk(block, t, c)
            self.emit_outproj(block, t)
            if t % 2 == 1:
                self.emit_ffn(block, t // 2)
        dos = [self.sm[f"do{i}"] for i in range(4)]
        def sy_fin(sy):
            for dsem in dos:
                sy.wait_ge(dsem.h, dsem.n)
        block.sync(_wrap(sy_fin))


def build_program():
    nc = bass.Bass("TRN2", target_bir_lowering=False)
    p = Prog(nc)
    with nc.Block() as block:
        p.emit_all(block)
    return nc


_NC_CACHE = None


def kernel(**inputs):
    global _NC_CACHE
    shared, per_core = prep_inputs(inputs)
    if _NC_CACHE is None:
        _NC_CACHE = build_program()
    nc = _NC_CACHE
    in_maps = []
    for core in range(NCORES):
        m = dict(shared)
        m.update(per_core[core])
        in_maps.append(m)
    res = run_bass_kernel_spmd(nc, in_maps, core_ids=list(range(NCORES)))
    return unpack_output(res.results)


# version 10
# speedup vs baseline: 1.1774x; 1.0132x over previous
import numpy as np
import ml_dtypes
import concourse.bass as bass
import concourse.mybir as mybir
from concourse.bass_utils import run_bass_kernel_spmd

F32 = mybir.dt.float32
F32R = mybir.dt.float32r
BF16 = mybir.dt.bfloat16
I8 = mybir.dt.int8
AF = mybir.ActivationFunctionType
ALU = mybir.AluOpType

B = 2
L = 8192
DM = 2048
DI = 4096
DXB = 1024
DST = 128
HD = 64
NH = 64
NG = 32
CHUNK = 256
DCONV = 4
CONV_DIM = DI + 2 * DXB
DIP = DI + CONV_DIM + NH
INTER = 8192
EPS = 1e-5

NCORES = 8
NCS = 4
NT = 4
TT = 512
NCHUNK = 2
KC = DM // 128
ZC = DI // 128
XBC_C = CONV_DIM // 128
IC = INTER // 128
TOKC = TT // 128
NBG = 8

MASK_NEG = -1e30



class Sem:

    def __init__(self, nc, name):
        self.h = nc.alloc_semaphore(name)
        self.n = 0

    def inc(self, k=1):
        self.n += k
        return self.n


class _Rec:
    _PASS = {"wait_ge", "nop"}

    _DMA = {"dma_start", "dma_start_transpose", "collective_compute"}

    def __init__(self, e):
        self.e = e
        self.last = None
        self.last_dma = False

    def __getattr__(self, n):
        f = getattr(self.e, n)
        if not callable(f) or n in _Rec._PASS or n.startswith("_"):
            return f
        def g(*a, **k):
            r = f(*a, **k)
            if r is not None and hasattr(r, "then_inc"):
                self.last = r
                self.last_dma = n in _Rec._DMA
            return r
        return g


def _wrap(fn):
    return lambda e: fn(_Rec(e))


def _fin(e, h, k=1):
    li = e.last if isinstance(e, _Rec) else None
    if li is None or (isinstance(e, _Rec) and e.last_dma):
        li = (e.e if isinstance(e, _Rec) else e).drain()
    li.then_inc(h, k)



def _bf(x):
    return np.ascontiguousarray(x.astype(ml_dtypes.bfloat16))



def prep_inputs(inp):
    u = np.asarray(inp["L0_attn_u"], np.float32)
    W_in = np.asarray(inp["L0_attn_W_in"], np.float32)
    conv_w = np.asarray(inp["L0_attn_conv_w"], np.float32)
    conv_b = np.asarray(inp["L0_attn_conv_b"], np.float32)
    dt_bias = np.asarray(inp["L0_attn_dt_bias"], np.float32)
    A_log = np.asarray(inp["L0_attn_A_log"], np.float32)
    Dv = np.asarray(inp["L0_attn_D"], np.float32)
    nw = np.asarray(inp["L0_attn_norm_weight"], np.float32)
    W_out = np.asarray(inp["L0_attn_W_out"], np.float32)
    wg = np.asarray(inp["L0_ffn_w_gate"])
    wu = np.asarray(inp["L0_ffn_w_up"])
    wd = np.asarray(inp["L0_ffn_w_down"])
    sg = np.asarray(inp["L0_ffn_s_gate"], np.float32)
    su = np.asarray(inp["L0_ffn_s_up"], np.float32)
    sd = np.asarray(inp["L0_ffn_s_down"], np.float32)

    WinT = np.ascontiguousarray(W_in.T)
    Wdt = np.ascontiguousarray(
        WinT[:, DI + CONV_DIM:].reshape(KC, 128, NH))
    WxBC = np.ascontiguousarray(
        WinT[:, DI:DI + CONV_DIM].reshape(KC, 128, XBC_C, 128)
        .transpose(2, 0, 1, 3))
    Wz = np.ascontiguousarray(
        WinT[:, :DI].reshape(KC, 128, ZC, 128)
        .transpose(2, 0, 1, 3))
    WoT = np.ascontiguousarray(W_out.T.reshape(DI // 128, 128, DM))
    Wg = np.ascontiguousarray(
        wg.reshape(KC, 128, IC, 128).transpose(2, 0, 1, 3))
    Wu = np.ascontiguousarray(
        wu.reshape(KC, 128, IC, 128).transpose(2, 0, 1, 3))
    Wd = np.ascontiguousarray(wd.reshape(IC, 128, DM))

    cwp = np.ascontiguousarray(
        conv_w.reshape(XBC_C, 128, DCONV).transpose(1, 0, 2)
        .reshape(128, XBC_C * DCONV))
    cbp = np.ascontiguousarray(conv_b.reshape(XBC_C, 128).T)
    nwp = np.ascontiguousarray(nw.reshape(NG, 128).T)
    p = np.arange(128)
    dpp = np.ascontiguousarray(
        Dv[2 * np.arange(NG)[None, :] + (p[:, None] // 64)].astype(np.float32))
    sgp = np.ascontiguousarray(sg.reshape(IC, 128).T)
    sup = np.ascontiguousarray(su.reshape(IC, 128).T)
    sdp = np.ascontiguousarray(sd.reshape(KC, 128).T)

    mneg = np.zeros((128, 2 * CHUNK), np.float32)
    ll = np.arange(CHUNK)
    for st in range(2):
        s = np.arange(128) + st * 128
        mneg[:, st * CHUNK:(st + 1) * CHUNK] = np.where(
            ll[None, :] < s[:, None], MASK_NEG, 0.0)
    mneg = mneg.astype(ml_dtypes.bfloat16)
    ident_bf = np.eye(128, dtype=ml_dtypes.bfloat16)
    ident_f = np.eye(128, dtype=np.float32)
    sel64 = np.eye(64, dtype=np.float32)

    shared = dict(
        Wdt=Wdt, WxBC=WxBC, Wz=Wz, WoT=WoT, Wg=Wg, Wu=Wu, Wd=Wd,
        cwp=cwp, cbp=cbp, nwp=nwp, sgp=sgp, sup=sup, sdp=sdp,
        dtb=dt_bias.reshape(NH, 1), alog=A_log.reshape(NH, 1),
        dvec=Dv.reshape(NH, 1), dpp=dpp,
        mneg=mneg, ident=ident_bf, identf=ident_f, sel64=sel64,
    )

    per_core = []
    for core in range(NCORES):
        seq = core // NCS
        ci = core % NCS
        ut = np.empty((NT, TT, DM), np.float32)
        uh = np.zeros((NT, 3, DM), np.float32)
        for t in range(NT):
            b = 4 * t + ci
            ut[t] = u[seq, TT * b:TT * (b + 1)]
            if b > 0:
                uh[t] = u[seq, TT * b - 3:TT * b]
        msel = np.zeros((NH, NCS), np.float32)
        msel[:, :ci] = 1.0
        hsel = np.zeros((NH, NCS), np.float32)
        if ci > 0:
            hsel[:, ci - 1] = 1.0
        per_core.append(dict(ut=ut, uh=np.ascontiguousarray(uh),
                             msel=msel, hsel=hsel))
    return shared, per_core


def unpack_output(res_list):
    out = np.empty((B, L, DM), np.float32)
    for core in range(NCORES):
        seq = core // NCS
        ci = core % NCS
        o = res_list[core]["out"]
        for t in range(NT):
            b = 4 * t + ci
            out[seq, TT * b:TT * (b + 1), :] = o[t]
    return out



AR_ALIGN = 64


def _pad(n, a=AR_ALIGN):
    return (n + a - 1) // a * a


class Prog:

    def __init__(self, nc):
        self.nc = nc
        self._mk_dram()
        self._mk_sbuf()
        self._mk_psum()
        self._mk_sems()

    def _mk_dram(self):
        nc = self.nc
        d = {}
        d["ut"] = nc.dram_tensor("ut", [NT, TT, DM], F32, kind="ExternalInput")
        d["uh"] = nc.dram_tensor("uh", [NT, 3, DM], F32, kind="ExternalInput")
        d["Wdt"] = nc.dram_tensor("Wdt", [KC, 128, NH], F32, kind="ExternalInput")
        d["WxBC"] = nc.dram_tensor("WxBC", [XBC_C, KC, 128, 128], F32, kind="ExternalInput")
        d["Wz"] = nc.dram_tensor("Wz", [ZC, KC, 128, 128], F32, kind="ExternalInput")
        d["WoT"] = nc.dram_tensor("WoT", [DI // 128, 128, DM], F32, kind="ExternalInput")
        d["Wg"] = nc.dram_tensor("Wg", [IC, KC, 128, 128], I8, kind="ExternalInput")
        d["Wu"] = nc.dram_tensor("Wu", [IC, KC, 128, 128], I8, kind="ExternalInput")
        d["Wd"] = nc.dram_tensor("Wd", [IC, 128, DM], I8, kind="ExternalInput")
        d["cwp"] = nc.dram_tensor("cwp", [128, XBC_C * DCONV], F32, kind="ExternalInput")
        d["cbp"] = nc.dram_tensor("cbp", [128, XBC_C], F32, kind="ExternalInput")
        d["nwp"] = nc.dram_tensor("nwp", [128, NG], F32, kind="ExternalInput")
        d["sgp"] = nc.dram_tensor("sgp", [128, IC], F32, kind="ExternalInput")
        d["sup"] = nc.dram_tensor("sup", [128, IC], F32, kind="ExternalInput")
        d["sdp"] = nc.dram_tensor("sdp", [128, KC], F32, kind="ExternalInput")
        d["dtb"] = nc.dram_tensor("dtb", [NH, 1], F32, kind="ExternalInput")
        d["alog"] = nc.dram_tensor("alog", [NH, 1], F32, kind="ExternalInput")
        d["dvec"] = nc.dram_tensor("dvec", [NH, 1], F32, kind="ExternalInput")
        d["dpp"] = nc.dram_tensor("dpp", [128, NG], F32, kind="ExternalInput")
        d["mneg"] = nc.dram_tensor("mneg", [128, 2 * CHUNK], BF16, kind="ExternalInput")
        d["ident"] = nc.dram_tensor("ident", [128, 128], BF16, kind="ExternalInput")
        d["identf"] = nc.dram_tensor("identf", [128, 128], F32, kind="ExternalInput")
        d["sel64"] = nc.dram_tensor("sel64", [NH, NH], F32, kind="ExternalInput")
        d["msel"] = nc.dram_tensor("msel", [NH, NCS], F32, kind="ExternalInput")
        d["hsel"] = nc.dram_tensor("hsel", [NH, NCS], F32, kind="ExternalInput")
        d["out"] = nc.dram_tensor("out", [NT, TT, DM], F32, kind="ExternalOutput")
        d["h_sp"] = nc.dram_tensor("h_sp", [IC, 128, 2 * TT], BF16)
        d["Wdtb"] = nc.dram_tensor("Wdtb", [KC, 128, NH], BF16)
        d["WxBCb"] = nc.dram_tensor("WxBCb", [XBC_C, KC, 128, 128], BF16)
        d["Wzb"] = nc.dram_tensor("Wzb", [ZC, KC, 128, 128], BF16)
        d["WoTb"] = nc.dram_tensor("WoTb", [DI // 128, 128, DM], BF16)
        d["r1_sp"] = nc.dram_tensor("r1_sp", [2, TOKC, 128, DM], BF16)
        for t in range(NT):
            d[f"Sin{t}"] = nc.dram_tensor(f"Sin{t}", [128, DI], BF16)
            d[f"Sout{t}"] = nc.dram_tensor(f"Sout{t}", [NCS * 128, DI], BF16)
            d[f"CEin{t}"] = nc.dram_tensor(f"CEin{t}", [128, DI], BF16)
            d[f"CEout{t}"] = nc.dram_tensor(f"CEout{t}", [NCS * 128, DI], BF16)
            d[f"Tin{t}"] = nc.dram_tensor(f"Tin{t}", [1, NH], F32)
            d[f"Tout{t}"] = nc.dram_tensor(f"Tout{t}", [NCS, NH], F32)
        self.d = d

    def _mk_sbuf(self):
        nc = self.nc
        s = {}
        s["xl2"] = nc.alloc_sbuf_tensor("sb_xl2", [128, 2 * KC * TT], BF16)
        s["cw"] = nc.alloc_sbuf_tensor("sb_cw", [128, XBC_C * DCONV], F32)
        s["cb"] = nc.alloc_sbuf_tensor("sb_cb", [128, XBC_C], F32)
        s["nw"] = nc.alloc_sbuf_tensor("sb_nw", [128, NG], F32)
        s["sg"] = nc.alloc_sbuf_tensor("sb_sg", [128, IC], F32)
        s["su"] = nc.alloc_sbuf_tensor("sb_su", [128, IC], F32)
        s["sd"] = nc.alloc_sbuf_tensor("sb_sd", [128, KC], F32)
        s["dtb"] = nc.alloc_sbuf_tensor("sb_dtb", [NH, 1], F32)
        s["alog"] = nc.alloc_sbuf_tensor("sb_alog", [NH, 1], F32)
        s["negA"] = nc.alloc_sbuf_tensor("sb_negA", [NH, 1], F32)
        s["dvec"] = nc.alloc_sbuf_tensor("sb_dvec", [NH, 1], F32)
        s["dpair"] = nc.alloc_sbuf_tensor("sb_dpair", [128, NG], F32)
        s["mneg"] = nc.alloc_sbuf_tensor("sb_mneg", [128, 2 * CHUNK], BF16)
        s["ident"] = nc.alloc_sbuf_tensor("sb_ident", [128, 128], BF16)
        s["identf"] = nc.alloc_sbuf_tensor("sb_identf", [128, 128], F32)
        s["sel64"] = nc.alloc_sbuf_tensor("sb_sel64", [NH, NH], F32)
        s["sel64b"] = nc.alloc_sbuf_tensor("sb_sel64b", [NH, NH], BF16)
        s["msel"] = nc.alloc_sbuf_tensor("sb_msel", [NH, NCS], F32)
        s["hsel"] = nc.alloc_sbuf_tensor("sb_hsel", [NH, NCS], F32)
        s["epsc"] = nc.alloc_sbuf_tensor("sb_epsc", [128, 1], F32)
        s["onec"] = nc.alloc_sbuf_tensor("sb_onec", [128, 1], F32)
        s["negc"] = nc.alloc_sbuf_tensor("sb_negc", [128, 1], F32)
        s["nhc"] = nc.alloc_sbuf_tensor("sb_nhc", [128, 1], F32)
        s["dtraw"] = nc.alloc_sbuf_tensor("sb_dtraw", [NH, TT], F32)
        s["dt"] = nc.alloc_sbuf_tensor("sb_dt", [NH, TT], F32)
        s["dA"] = nc.alloc_sbuf_tensor("sb_dA", [NH, TT], F32)
        s["cst"] = nc.alloc_sbuf_tensor("sb_cst", [NH, TT], F32)
        s["csc"] = nc.alloc_sbuf_tensor("sb_csc", [NH, TT], F32)
        s["ones64"] = nc.alloc_sbuf_tensor("sb_ones64", [NH, TT], F32)
        s["dtds"] = s["dtraw"]
        s["expda"] = nc.alloc_sbuf_tensor("sb_expda", [NH, CHUNK], BF16)
        s["hcol"] = nc.alloc_sbuf_tensor("sb_hcol", [NH, 8], F32)
        s["tokc"] = nc.alloc_sbuf_tensor("sb_tokc", [128, 8 * NH], F32)
        s["bnst"] = nc.alloc_sbuf_tensor("sb_bnst", [128, 4 * 6], F32)
        s["bnag"] = nc.alloc_sbuf_tensor("sb_bnag", [128, 2], F32)
        s["mcol"] = nc.alloc_sbuf_tensor("sb_mcol", [128, 1], F32)
        s["rcol"] = nc.alloc_sbuf_tensor("sb_rcol", [128, 1], F32)
        s["Tg"] = nc.alloc_sbuf_tensor("sb_Tg", [NH, NCS], F32)
        s["Tcum"] = nc.alloc_sbuf_tensor("sb_Tcum", [NH, NCS], F32)
        s["ones4"] = nc.alloc_sbuf_tensor("sb_ones4", [NH, NCS], F32)
        s["fcol"] = nc.alloc_sbuf_tensor("sb_fcol", [NH, 8], F32)
        s["fbf"] = nc.alloc_sbuf_tensor("sb_fbf", [NH, 8], BF16)
        s["pscol"] = nc.alloc_sbuf_tensor("sb_pscol", [NH, 1], F32)
        s["rstd"] = nc.alloc_sbuf_tensor("sb_rstd", [128, 64], F32)
        s["selp"] = nc.alloc_sbuf_tensor("sb_selp", [NH, 128], F32)
        s["cdp"] = nc.alloc_sbuf_tensor("sb_cdp", [128, NH], BF16)
        s["r64"] = nc.alloc_sbuf_tensor("sb_r64", [NH, NH], BF16)
        s["onesbf"] = nc.alloc_sbuf_tensor("sb_onesbf", [128, 1], BF16)
        s["carry"] = nc.alloc_sbuf_tensor("sb_carry", [128, DI], BF16)
        self.s = s

        off = {}
        o = 0
        def put(name, nbytes):
            nonlocal o
            off[name] = o
            o += _pad(nbytes)
        put("xfm", NBG * TT * 2)
        put("bfm", NBG * TT * 2)
        put("cfm", NG * TT * 2)
        put("zy", ZC * TT * 2)
        put("xln", KC * (TT + 3) * 2)
        put("pc", 3 * (TT + 3) * 2)
        put("xtok", DI * 2)
        put("xds", DI * 2)
        put("btok", DXB * 2)
        put("gl", 4 * CHUNK * 2)
        put("wbuf", 3 * KC * 128 * 2)
        put("ust", DM * 2)
        put("r1", DM * 2)
        off["ln1"] = off["r1"]
        put("scr", 6 * TT * 2)
        m_end = o
        f_o = 0
        def putf(name, nbytes):
            nonlocal f_o
            off[name] = f_o
            f_o += _pad(nbytes)
        putf("fh", 4 * IC * 128 * 2)
        putf("fw", 6 * KC * 128 * 2)
        putf("fwd", 6 * 1024 * 2)
        putf("fr1", 4 * DM * 2)
        putf("fo", 4 * 1024 * 4)
        putf("fg", 2 * 1024 * 2)
        putf("fu", 2 * 1024 * 2)
        putf("fho", 2 * 1024 * 2)
        f_end = f_o
        ar_bytes = _pad(max(m_end, f_end))
        self.off = off
        self.ar_bytes = ar_bytes
        self.arena = nc.alloc_sbuf_tensor("arena", [128, ar_bytes], mybir.dt.uint8)

    def pv(self, pool, off_f32, P, E, dtype):
        if dtype == F32:
            return pool[0:P, off_f32:off_f32 + E]
        ap = pool[0:P, off_f32:off_f32 + (E + 1) // 2].bitcast(BF16)
        return bass.AP(ap.tensor, ap.offset, [[ap.ap[0][0], P], [1, E]])

    def av(self, name, eoff, P, E, dtype, poff=0):
        esz = {F32: 4, BF16: 2, F32R: 4}[dtype]
        b0 = self.off[name] + eoff * esz
        return self.arena[poff:poff + P, b0:b0 + E * esz].bitcast(dtype)

    def _mk_psum(self):
        nc = self.nc
        self.pA = nc.alloc_psum_tensor("pA", [128, 2048], F32)
        self.pB = nc.alloc_psum_tensor("pB", [128, 2048], F32)

    def _mk_sems(self):
        nc = self.nc
        self.sm = {k: Sem(nc, k) for k in
                   ["dg", "dg0", "dg1", "dh0", "dh1", "ds", "du", "dtn",
                    "dr1", "df0", "df1", "do", "t", "v", "sc", "gp", "cc",
                    "fw", "fv", "dg2", "dsh", "dsr"] +
                   [f"do{i}" for i in range(4)] +
                   ["dfg0", "dfg1"] +
                   [f"dgw{i}" for i in range(6)] +
                   [f"dcast{i}" for i in range(7)]}


    def barrier(self, block, engines=("tensor", "vector", "scalar", "gpsimd", "sync")):
        sm = self.sm["fv"]
        base = sm.n
        tgt = base + len(engines)
        sm.n = tgt
        for e in engines:
            def f(eng, _t=tgt):
                _fin(eng, sm.h, 1)
                eng.wait_ge(sm.h, _t)
            getattr(block, e)(f)

    def tap(self, block, name, ap, wait=None):
        if name not in self.taps:
            return
        nc = self.nc
        t = nc.dram_tensor("tap_" + name, list(ap.shape), ap.dtype,
                           kind="ExternalOutput")
        sm = self.sm["ds"]
        tgt = sm.inc(16)
        def f(eng):
            if wait is not None:
                eng.wait_ge(wait[0].h, wait[1])
            eng.dma_start(out=t[tuple(slice(None) for _ in ap.shape)],
                          in_=ap).then_inc(sm.h, 16)
            eng.wait_ge(sm.h, tgt)
        block.sync(_wrap(f))

    def emit_init(self, block):
        nc, s, d = self.nc, self.s, self.d
        dg = self.sm["dg"]
        gp = self.sm["gp"]
        sc = self.sm["sc"]

        def g_fn(g):
            for name in ["cwp", "cbp", "nwp", "sgp", "sup", "sdp", "dtb",
                         "alog", "dvec", "dpp", "mneg", "ident", "identf",
                         "sel64", "msel", "hsel"]:
                dst = {"cwp": s["cw"], "cbp": s["cb"], "nwp": s["nw"],
                       "sgp": s["sg"], "sup": s["su"], "sdp": s["sd"],
                       "dtb": s["dtb"], "alog": s["alog"], "dvec": s["dvec"],
                       "dpp": s["dpair"], "mneg": s["mneg"],
                       "ident": s["ident"], "identf": s["identf"],
                       "sel64": s["sel64"], "msel": s["msel"],
                       "hsel": s["hsel"]}[name]
                sh = dst.shape
                g.dma_start(out=dst[:sh[0], :sh[1]],
                            in_=d[name][:sh[0], :sh[1]]).then_inc(dg.h, 16)
            dg.inc(16 * 16)
            g.memset(s["epsc"][:, :], EPS)
            g.memset(s["onec"][:, :], 1.0)
            g.memset(s["negc"][:, :], -1.0)
            g.memset(s["nhc"][:, :], -0.5)
            g.memset(s["ones64"][:NH, :], 1.0)
            g.memset(s["ones4"][:NH, :], 1.0)
            g.memset(s["onesbf"][:, :], 1.0)
            zv = self.av("xds", 0, 128, DI, BF16)
            g.memset(zv, 0.0)
            g.drain()
            g.dma_start(out=d["CEin0"][:, :], in_=zv).then_inc(dg.h, 16)
            dg.inc(16)
            self.cast_rec = {}
            self._ncast = [0]
            def cast(srcs, dsts, n, name):
                dc = self.sm[f"dcast{self._ncast[0]}"]
                self._ncast[0] += 1
                for i in range(n):
                    g.dma_start(out=dsts[i], in_=srcs[i]).then_inc(dc.h, 16)
                    dc.inc(16)
                self.cast_rec[name] = (dc, dc.n)
            cast([d["Wdt"][:, :, :]], [d["Wdtb"][:, :, :]], 1, "Wdt")
            cast([d["WxBC"][o, :, :, :] for o in range(XBC_C)],
                 [d["WxBCb"][o, :, :, :] for o in range(XBC_C)], XBC_C, "WxBC")
            cast([d["Wz"][o, :, :, :] for o in range(ZC)],
                 [d["Wzb"][o, :, :, :] for o in range(ZC)], ZC, "Wz")
            cast([d["WoT"][k, :, :] for k in range(DI // 128)],
                 [d["WoTb"][k, :, :] for k in range(DI // 128)], DI // 128, "WoT")
            _fin(g, gp.h, 1)
        gp.inc()
        block.gpsimd(_wrap(g_fn))

        def s_fn(scl):
            scl.wait_ge(dg.h, dg.n)
            scl.wait_ge(gp.h, self.sm["gp"].n)
            scl.activation(s["negA"][:NH, :], s["alog"][:NH, :], AF.Exp)
            scl.drain()
            scl.mul(s["negA"][:NH, :], s["negA"][:NH, :], s["negc"][:NH, 0:1])
            _fin(scl, sc.h, 1)
        sc.inc()
        block.scalar(_wrap(s_fn))

        v = self.sm["v"]
        def v_fn(vec):
            vec.wait_ge(dg.h, dg.n)
            vec.tensor_copy(s["sel64b"][:NH, :], s["sel64"][:NH, :])
            _fin(vec, v.h, 1)
        v.inc()
        block.vector(_wrap(v_fn))
        self.barrier(block)


    def xln_v(self, k):
        return self.av("xln", k * (TT + 3), 128, TT + 3, BF16)

    def wb_v(self, buf, k, M=128):
        return self.av("wbuf", buf * (KC * 128) + k * 128, 128, M, BF16)

    def dma_slab(self, g, dram, o, dstname, dstoff, M=128, kc=KC):
        src = bass.AP(dram, o * kc * 128 * M, [[M, 128], [128 * M, kc], [1, M]])
        dv = self.av(dstname, dstoff, 128, kc * M, BF16)
        dst = bass.AP(dv.tensor, dv.offset, [[dv.ap[0][0], 128], [M, kc], [1, M]])
        return g.dma_start(out=dst, in_=src)

    def emit_ln1(self, block, t):
        s, d = self.s, self.d
        ds, v, sc, tt = self.sm["ds"], self.sm["v"], self.sm["sc"], self.sm["t"]
        ust = self.av("ust", 0, 128, DM, BF16)
        du = self.sm["du"]
        v_snap = self.sm["v"].n
        apply_rec = {}
        trans_rec = {}
        copy_rec = {}
        chunks = list(range(-1, TOKC))
        for ci, tc in enumerate(chunks):
            P = 3 if tc < 0 else 128
            par = ci % 2
            ln1 = self.av("ln1", par * DM, P, DM, BF16)
            r_dma = du.inc(16)
            def sy_fn(sy, tc=tc, prev=chunks[ci - 1] if ci >= 1 else None):
                if prev is not None:
                    sy.wait_ge(v.h, apply_rec[prev])
                else:
                    sy.wait_ge(v.h, v_snap)
                if tc < 0:
                    sy.dma_start(out=ust[0:3, :], in_=d["uh"][t, :, :]).then_inc(du.h, 16)
                else:
                    sy.dma_start(out=ust, in_=d["ut"][t, 128 * tc:128 * (tc + 1), :]).then_inc(du.h, 16)
            block.gpsimd(_wrap(sy_fn))

            r_stats = v.inc()
            def v_fn(vec, P=P, r_dma=r_dma):
                vec.wait_ge(du.h, r_dma)
                for q in range(4):
                    vec.bn_stats(s["bnst"][:P, 6 * q:6 * (q + 1)],
                                 ust[:P, 512 * q:512 * (q + 1)])
                vec.drain()
                vec.bn_aggr(s["bnag"][:P, :], s["bnst"][:P, :])
                _fin(vec, v.h, 1)
            block.vector(_wrap(v_fn))

            r_sqrt = sc.inc()
            def s_fn(scl, P=P, r_stats=r_stats):
                scl.wait_ge(v.h, r_stats)
                scl.activation(s["rcol"][:P, :], s["bnag"][:P, 1:2], AF.Ln,
                               bias=s["epsc"][:P, 0:1])
                scl.drain()
                scl.activation(s["rcol"][:P, :], s["rcol"][:P, :], AF.Exp,
                               scale=s["nhc"][:P, 0:1])
                _fin(scl, sc.h, 1)
            block.scalar(_wrap(s_fn))

            r_apply = v.inc()
            apply_rec[tc] = r_apply
            def v_fn2(vec, P=P, ln1=ln1, r_sqrt=r_sqrt,
                      guard=chunks[ci - 2] if ci >= 2 else None):
                vec.wait_ge(sc.h, r_sqrt)
                if guard is not None:
                    vec.wait_ge(tt.h, trans_rec[guard])
                vec.tensor_scalar(out=ln1[:P, :], in0=ust[:P, :],
                                  scalar1=s["bnag"][:P, 0:1],
                                  scalar2=s["rcol"][:P, 0:1],
                                  op0=ALU.subtract, op1=ALU.mult)
                _fin(vec, v.h, 1)
            block.vector(_wrap(v_fn2))

            for g in range(4):
                r_tr = tt.inc()
                def t_fn(ten, g=g, tc=tc, P=P, ln1=ln1, r_apply=r_apply,
                         prevkey=(tc, g) if False else None, ci=ci):
                    ten.wait_ge(v.h, r_apply)
                    if g >= 1:
                        ten.wait_ge(v.h, copy_rec[(tc, g - 1)])
                    elif ci >= 1:
                        ten.wait_ge(v.h, copy_rec[(chunks[ci - 1], 3)])
                    for j in range(4):
                        k = 4 * g + j
                        ps = self.pv(self.pB, 1024 + 64 * j, 128, P, BF16)
                        ten.transpose(ps, ln1[:P, 128 * k:128 * (k + 1)],
                                      s["ident"][:P, :P])
                    _fin(ten, tt.h, 1)
                block.tensor(_wrap(t_fn))
                if g == 3:
                    trans_rec[tc] = r_tr

                r_cp = v.inc()
                copy_rec[(tc, g)] = r_cp
                def v_fn3(vec, g=g, tc=tc, P=P, r_tr=r_tr):
                    vec.wait_ge(tt.h, r_tr)
                    col = 0 if tc < 0 else 3 + 128 * tc
                    for j in range(4):
                        k = 4 * g + j
                        ps = self.pv(self.pB, 1024 + 64 * j, 128, P, BF16)
                        vec.tensor_copy(self.xln_v(k)[:, col:col + P], ps)
                    _fin(vec, v.h, 1)
                block.vector(_wrap(v_fn3))
        self.ln1_done = (v, v.n)

    def emit_dt(self, block, t):
        s, d = self.s, self.d
        dg, v, sc, tt = self.sm["dg"], self.sm["v"], self.sm["sc"], self.sm["t"]
        dg0 = self.sm["dg0"]
        r_w = dg0.inc(16)
        tt_snap = tt.n
        def g_fn(g):
            g.wait_ge(tt.h, tt_snap)
            g.wait_ge(self.cast_rec["Wdt"][0].h, self.cast_rec["Wdt"][1])
            self.dma_slab(g, d["Wdtb"], 0, "wbuf", 0, M=NH).then_inc(dg0.h, 16)
        block.sync(_wrap(g_fn))

        r_mm = tt.inc()
        self.dt_mm = r_mm
        def t_fn(ten):
            ten.wait_ge(self.sm["dg0"].h, r_w)
            ten.wait_ge(self.ln1_done[0].h, self.ln1_done[1])
            for k in range(KC):
                wv = self.av("wbuf", k * NH, 128, NH, BF16)
                ten.matmul(self.pA[0:NH, 0:TT], wv, self.xln_v(k)[:, 3:3 + TT],
                           start=(k == 0), stop=(k == KC - 1))
            _fin(ten, tt.h, 1)
        block.tensor(_wrap(t_fn))

        r_dt = sc.inc()
        r_dtv = v.inc()
        self.dt_evac = r_dtv
        def v_fn(vec):
            vec.wait_ge(tt.h, r_mm)
            vec.tensor_copy(s["dtraw"][:NH, :], self.pA[0:NH, 0:TT])
            _fin(vec, v.h, 1)
        block.vector(_wrap(v_fn))

        def s_fn(scl):
            scl.wait_ge(v.h, r_dtv)
            scl.activation(s["dt"][:NH, :], s["dtraw"][:NH, :], AF.Identity,
                           bias=s["dtb"][:NH, 0:1])
            scl.drain()
            scl.activation(s["dtds"][:NH, :], s["dt"][:NH, :], AF.Abs)
            scl.drain()
            scl.activation(s["dtds"][:NH, :], s["dtds"][:NH, :], AF.Exp,
                           scale=s["negc"][:NH, 0:1])
            scl.drain()
            scl.activation(s["dtds"][:NH, :], s["dtds"][:NH, :], AF.Ln,
                           bias=s["onec"][:NH, 0:1])
            scl.drain()
            scl.activation(s["dt"][:NH, :], s["dt"][:NH, :], AF.Relu)
            _fin(scl, sc.h, 1)
        block.scalar(_wrap(s_fn))

        r_cs = v.inc()
        def v_fn2(vec):
            vec.wait_ge(sc.h, r_dt)
            vec.tensor_add(s["dt"][:NH, :], s["dt"][:NH, :], s["dtds"][:NH, :])
            vec.drain()
            vec.tensor_scalar(out=s["dA"][:NH, :], in0=s["dt"][:NH, :],
                              scalar1=s["negA"][:NH, 0:1], scalar2=None,
                              op0=ALU.mult)
            vec.drain()
            vec.tensor_tensor_scan(s["cst"][:NH, :], s["ones64"][:NH, :],
                                   s["dA"][:NH, :], 0.0, ALU.mult, ALU.add)
            for c in range(NCHUNK):
                vec.tensor_tensor_scan(
                    s["csc"][:NH, CHUNK * c:CHUNK * (c + 1)],
                    s["ones64"][:NH, CHUNK * c:CHUNK * (c + 1)],
                    s["dA"][:NH, CHUNK * c:CHUNK * (c + 1)],
                    0.0, ALU.mult, ALU.add)
            _fin(vec, v.h, 1)
        block.vector(_wrap(v_fn2))

        r_dtds = sc.inc()
        def s_fn2(scl):
            scl.wait_ge(v.h, r_cs)
            scl.activation(s["dtds"][:NH, :], s["cst"][:NH, :], AF.Exp,
                           scale=s["negc"][:NH, 0:1],
                           bias=s["cst"][:NH, TT - 1:TT])
            _fin(scl, sc.h, 1)
        block.scalar(_wrap(s_fn2))

        r_dtds2 = v.inc()
        def v_fn3(vec):
            vec.wait_ge(sc.h, r_dtds)
            vec.tensor_mul(s["dtds"][:NH, :], s["dtds"][:NH, :], s["dt"][:NH, :])
            _fin(vec, v.h, 1)
        block.vector(_wrap(v_fn3))

        dtn = self.sm["dtn"]
        r_ts = dtn.inc(16)
        def sy_fn(sy):
            sy.wait_ge(v.h, r_dtds2)
            sy.dma_start(out=d[f"Tin{t}"][0:1, 0:NH],
                         in_=s["cst"][0:NH, TT - 1:TT]).then_inc(dtn.h, 16)
        block.sync(_wrap(sy_fn))
        self.tin_rec = r_ts
        self.dt_done = (v, r_dtds2)

    def emit_inproj_xbc(self, block, t):
        s = self.s
        dg, v, sc, tt = self.sm["dg"], self.sm["v"], self.sm["sc"], self.sm["t"]
        NO = XBC_C
        GRP = 3
        ngrp = NO // GRP
        w_rec, mm_rec, ev_rec, cv_rec, si_rec = {}, {}, {}, {}, {}
        stp = TT + 3

        for gi in range(ngrp):
            ppool = self.pA if gi % 2 == 0 else self.pB
            for j in range(GRP):
                o = gi * GRP + j

                def g_fn(g, o=o, t=t):
                    dsl = self.sm[f"dg{o % 3}"]
                    if o < 3:
                        g.wait_ge(tt.h, self.dt_mm)
                        if t == 0 and o == 0:
                            g.wait_ge(self.cast_rec["WxBC"][0].h, self.cast_rec["WxBC"][1])
                    if o - 3 in mm_rec:
                        g.wait_ge(tt.h, mm_rec[o - 3])
                    self.dma_slab(g, self.d["WxBCb"], o, "wbuf",
                                  (o % 3) * KC * 128).then_inc(dsl.h, 16)
                    w_rec[o] = dsl.inc(16)
                block.sync(_wrap(g_fn))

                def t_fn(ten, gi=gi, j=j, o=o, ppool=ppool):
                    dsl = self.sm[f"dg{o % 3}"]
                    if o == 0:
                        ten.wait_ge(v.h, self.dt_evac)
                    ten.wait_ge(dsl.h, w_rec[o])
                    if j == 0 and gi >= 2:
                        ten.wait_ge(sc.h, ev_rec[gi - 2])
                    main = ppool[:, 512 * j:512 * (j + 1)]
                    halo = ppool[:, 1536 + 128 * j:1536 + 128 * j + 3]
                    for k in range(KC):
                        wv = self.wb_v(o % 3, k)
                        ten.matmul(main, wv, self.xln_v(k)[:, 3:3 + TT],
                                   start=(k == 0), stop=(k == KC - 1))
                        ten.matmul(halo, wv, self.xln_v(k)[:, 0:3],
                                   start=(k == 0), stop=(k == KC - 1))
                    mm_rec[o] = tt.inc()
                    _fin(ten, tt.h, 1)
                block.tensor(_wrap(t_fn))

            def s_fn(scl, gi=gi, ppool=ppool):
                scl.wait_ge(tt.h, mm_rec[gi * GRP + GRP - 1])
                if gi >= 1:
                    scl.wait_ge(sc.h, si_rec[gi - 1])
                for j in range(GRP):
                    main = ppool[:, 512 * j:512 * (j + 1)]
                    halo = ppool[:, 1536 + 128 * j:1536 + 128 * j + 3]
                    pc = self.av("pc", j * stp, 128, stp, BF16)
                    scl.copy(pc[:, 0:3], halo)
                    scl.copy(pc[:, 3:3 + TT], main)
                ev_rec[gi] = sc.inc()
                _fin(scl, sc.h, 1)
            block.scalar(_wrap(s_fn))

            def v_fn(vec, gi=gi):
                vec.wait_ge(sc.h, ev_rec[gi])
                o0 = gi * GRP
                cw = s["cw"]
                for j in range(GRP):
                    o = o0 + j
                    pcj = self.av("pc", j * stp, 128, stp, BF16)
                    mt = [self.av("scr", q * TT, 128, TT, BF16) for q in range(4)]
                    cvt = self.av("scr", 4 * TT, 128, 2 * TT, BF16)
                    for kk in range(4):
                        vec.tensor_scalar(out=mt[kk][:, :], in0=pcj[:, kk:kk + TT],
                                          scalar1=cw[:, 4 * o + kk:4 * o + kk + 1],
                                          scalar2=None, op0=ALU.mult)
                    vec.drain()
                    vec.tensor_add(cvt[:, 0:TT], mt[0][:, :], mt[1][:, :])
                    vec.tensor_add(cvt[:, TT:2 * TT], mt[2][:, :], mt[3][:, :])
                    vec.drain()
                    vec.tensor_add(pcj[:, 0:TT], cvt[:, 0:TT], cvt[:, TT:2 * TT])
                    vec.drain()
                cv_rec[gi] = v.inc()
                _fin(vec, v.h, 1)
            block.vector(_wrap(v_fn))

            def s_fn2(scl, gi=gi):
                scl.wait_ge(v.h, cv_rec[gi])
                for j in range(GRP):
                    o = gi * GRP + j
                    pcj = self.av("pc", j * stp, 128, stp, BF16)
                    if o < 8:
                        dst = self.av("xfm", o * TT, 128, TT, BF16)
                    elif o < 16:
                        dst = self.av("bfm", (o - 8) * TT, 128, TT, BF16)
                    else:
                        dst = self.av("cfm", (o - 16) * TT, 128, TT, BF16)
                    scl.activation(dst, pcj[:, 0:TT], AF.Silu,
                                   bias=s["cb"][:, o:o + 1])
                si_rec[gi] = sc.inc()
                _fin(scl, sc.h, 1)
            block.scalar(_wrap(s_fn2))
        self.xbc_done = (sc, si_rec[ngrp - 1])
        self.xbc_mm_done = (tt, mm_rec[NO - 1])

    def emit_tapbar(self, block, specs):
        self.barrier(block)
        ds = self.sm["ds"]
        def sy_fn(sy):
            for name, ap in specs:
                tt_ = self.nc.dram_tensor("tap_" + name, list(ap.shape),
                                          ap.dtype, kind="ExternalOutput")
                idx = tuple(slice(None) for _ in ap.shape)
                sy.dma_start(out=tt_[idx], in_=ap).then_inc(ds.h, 16)
                ds.inc(16)
            sy.wait_ge(ds.h, ds.n)
        block.sync(_wrap(sy_fn))

    def _transpose_batch(self, block, items, pbank_off=0, pool=None):
        tt, v = self.sm["t"], self.sm["v"]
        pool = pool or self.pA
        for g0 in range(0, len(items), 4):
            grp = items[g0:g0 + 4]
            def t_fn(ten, grp=grp, g0=g0):
                if g0 >= 4:
                    ten.wait_ge(v.h, self._tb_cp)
                for j, (ina, outa, dt_) in enumerate(grp):
                    ps = self.pv(pool, pbank_off + 64 * j, ina.shape[1],
                                 ina.shape[0], dt_)
                    idn = self.s["ident"] if dt_ == BF16 else self.s["identf"]
                    ten.transpose(ps, ina, idn[:ina.shape[0], :ina.shape[0]])
                _fin(ten, tt.h, 1)
            self._tb_tr = tt.inc()
            block.tensor(_wrap(t_fn))
            r_tr = self._tb_tr
            def v_fn(vec, grp=grp, r_tr=r_tr):
                vec.wait_ge(tt.h, r_tr)
                for j, (ina, outa, dt_) in enumerate(grp):
                    ps = self.pv(pool, pbank_off + 64 * j, ina.shape[1],
                                 ina.shape[0], dt_)
                    vec.tensor_copy(outa, ps)
                _fin(vec, v.h, 1)
            self._tb_cp = v.inc()
            block.vector(_wrap(v_fn))

    def emit_phaseA(self, block, t):
        s, d = self.s, self.d
        dg, ds, v, sc, tt = (self.sm[k] for k in ["dg", "ds", "v", "sc", "t"])
        xtok = lambda st: self.av("xtok", st * DXB, 128, DXB, BF16)
        btokf = lambda st: self.av("zy", st * DXB, 128, DXB, BF16)
        xds = self.av("xds", 0, 128, DI, BF16)

        if t > 0:
            r_ce = dg.inc(16)
            def g_ce(g):
                g.dma_start(out=d[f"CEin{t}"][:, :],
                            in_=s["carry"][:, :]).then_inc(dg.h, 16)
            block.gpsimd(_wrap(g_ce))
            self.ce_sent = (dg, r_ce)
        else:
            self.ce_sent = (dg, dg.n)

        def t_gate(ten):
            ten.wait_ge(self.xbc_done[0].h, self.xbc_done[1])
        block.tensor(_wrap(t_gate))

        items = []
        for st in range(4):
            for g_ in range(NBG):
                items.append((self.av("xfm", g_ * TT + 128 * st, 128, 128, BF16),
                              bass.AP(xtok(st).tensor, xtok(st).offset + g_ * 128,
                                      [[xtok(st).ap[0][0], 128], [1, 128]]),
                              BF16))
        for st in range(4):
            for g_ in range(NBG):
                items.append((self.av("bfm", g_ * TT + 128 * st, 128, 128, BF16),
                              bass.AP(btokf(st).tensor, btokf(st).offset + g_ * 128,
                                      [[btokf(st).ap[0][0], 128], [1, 128]]),
                              BF16))
        for st in range(4):
            items.append((s["dtds"][0:NH, 128 * st:128 * (st + 1)],
                          s["tokc"][0:128, NH * st:NH * (st + 1)], F32))
        self._tb_cp = v.n
        self._transpose_batch(block, items)
        r_tp = self._tb_cp

        smm = None
        for st in range(4):
            r_xds = v.inc()
            def v_fn(vec, st=st, r_xds=r_xds, smm=smm):
                if st == 0:
                    vec.wait_ge(v.h, r_tp)
                else:
                    vec.wait_ge(tt.h, smm)
                for h in range(NH):
                    xc = (h // 8) * 128 + 64 * (h % 2)
                    vec.tensor_scalar(
                        out=bass.AP(xds.tensor, xds.offset + 64 * h,
                                    [[xds.ap[0][0], 128], [1, 64]]),
                        in0=bass.AP(xtok(st).tensor, xtok(st).offset + xc,
                                    [[xtok(st).ap[0][0], 128], [1, 64]]),
                        scalar1=s["tokc"][0:128, NH * st + h:NH * st + h + 1],
                        scalar2=None, op0=ALU.mult)
                _fin(vec, v.h, 1)
            block.vector(_wrap(v_fn))

            smm = tt.inc()
            def t_fn(ten, st=st, r_xds=r_xds):
                ten.wait_ge(v.h, r_xds)
                for bg in range(NBG):
                    ps = self.pA[:, 512 * bg:512 * (bg + 1)] if bg < 4 else \
                        self.pB[:, 512 * (bg - 4):512 * (bg - 3)]
                    lhs = bass.AP(btokf(st).tensor, btokf(st).offset + bg * 128,
                                  [[btokf(st).ap[0][0], 128], [1, 128]])
                    rhs = bass.AP(xds.tensor, xds.offset + 512 * bg,
                                  [[xds.ap[0][0], 128], [1, 512]])
                    ten.matmul(ps, lhs, rhs, start=(st == 0), stop=(st == 3))
                _fin(ten, tt.h, 1)
            block.tensor(_wrap(t_fn))

        r_sev = v.inc()
        def v_fn2(vec, smm=smm):
            vec.wait_ge(tt.h, smm)
            for bg in range(NBG):
                ps = self.pA[:, 512 * bg:512 * (bg + 1)] if bg < 4 else \
                    self.pB[:, 512 * (bg - 4):512 * (bg - 3)]
                vec.tensor_copy(bass.AP(xds.tensor, xds.offset + 512 * bg,
                                        [[xds.ap[0][0], 128], [1, 512]]), ps)
            _fin(vec, v.h, 1)
        block.vector(_wrap(v_fn2))

        r_ssend = dg.inc(16)
        def g_fn(g):
            g.wait_ge(v.h, r_sev)
            g.dma_start(out=d[f"Sin{t}"][:, :], in_=xds).then_inc(dg.h, 16)
        block.gpsimd(_wrap(g_fn))
        self.s_sent = (dg, r_ssend)
        self.sA_done = (v, r_sev)

    def emit_collective(self, block, t):
        d = self.d
        cc, dg, ds = self.sm["cc"], self.sm["dg"], self.sm["ds"]
        groups = [[0, 1, 2, 3], [4, 5, 6, 7]]
        r_cc = cc.inc(3)
        def g_fn(g):
            g.wait_ge(dg.h, max(self.s_sent[1], self.ce_sent[1]))
            g.wait_ge(self.sm["dtn"].h, self.tin_rec)
            for a, b in [(f"Sin{t}", f"Sout{t}"), (f"CEin{t}", f"CEout{t}"),
                         (f"Tin{t}", f"Tout{t}")]:
                g.collective_compute(
                    "AllGather", ALU.bypass, replica_groups=groups,
                    ins=[d[a][:, :].opt()], outs=[d[b][:, :].opt()],
                ).then_inc(cc.h, 1)
        block.gpsimd(_wrap(g_fn))
        self.cc_done = (cc, r_cc)

    def emit_inproj_z(self, block, t):
        dg0, dg1, v, sc, tt = (self.sm[k] for k in
                               ["dg0", "dg1", "v", "sc", "t"])
        w_rec, mm_rec, si_rec = {}, {}, {}
        for o in range(ZC):
            dsl = self.sm[f"dg{o % 3}"]
            def g_fn(g, o=o, dsl=dsl, t=t):
                if o < 3:
                    g.wait_ge(tt.h, self.xbc_mm_done[1])
                    if t == 0 and o == 0:
                        g.wait_ge(self.cast_rec["Wz"][0].h, self.cast_rec["Wz"][1])
                else:
                    g.wait_ge(tt.h, mm_rec[o - 3])
                self.dma_slab(g, self.d["Wzb"], o, "wbuf",
                              (o % 3) * KC * 128).then_inc(dsl.h, 16)
                w_rec[o] = dsl.inc(16)
            block.sync(_wrap(g_fn))

            def t_fn(ten, o=o, dsl=dsl):
                ten.wait_ge(dsl.h, w_rec[o])
                if o < 2:
                    ten.wait_ge(self.sA_done[0].h, self.sA_done[1])
                else:
                    ten.wait_ge(self.sm["sc"].h, si_rec[o - 2])
                ps = self.pA[:, 512 * (o % 2):512 * (o % 2) + 512]
                for k in range(KC):
                    ten.matmul(ps, self.wb_v(o % 3, k),
                               self.xln_v(k)[:, 3:3 + TT],
                               start=(k == 0), stop=(k == KC - 1))
                mm_rec[o] = tt.inc()
                _fin(ten, tt.h, 1)
            block.tensor(_wrap(t_fn))

            def s_fn(scl, o=o):
                scl.wait_ge(tt.h, mm_rec[o])
                ps = self.pA[:, 512 * (o % 2):512 * (o % 2) + 512]
                scl.activation(self.av("zy", o * TT, 128, TT, BF16), ps,
                               AF.Silu)
                si_rec[o] = self.sm["sc"].inc()
                _fin(scl, self.sm["sc"].h, 1)
            block.scalar(_wrap(s_fn))
        self.z_done = (self.sm["sc"], si_rec[ZC - 1])

    def emit_kasm(self, block, t):
        s, d = self.s, self.d
        dg, ds, v, sc, tt, cc = (self.sm[k] for k in
                                 ["dg", "ds", "v", "sc", "t", "cc"])
        recv = self.av("xds", 0, 128, DI, BF16)
        tmp = self.av("scr", 0, 128, 512, BF16)

        dtn = self.sm["dtn"]
        r_tg = dtn.inc(16)
        def sy_fn(sy):
            sy.wait_ge(cc.h, self.cc_done[1])
            src = bass.AP(d[f"Tout{t}"], 0, [[1, NH], [NH, NCS]])
            sy.dma_start(out=s["Tg"][0:NH, 0:NCS], in_=src,
                         allow_slow_non_contiguous=True).then_inc(dtn.h, 16)
        block.sync(_wrap(sy_fn))

        r_f = v.inc()
        def v_fn(vec):
            vec.wait_ge(dtn.h, r_tg)
            vec.tensor_tensor_scan(s["Tcum"][:NH, :], s["ones4"][:NH, :NCS],
                                   s["Tg"][:NH, :], 0.0, ALU.mult, ALU.add)
            vec.drain()
            vec.tensor_mul(s["fcol"][:NH, 0:NCS], s["hsel"][:NH, :],
                           s["Tcum"][:NH, :])
            vec.drain()
            vec.tensor_reduce(s["pscol"][:NH, 0:1], s["fcol"][:NH, 0:NCS],
                              axis=mybir.AxisListType.X, op=ALU.add)
            _fin(vec, v.h, 1)
        block.vector(_wrap(v_fn))

        r_cl = v.inc()
        def v_cl(vec):
            vec.wait_ge(v.h, r_f)
            for j in range(3):
                vec.tensor_scalar(out=s["fcol"][:NH, 1 + j:2 + j],
                                  in0=s["Tcum"][:NH, j:j + 1],
                                  scalar1=s["pscol"][:NH, 0:1], scalar2=0.0,
                                  op0=ALU.subtract, op1=ALU.max)
            _fin(vec, v.h, 1)
        block.vector(_wrap(v_cl))
        r_fs = sc.inc()
        def s_fn(scl):
            scl.wait_ge(v.h, r_cl)
            scl.activation(s["fcol"][:NH, 0:1], s["pscol"][:NH, 0:1], AF.Exp)
            for j in range(3):
                scl.drain()
                scl.activation(s["fcol"][:NH, 1 + j:2 + j],
                               s["fcol"][:NH, 1 + j:2 + j], AF.Exp,
                               scale=s["negc"][:NH, 0:1])
            _fin(scl, sc.h, 1)
        block.scalar(_wrap(s_fn))

        r_fb = v.inc()
        def v_fn2(vec):
            vec.wait_ge(sc.h, r_fs)
            for j in range(3):
                vec.tensor_mul(s["fcol"][:NH, 1 + j:2 + j],
                               s["fcol"][:NH, 1 + j:2 + j],
                               s["msel"][:NH, j:j + 1])
            vec.drain()
            vec.tensor_copy(s["fbf"][:NH, 0:4], s["fcol"][:NH, 0:4])
            _fin(vec, v.h, 1)
        block.vector(_wrap(v_fn2))

        last_v = r_fb
        for q in range(4):
            src = d[f"CEout{t}"] if q == 0 else d[f"Sout{t}"]
            row0 = 3 * 128 if q == 0 else (q - 1) * 128
            r_recv = dg.inc(16)
            def g_fn(g, src=src, row0=row0, last_v=last_v, q=q):
                if q == 0:
                    g.wait_ge(cc.h, self.cc_done[1])
                    g.wait_ge(dg.h, self.s_sent[1])
                g.wait_ge(v.h, last_v)
                g.dma_start(out=recv,
                            in_=src[row0:row0 + 128, :]).then_inc(dg.h, 16)
            block.gpsimd(_wrap(g_fn))

            for slab in range(8):
                ps = self.pA[:, 512 * (slab % 2):512 * (slab % 2) + 512]
                r_mm = tt.inc()
                def t_fn(ten, q=q, slab=slab, ps=ps, r_recv=r_recv):
                    ten.wait_ge(v.h, r_fb)
                    if q == 0 and slab < 2:
                        ten.wait_ge(self.z_done[0].h, self.z_done[1])
                    if not (q == 0 and slab < 2):
                        ten.wait_ge(v.h, self._ka_v)
                    lhs = s["fbf"][0:NH, q:q + 1].to_broadcast((NH, 128))
                    rhs = bass.AP(s["sel64b"], 8 * slab,
                                  [[NH, NH], [1, 8], [0, 64]])
                    ten.matmul(ps, lhs, rhs, start=True, stop=True)
                    _fin(ten, tt.h, 1)
                block.tensor(_wrap(t_fn))

                self._ka_v = v.inc()
                def v_fn3(vec, q=q, slab=slab, ps=ps, r_mm=r_mm, r_recv=r_recv):
                    vec.wait_ge(tt.h, r_mm)
                    vec.wait_ge(dg.h, r_recv)
                    cslab = s["carry"][:, 512 * slab:512 * (slab + 1)]
                    rslab = bass.AP(recv.tensor, recv.offset + 512 * slab,
                                    [[recv.ap[0][0], 128], [1, 512]])
                    if q == 0:
                        vec.tensor_mul(cslab, rslab, ps)
                    else:
                        vec.tensor_mul(tmp[:, :], rslab, ps)
                        vec.drain()
                        vec.tensor_add(cslab, cslab, tmp[:, :])
                    _fin(vec, v.h, 1)
                block.vector(_wrap(v_fn3))
            last_v = self._ka_v
        self.kasm_done = (v, last_v)

    def emit_chunk(self, block, t, c):
        s = self.s
        v, sc, tt = self.sm["v"], self.sm["sc"], self.sm["t"]
        CC = CHUNK * c
        csc_c = s["csc"][0:NH, CC:CC + CHUNK]
        xds = self.av("xds", 0, 128, DI, BF16)
        xtok = self.av("xtok", 0, 128, 4096, BF16)
        first_gate = self.kasm_done if c == 0 else self.chunk_done

        def xtv(col, w=64):
            return bass.AP(xtok.tensor, xtok.offset + col,
                           [[xtok.ap[0][0], 128], [1, w]])

        def xdv(col, w):
            return bass.AP(xds.tensor, xds.offset + col,
                           [[xds.ap[0][0], 128], [1, w]])

        r_prep_s = sc.inc()
        def s_fn(scl):
            scl.wait_ge(first_gate[0].h, first_gate[1])
            scl.activation(s["expda"][:NH, :], csc_c, AF.Exp)
            scl.activation(s["dtds"][:NH, CC:CC + CHUNK], csc_c, AF.Exp,
                           scale=s["negc"][:NH, 0:1],
                           bias=s["csc"][:NH, CC + CHUNK - 1:CC + CHUNK])
            scl.activation(s["hcol"][:NH, 0:1],
                           s["csc"][:NH, CC + CHUNK - 1:CC + CHUNK], AF.Exp)
            _fin(scl, sc.h, 1)
        block.scalar(_wrap(s_fn))

        r_prep_v = v.inc()
        def v_fn(vec):
            vec.wait_ge(sc.h, r_prep_s)
            vec.tensor_mul(s["dtds"][:NH, CC:CC + CHUNK],
                           s["dtds"][:NH, CC:CC + CHUNK],
                           s["dt"][:NH, CC:CC + CHUNK])
            vec.tensor_scalar(out=s["r64"][:NH, :], in0=s["sel64b"][:NH, :],
                              scalar1=s["hcol"][:NH, 0:1], scalar2=None,
                              op0=ALU.mult)
            _fin(vec, v.h, 1)
        block.vector(_wrap(v_fn))

        r_cdp_t = tt.inc()
        def t_fn(ten):
            ten.wait_ge(v.h, r_prep_v)
            ten.matmul(self.pA[:, 0:64],
                       s["onesbf"][0:NH, 0:1].to_broadcast((NH, 128)),
                       s["r64"][0:NH, :], start=True, stop=True)
            _fin(ten, tt.h, 1)
        block.tensor(_wrap(t_fn))
        r_cdp_v = v.inc()
        def v_fn2(vec):
            vec.wait_ge(tt.h, r_cdp_t)
            vec.tensor_copy(s["cdp"][:, 0:NH], self.pA[:, 0:64])
            _fin(vec, v.h, 1)
        block.vector(_wrap(v_fn2))

        items = []
        for st in range(2):
            sl = slice(CC + 128 * st, CC + 128 * (st + 1))
            items.append((s["csc"][0:NH, sl], s["tokc"][0:128, 64 * st:64 * (st + 1)], F32))
            items.append((s["dt"][0:NH, sl], s["tokc"][0:128, 128 + 64 * st:192 + 64 * st], F32))
            items.append((s["dtds"][0:NH, sl], s["tokc"][0:128, 256 + 64 * st:320 + 64 * st], F32))
        for st in range(2):
            for g_ in range(NBG):
                items.append((self.av("xfm", g_ * TT + CC + 128 * st, 128, 128, BF16),
                              xtv(1024 * st + g_ * 128, 128), BF16))
        self._tb_cp = v.n
        self._transpose_batch(block, items, pbank_off=512, pool=self.pA)
        r_tp = self._tb_cp

        r_neg = v.inc()
        def v_fn3(vec):
            vec.wait_ge(v.h, r_tp)
            vec.tensor_scalar(out=s["tokc"][0:128, 384:512],
                              in0=s["tokc"][0:128, 0:128], scalar1=-1.0,
                              scalar2=None, op0=ALU.mult)
            _fin(vec, v.h, 1)
        block.vector(_wrap(v_fn3))

        btok = self.av("btok", 0, 128, DXB, BF16)
        r_cupd = {}
        for ph in range(2):
            pacc = self.pA if ph == 0 else self.pB
            ptr = self.pB if ph == 0 else self.pA
            smm = None
            for st in range(2):
                bitems = [(self.av("bfm", g_ * TT + CC + 128 * st, 128, 128, BF16),
                           bass.AP(btok.tensor, btok.offset + g_ * 128,
                                   [[btok.ap[0][0], 128], [1, 128]]), BF16)
                          for g_ in range(NBG)]
                def tg(ten, ph=ph, st=st, smm=smm):
                    if ph == 0 and st == 0:
                        ten.wait_ge(v.h, r_cdp_v)
                    if ph == 1 and st == 0:
                        ten.wait_ge(v.h, r_cupd[0])
                    if smm is not None:
                        ten.wait_ge(tt.h, smm)
                block.tensor(_wrap(tg))
                self._tb_cp = v.n
                self._transpose_batch(block, bitems, pbank_off=512 + 1024,
                                      pool=ptr)
                r_bt = self._tb_cp

                r_xds = v.inc()
                def v_fn4(vec, ph=ph, st=st, r_bt=r_bt, smm=smm):
                    vec.wait_ge(v.h, r_bt)
                    if st == 0 and ph == 0:
                        vec.wait_ge(v.h, r_neg)
                    if smm is not None:
                        vec.wait_ge(tt.h, smm)
                    for hh in range(32):
                        h = 32 * ph + hh
                        xc = 1024 * st + (h // 8) * 128 + 64 * (h % 2)
                        vec.tensor_scalar(
                            out=xdv(2048 * ph + 64 * hh, 64), in0=xtv(xc),
                            scalar1=s["tokc"][0:128, 256 + 64 * st + h:257 + 64 * st + h],
                            scalar2=None, op0=ALU.mult)
                    _fin(vec, v.h, 1)
                block.vector(_wrap(v_fn4))

                smm = tt.inc()
                def t_fn2(ten, ph=ph, st=st, r_xds=r_xds, pacc=pacc):
                    ten.wait_ge(v.h, r_xds)
                    for bgl in range(4):
                        bg = 4 * ph + bgl
                        ps = pacc[:, 512 * bgl:512 * (bgl + 1)]
                        lhs = bass.AP(btok.tensor, btok.offset + bg * 128,
                                      [[btok.ap[0][0], 128], [1, 128]])
                        rhs = xdv(2048 * ph + 512 * bgl, 512)
                        ten.matmul(ps, lhs, rhs, start=(st == 0), stop=(st == 1))
                    _fin(ten, tt.h, 1)
                block.tensor(_wrap(t_fn2))

            r_cu = v.inc()
            r_cupd[ph] = r_cu
            def v_fn5(vec, ph=ph, smm=smm, pacc=pacc):
                vec.wait_ge(tt.h, smm)
                vec.tensor_copy(xdv(2048 * ph, 2048),
                                s["carry"][:, 2048 * ph:2048 * (ph + 1)])
                vec.drain()
                for bgl in range(4):
                    bg = 4 * ph + bgl
                    ps = pacc[:, 512 * bgl:512 * (bgl + 1)]
                    csl = s["carry"][:, 512 * bg:512 * (bg + 1)]
                    cdap = bass.AP(s["cdp"], 8 * bg, [[NH, 128], [1, 8], [0, 64]])
                    vec.tensor_tensor(csl, csl, cdap, ALU.mult)
                    vec.drain()
                    vec.tensor_add(csl, csl, ps)
                    vec.drain()
                _fin(vec, v.h, 1)
            block.vector(_wrap(v_fn5))
        r_cup = r_cupd[1]

        ydone, exps_rec, yd_rec, gl_rec = {}, {}, {}, {}

        def pools(j):
            pp = self.pA if j % 2 == 0 else self.pB
            return pp, (pp, 512), (pp, 1024), (pp, 1536)

        def emit_t3(j):
            pG, pe, pYd, pYo = pools(j)
            r = tt.inc()
            def t_fn3(ten, j=j, pG=pG, pe=pe, pYd=pYd):
                if j < 2:
                    ten.wait_ge(v.h, r_cup)
                else:
                    ten.wait_ge(v.h, ydone[j - 2])
                crhs = self.av("cfm", j * TT + CC, 128, CHUNK, BF16)
                crhs1 = self.av("cfm", j * TT + CC + 128, 128, 128, BF16)
                for st in range(2):
                    bl = self.av("bfm", (j // 4) * TT + CC + 128 * st, 128, 128, BF16)
                    if st == 0:
                        ten.matmul(pG[:, 0:CHUNK], bl, crhs, start=True, stop=True)
                    else:
                        ten.matmul(pG[:, CHUNK:CHUNK + 128], bl, crhs1,
                                   start=True, stop=True)
                for hh in range(2):
                    lhs = s["sel64"][0:NH, 2 * j + hh:2 * j + hh + 1] \
                        .to_broadcast((NH, 128))
                    ten.matmul(pe[0][:, pe[1] + 256 * hh:pe[1] + 256 * hh + 256],
                               lhs, csc_c, start=True, stop=True)
                    ten.matmul(pYd[0][:, pYd[1] + 256 + 128 * hh:pYd[1] + 256 + 128 * hh + 128],
                               lhs, s["csc"][0:NH, CC + 128:CC + CHUNK],
                               start=True, stop=True)
                _fin(ten, tt.h, 1)
            block.tensor(_wrap(t_fn3))
            return r

        t3_rec = {0: emit_t3(0), 1: emit_t3(1)}
        for j in range(NG):
            pG, pe, pYd, pYo = pools(j)

            r_xb = v.inc()
            def v_fn6a(vec, j=j):
                if j == 0:
                    vec.wait_ge(v.h, r_neg)
                if j >= 4:
                    vec.wait_ge(tt.h, yd_rec[j - 4])
                so = 2048 + 256 * (j % 4)
                for st in range(2):
                    for hh in range(2):
                        h = 2 * j + hh
                        xc = 1024 * st + (h // 8) * 128 + 64 * (h % 2)
                        vec.tensor_scalar(
                            out=xtv(so + 64 * (2 * st + hh)), in0=xtv(xc),
                            scalar1=s["tokc"][0:128, 128 + 64 * st + h:129 + 64 * st + h],
                            scalar2=None, op0=ALU.mult)
                _fin(vec, v.h, 1)
            block.vector(_wrap(v_fn6a))

            r_vm = v.inc()
            def v_fn6b(vec, j=j, pe=pe, pYd=pYd, r_tj=t3_rec[j]):
                vec.wait_ge(tt.h, r_tj)
                ap_e = bass.AP(pe[0], pe[1], [[2048, 128], [256, 2], [1, 128]])
                mm0 = bass.AP(s["mneg"], 0, [[512, 128], [0, 2], [1, 128]])
                vec.tensor_tensor(ap_e, ap_e, mm0, ALU.add)
                ap_e1 = bass.AP(pYd[0], pYd[1] + 256, [[2048, 128], [128, 2], [1, 128]])
                mm1 = bass.AP(s["mneg"], CHUNK + 128, [[512, 128], [0, 2], [1, 128]])
                vec.tensor_tensor(ap_e1, ap_e1, mm1, ALU.add)
                _fin(vec, v.h, 1)
            block.vector(_wrap(v_fn6b))

            r_exps = sc.inc()
            exps_rec[j] = r_exps
            def s_fn2(scl, j=j, pe=pe, pYd=pYd, r_vm=r_vm):
                scl.wait_ge(v.h, r_vm)
                for hh in range(2):
                    h = 2 * j + hh
                    gl0 = self.av("gl", hh * 2 * CHUNK, 128, CHUNK, BF16)
                    scl.activation(gl0,
                                   pe[0][:, pe[1] + 256 * hh:pe[1] + 256 * hh + 256],
                                   AF.Exp, bias=s["tokc"][0:128, 384 + h:385 + h])
                    gl1 = self.av("gl", hh * 2 * CHUNK + CHUNK, 128, 128, BF16)
                    scl.activation(gl1,
                                   pYd[0][:, pYd[1] + 256 + 128 * hh:pYd[1] + 256 + 128 * hh + 128],
                                   AF.Exp, bias=s["tokc"][0:128, 448 + h:449 + h])
                _fin(scl, sc.h, 1)
            block.scalar(_wrap(s_fn2))

            r_em = tt.inc()
            def t_fn4(ten, j=j, pYd=pYd, r_exps=r_exps):
                ten.wait_ge(sc.h, r_exps)
                if j >= 2:
                    ten.wait_ge(v.h, ydone[j - 2])
                for hh in range(2):
                    lhs = s["sel64b"][0:NH, 2 * j + hh:2 * j + hh + 1] \
                        .to_broadcast((NH, 64))
                    ten.matmul(pYd[0][64 * hh:64 * (hh + 1), pYd[1]:pYd[1] + 256],
                               lhs, s["expda"][0:NH, :],
                               start=True, stop=True)
                _fin(ten, tt.h, 1)
            block.tensor(_wrap(t_fn4))

            r_ee = v.inc()
            def s_fn3(vec, j=j, pYd=pYd, r_em=r_em):
                vec.wait_ge(tt.h, r_em)
                eda = self.av("scr", CHUNK * (j % 2), 128, CHUNK, BF16)
                vec.tensor_copy(eda, pYd[0][:, pYd[1]:pYd[1] + 256])
                _fin(vec, v.h, 1)
            block.vector(_wrap(s_fn3))

            r_gl = v.inc()
            gl_rec[j] = r_gl
            def v_fn7(vec, j=j, pG=pG, r_exps=r_exps):
                vec.wait_ge(sc.h, r_exps)
                for hh in range(2):
                    gl0 = self.av("gl", hh * 2 * CHUNK, 128, CHUNK, BF16)
                    vec.tensor_mul(gl0, gl0, pG[:, 0:CHUNK])
                    gl1 = self.av("gl", hh * 2 * CHUNK + CHUNK, 128, 128, BF16)
                    vec.tensor_mul(gl1, gl1, pG[:, CHUNK:CHUNK + 128])
                _fin(vec, v.h, 1)
            block.vector(_wrap(v_fn7))

            r_ym = tt.inc()
            yd_rec[j] = r_ym
            def t_fn5(ten, j=j, pYd=pYd, pYo=pYo, r_gl=r_gl, r_ee=r_ee):
                ten.wait_ge(v.h, max(r_gl, r_ee))
                so = 2048 + 256 * (j % 4)
                for hh in range(2):
                    for st in range(2):
                        lhsT = xtv(so + 64 * (2 * st + hh))
                        gl0 = self.av("gl", hh * 2 * CHUNK + st * CHUNK, 128,
                                      CHUNK if st == 0 else 128, BF16)
                        out = pYd[0][64 * hh:64 * hh + 64,
                                     pYd[1] + (0 if st == 0 else 128):pYd[1] + 256]
                        ten.matmul(out, lhsT, gl0, start=(st == 0), stop=(st == 1))
                crhs = self.av("cfm", j * TT + CC, 128, CHUNK, BF16)
                ten.matmul(pYo[0][:, pYo[1]:pYo[1] + 256], xdv(128 * j, 128),
                           crhs, start=True, stop=True)
                _fin(ten, tt.h, 1)
            block.tensor(_wrap(t_fn5))

            r_ya = v.inc()
            ydone[j] = r_ya
            def v_fn8(vec, j=j, pYd=pYd, pYo=pYo, r_ym=r_ym):
                vec.wait_ge(tt.h, r_ym)
                eda = self.av("scr", CHUNK * (j % 2), 128, CHUNK, BF16)
                ysb = self.av("scr", 2 * CHUNK, 128, CHUNK, BF16)
                vec.tensor_mul(ysb, eda, pYo[0][:, pYo[1]:pYo[1] + 256])
                vec.drain()
                vec.tensor_add(ysb, ysb, pYd[0][:, pYd[1]:pYd[1] + 256])
                vec.drain()
                xsl = self.av("xfm", (j // 4) * TT + CC, 128, CHUNK, BF16)
                vec.scalar_tensor_tensor(out=ysb, in0=xsl,
                                         scalar=s["dpair"][:, j:j + 1],
                                         in1=ysb, op0=ALU.mult, op1=ALU.add)
                vec.drain()
                zsl = self.av("zy", j * TT + CC, 128, CHUNK, BF16)
                vec.tensor_mul(zsl, ysb, zsl)
                _fin(vec, v.h, 1)
            block.vector(_wrap(v_fn8))

            if j + 2 < NG:
                t3_rec[j + 2] = emit_t3(j + 2)

        sq_rec = {}
        for j in range(NG):
            r_sq = v.inc()
            def v_fn9(vec, j=j):
                vec.wait_ge(v.h, ydone[j])
                if j >= 1:
                    vec.wait_ge(tt.h, sq_rec[j - 1])
                vec.drain()
                ysq = self.av("scr", 2 * CHUNK, 128, CHUNK, BF16)
                zsl = self.av("zy", j * TT + CC, 128, CHUNK, BF16)
                vec.tensor_mul(ysq, zsl, zsl)
                _fin(vec, v.h, 1)
            block.vector(_wrap(v_fn9))
            r_sm = tt.inc()
            sq_rec[j] = r_sm
            def t_fn6(ten, j=j, r_sq=r_sq):
                ten.wait_ge(v.h, r_sq)
                if j == 0:
                    ten.wait_ge(v.h, ydone[NG - 1])
                ysq = self.av("scr", 2 * CHUNK, 128, CHUNK, BF16)
                for half in range(2):
                    lhsT = bass.AP(ysq.tensor, ysq.offset + 128 * half,
                                   [[ysq.ap[0][0], 128], [1, 128]])
                    ten.matmul(self.pA[:, 2 * j + half:2 * j + half + 1],
                               lhsT, s["onesbf"][:, 0:1].to_broadcast((128, 1)),
                               start=True, stop=True)
                _fin(ten, tt.h, 1)
            block.tensor(_wrap(t_fn6))
        r_rs = v.inc()
        def v_fn10(vec):
            vec.wait_ge(tt.h, sq_rec[NG - 1])
            vec.tensor_scalar(out=s["rstd"][:, 0:NH], in0=self.pA[:, 0:64],
                              scalar1=1.0 / 128, scalar2=None, op0=ALU.mult)
            _fin(vec, v.h, 1)
        block.vector(_wrap(v_fn10))
        r_rsq = sc.inc()
        def s_fn4(scl):
            scl.wait_ge(v.h, r_rs)
            scl.activation(s["rstd"][:, 0:NH], s["rstd"][:, 0:NH], AF.Ln,
                           bias=s["epsc"][:, 0:1])
            scl.drain()
            scl.activation(s["rstd"][:, 0:NH], s["rstd"][:, 0:NH], AF.Exp,
                           scale=s["nhc"][:, 0:1])
            _fin(scl, sc.h, 1)
        block.scalar(_wrap(s_fn4))
        r_rrec = r_rsq

        ap_rec = {}
        for j in range(NG):
            r_rb = v.inc()
            def v_fn12(vec, j=j):
                vec.wait_ge(sc.h, r_rrec)
                if j >= 2:
                    vec.wait_ge(v.h, ap_rec[j - 2])
                for half in range(2):
                    rh = self.av("scr", 3 * CHUNK + 128 * half, 128, 128, BF16)
                    vec.tensor_scalar(out=rh, in0=s["ident"][:, :],
                                      scalar1=s["rstd"][:, 2 * j + half:2 * j + half + 1],
                                      scalar2=None, op0=ALU.mult)
                _fin(vec, v.h, 1)
            block.vector(_wrap(v_fn12))
            r_rm = tt.inc()
            def t_fn7(ten, j=j, r_rb=r_rb):
                ten.wait_ge(v.h, r_rb)
                pR = self.pB if j % 2 == 0 else self.pA
                for half in range(2):
                    rh = self.av("scr", 3 * CHUNK + 128 * half, 128, 128, BF16)
                    ten.matmul(pR[:, 128 * half:128 * (half + 1)],
                               s["onesbf"][:, 0:1].to_broadcast((128, 128)),
                               rh, start=True, stop=True)
                _fin(ten, tt.h, 1)
            block.tensor(_wrap(t_fn7))
            r_ap = v.inc()
            ap_rec[j] = r_ap
            def v_fn13(vec, j=j, r_rm=r_rm):
                vec.wait_ge(tt.h, r_rm)
                pR = self.pB if j % 2 == 0 else self.pA
                for half in range(2):
                    zsl = self.av("zy", j * TT + CC + 128 * half, 128, 128, BF16)
                    vec.scalar_tensor_tensor(
                        out=zsl, in0=zsl, scalar=s["nw"][:, j:j + 1],
                        in1=pR[:, 128 * half:128 * (half + 1)],
                        op0=ALU.mult, op1=ALU.mult)
                _fin(vec, v.h, 1)
            block.vector(_wrap(v_fn13))
        self.chunk_done = (v, ap_rec[NG - 1])

    def emit_outproj(self, block, t):
        s, d = self.s, self.d
        dg, ds, v, sc, tt = (self.sm[k] for k in ["dg", "ds", "v", "sc", "t"])
        dg0, dg1 = self.sm["dg0"], self.sm["dg1"]
        ust = self.av("ust", 0, 128, DM, BF16)
        r1 = self.av("r1", 0, 128, DM, BF16)
        w = t % 2
        w_rec, mm_rec, ev_rec, tr_rec = {}, {}, {}, {}
        u_rec, ln_rec = {}, {}
        for tp in range(2):
            for k in range(KC + KC):
                dsl = self.sm[f"dg{k % 3}"]
                def g_fn(g, tp=tp, k=k, dsl=dsl, t=t):
                    if tp == 0 and k < 3:
                        g.wait_ge(self.z_done[0].h, self.z_done[1])
                        if t == 0 and k == 0:
                            g.wait_ge(self.cast_rec["WoT"][0].h, self.cast_rec["WoT"][1])
                    if k - 3 >= 0 or tp > 0:
                        pk = (tp, k - 3) if k >= 3 else (
                            tp - 1, 30 + k if k < 2 else 29)
                        g.wait_ge(tt.h, mm_rec[pk])
                    self.dma_slab(g, d["WoTb"], k, "wbuf",
                                  (k % 3) * KC * 128, M=DM, kc=1
                                  ).then_inc(dsl.h, 16)
                block.sync(_wrap(g_fn))
                w_rec[(tp, k)] = dsl.inc(16)

                def t_fn(ten, tp=tp, k=k, dsl=dsl,
                         cpw=(self._ol_cp if tp == 1 else None)):
                    ten.wait_ge(dsl.h, w_rec[(tp, k)])
                    if k == 0:
                        ten.wait_ge(v.h, self.chunk_done[1])
                        if tp == 1:
                            ten.wait_ge(v.h, cpw)
                    wv = self.av("wbuf", (k % 3) * KC * 128, 128, DM, BF16)
                    for tl in range(2):
                        tok = 2 * tp + tl
                        pp = self.pA if tl == 0 else self.pB
                        lhsT = self.av("zy", k * TT + 128 * tok, 128, 128, BF16)
                        for q in range(4):
                            ten.matmul(pp[:, 512 * q:512 * (q + 1)], lhsT,
                                       bass.AP(wv.tensor, wv.offset + 512 * q,
                                               [[wv.ap[0][0], 128], [1, 512]]),
                                       start=(k == 0), stop=(k == KC + KC - 1))
                    mm_rec[(tp, k)] = tt.inc()
                    _fin(ten, tt.h, 1)
                block.tensor(_wrap(t_fn))

            for tl in range(2):
                tok = 2 * tp + tl
                pp = self.pA if tl == 0 else self.pB
                du = self.sm["du"]
                r_u = du.inc(16)
                u_rec[tok] = r_u
                def sy_fn(sy, t=t, tok=tok, tp=tp, tl=tl):
                    if tok > 0:
                        sy.wait_ge(v.h, ev_rec[tok - 1])
                    sy.dma_start(out=ust, in_=d["ut"][t, 128 * tok:128 * (tok + 1), :]
                                 ).then_inc(du.h, 16)
                block.gpsimd(_wrap(sy_fn))

                r_ev = v.inc()
                ev_rec[tok] = r_ev
                def v_fn(vec, tok=tok, tp=tp, pp=pp, r_u=r_u):
                    vec.wait_ge(tt.h, mm_rec[(tp, KC + KC - 1)])
                    vec.wait_ge(self.sm["du"].h, r_u)
                    if tok > 0:
                        vec.wait_ge(self.sm["dr1"].h, self.r1_rec)
                        vec.wait_ge(tt.h, tr_rec[tok - 1])
                    vec.tensor_add(r1, pp[:, 0:DM], ust)
                    vec.drain()
                    for q in range(4):
                        vec.bn_stats(s["bnst"][:, 6 * q:6 * (q + 1)],
                                     r1[:, 512 * q:512 * (q + 1)])
                    vec.drain()
                    vec.bn_aggr(s["bnag"][:, :], s["bnst"][:, :])
                    _fin(vec, v.h, 1)
                block.vector(_wrap(v_fn))

                self.r1_rec = self.sm["dr1"].inc(16)
                def sy_fn2(sy, t=t, tok=tok, r_ev=r_ev):
                    sy.wait_ge(v.h, r_ev)
                    sy.dma_start(out=d["r1_sp"][t % 2, tok, :, :],
                                 in_=r1).then_inc(self.sm["dr1"].h, 16)
                block.sync(_wrap(sy_fn2))

                r_sq = sc.inc()
                def s_fn(scl, r_ev=r_ev):
                    scl.wait_ge(v.h, r_ev)
                    scl.activation(s["rcol"][:, :], s["bnag"][:, 1:2], AF.Ln,
                                   bias=s["epsc"][:, 0:1])
                    scl.drain()
                    scl.activation(s["rcol"][:, :], s["rcol"][:, :], AF.Exp,
                                   scale=s["nhc"][:, 0:1])
                    _fin(scl, sc.h, 1)
                block.scalar(_wrap(s_fn))

                r_ln = v.inc()
                ln_rec[tok] = r_ln
                def v_fn2(vec, r_sq=r_sq):
                    vec.wait_ge(sc.h, r_sq)
                    vec.wait_ge(self.sm["dr1"].h, self.r1_rec)
                    vec.tensor_scalar(out=r1, in0=r1,
                                      scalar1=s["bnag"][:, 0:1],
                                      scalar2=s["rcol"][:, 0:1],
                                      op0=ALU.subtract, op1=ALU.mult)
                    _fin(vec, v.h, 1)
                block.vector(_wrap(v_fn2))

                for g_ in range(4):
                    r_tr = tt.inc()
                    def t_fn2(ten, g_=g_, tok=tok, r_ln=r_ln):
                        ten.wait_ge(v.h, r_ln)
                        if g_ >= 1:
                            ten.wait_ge(v.h, self._ol_cp)
                        for jj in range(4):
                            k = 4 * g_ + jj
                            ps = self.pv(self.pA, 1024 + 64 * jj, 128, 128, BF16)
                            ten.transpose(ps, r1[:, 128 * k:128 * (k + 1)],
                                          s["ident"][:, :])
                        _fin(ten, tt.h, 1)
                    block.tensor(_wrap(t_fn2))
                    if g_ == 3:
                        tr_rec[tok] = r_tr
                    def v_fn3(vec, g_=g_, tok=tok, w=w, r_tr=r_tr):
                        vec.wait_ge(tt.h, r_tr)
                        for jj in range(4):
                            k = 4 * g_ + jj
                            ps = self.pv(self.pA, 1024 + 64 * jj, 128, 128, BF16)
                            vec.tensor_copy(
                                s["xl2"][:, (w * KC + k) * TT + 128 * tok:
                                         (w * KC + k) * TT + 128 * (tok + 1)], ps)
                        _fin(vec, v.h, 1)
                    self._ol_cp = v.inc()
                    block.vector(_wrap(v_fn3))
        self.op_done = (v, self._ol_cp)

    def emit_ffn(self, block, w):
        s, d = self.s, self.d
        dg, ds, v, sc, tt = (self.sm[k] for k in ["dg", "ds", "v", "sc", "t"])
        dg0, dg1 = self.sm["dg0"], self.sm["dg1"]
        w_rec, mm_rec, hev = {}, {}, {}
        h_rec = {}
        prev_out = self.out_rec.get(7, 0)
        for o in range(IC):
            dsl = self.sm["dfg0"] if o % 2 == 0 else self.sm["dfg1"]
            def g_fn(g, o=o, dsl=dsl):
                if o < 2:
                    g.wait_ge(self.op_done[0].h, self.op_done[1])
                if o >= 2:
                    g.wait_ge(tt.h, mm_rec[o - 2])
                self.dma_slab(g, d["Wg"], o, "fw", (o % 2) * KC * 128
                              ).then_inc(dsl.h, 16)
                w_rec[o] = dsl.inc(16)
            block.gpsimd(_wrap(g_fn))
            def g_fn2(g, o=o, dsl=dsl):
                if o >= 2:
                    g.wait_ge(tt.h, mm_rec[o - 2])
                self.dma_slab(g, d["Wu"], o, "fw", (2 + o % 2) * KC * 128
                              ).then_inc(dsl.h, 16)
                w_rec[(o, "u")] = dsl.inc(16)
            block.gpsimd(_wrap(g_fn2))

            def t_fn(ten, o=o, dsl=dsl, w=w, ldev=self.last_dev):
                ten.wait_ge(dsl.h, w_rec[(o, "u")])
                if o < 2 and ldev is not None:
                    ten.wait_ge(v.h, ldev)
                if o >= 2:
                    ten.wait_ge(v.h, hev[o - 2])
                pp = self.pA if o % 2 == 0 else self.pB
                for k in range(KC):
                    wg_ = self.av("fw", (o % 2) * KC * 128 + k * 128, 128, 128, BF16)
                    wu_ = self.av("fw", (2 + o % 2) * KC * 128 + k * 128, 128, 128, BF16)
                    for half in range(2):
                        rhs = s["xl2"][:, (half * KC + k) * TT:(half * KC + k) * TT + TT]
                        ten.matmul(pp[:, 512 * half:512 * (half + 1)], wg_, rhs,
                                   start=(k == 0), stop=(k == KC - 1))
                        ten.matmul(pp[:, 1024 + 512 * half:1536 + 512 * half], wu_,
                                   rhs, start=(k == 0), stop=(k == KC - 1))
                mm_rec[o] = tt.inc()
                _fin(ten, tt.h, 1)
            block.tensor(_wrap(t_fn))

            r_si = sc.inc()
            def s_fn(scl, o=o):
                scl.wait_ge(tt.h, mm_rec[o])
                pp = self.pA if o % 2 == 0 else self.pB
                fg = self.av("fg", (o % 2) * 1024, 128, 1024, BF16)
                scl.activation(fg, pp[:, 0:1024], AF.Silu,
                               scale=s["sg"][:, o:o + 1])
                _fin(scl, sc.h, 1)
            block.scalar(_wrap(s_fn))

            r_hv = v.inc()
            hev[o] = r_hv
            def v_fn(vec, o=o, r_su=r_si, r_si=r_si):
                vec.wait_ge(sc.h, r_si)
                if o >= 2:
                    vec.wait_ge(self.sm["dh0" if o % 2 == 0 else "dh1"].h,
                                h_rec[o - 2])
                pp = self.pA if o % 2 == 0 else self.pB
                fu = self.av("fu", (o % 2) * 1024, 128, 1024, BF16)
                fg = self.av("fg", (o % 2) * 1024, 128, 1024, BF16)
                fho = self.av("fho", (o % 2) * 1024, 128, 1024, BF16)
                vec.tensor_scalar(out=fu, in0=pp[:, 1024:2048],
                                  scalar1=s["su"][:, o:o + 1], scalar2=None,
                                  op0=ALU.mult)
                vec.drain()
                vec.tensor_mul(fho, fg, fu)
                _fin(vec, v.h, 1)
            block.vector(_wrap(v_fn))

            dhp = self.sm["dh0" if o % 2 == 0 else "dh1"]
            r_hd = dhp.inc(16)
            h_rec[o] = r_hd
            def g_fn3(g, o=o, r_hv=r_hv, ldn=self.last_dn, dhp=dhp):
                g.wait_ge(v.h, r_hv)
                if o == 0 and ldn is not None:
                    g.wait_ge(tt.h, ldn)
                fho = self.av("fho", (o % 2) * 1024, 128, 1024, BF16)
                g.dma_start(out=d["h_sp"][o, :, :], in_=fho).then_inc(dhp.h, 16)
            block.sync(_wrap(g_fn3))

        sdbc = self.av("fw", 0, 128, DM, BF16)
        r_sdb = {}
        for m in range(KC):
            r_rb = v.inc()
            def v_fn2(vec, m=m):
                if m == 0:
                    vec.wait_ge(tt.h, mm_rec[IC - 1])
                if m >= 2:
                    vec.wait_ge(v.h, r_sdb[m - 2])
                rh = self.av("fg", 0, 128, 128, BF16)
                vec.tensor_scalar(out=rh, in0=s["ident"][:, :],
                                  scalar1=s["sd"][:, m:m + 1], scalar2=None,
                                  op0=ALU.mult)
                _fin(vec, v.h, 1)
            block.vector(_wrap(v_fn2))
            r_mm = tt.inc()
            def t_fn2(ten, m=m, r_rb=r_rb):
                ten.wait_ge(v.h, r_rb)
                if m == 0:
                    ten.wait_ge(v.h, hev[IC - 1])
                rh = self.av("fg", 0, 128, 128, BF16)
                pp = self.pA if m % 2 == 0 else self.pB
                ten.matmul(pp[:, 0:128],
                           s["onesbf"][:, 0:1].to_broadcast((128, 128)),
                           rh, start=True, stop=True)
                _fin(ten, tt.h, 1)
            block.tensor(_wrap(t_fn2))
            r_cp = v.inc()
            r_sdb[m] = r_cp
            def v_fn3(vec, m=m, r_mm=r_mm):
                vec.wait_ge(tt.h, r_mm)
                pp = self.pA if m % 2 == 0 else self.pB
                vec.tensor_copy(bass.AP(sdbc.tensor, sdbc.offset + 128 * m,
                                        [[sdbc.ap[0][0], 128], [1, 128]]),
                                pp[:, 0:128])
                _fin(vec, v.h, 1)
        
            block.vector(_wrap(v_fn3))

        dn_rec, dev_rec, hld_rec, wd_rec = {}, {}, {}, {}
        dsh, dsr = self.sm["dsh"], self.sm["dsr"]
        for q in range(2):
            r_hl = dsh.inc(16 * 4)
            def g_hl(g, q=q, w=w):
                if q == 0:
                    g.wait_ge(self.sm["dh0"].h, h_rec[IC - 2])
                    g.wait_ge(self.sm["dh1"].h, h_rec[IC - 1])
                else:
                    g.wait_ge(tt.h, dn_rec[(0, 1, IC - 1)])
                for tl in range(4):
                    tok = 4 * q + tl
                    fh = self.av("fh", tl * IC * 128, 128, IC * 128, BF16)
                    srcp = bass.AP(d["h_sp"], 128 * tok,
                                   [[2 * TT, 128], [128 * 2 * TT, IC], [1, 128]])
                    g.dma_start(out=bass.AP(fh.tensor, fh.offset,
                                            [[fh.ap[0][0], 128], [128, IC], [1, 128]]),
                                in_=srcp).then_inc(dsh.h, 16)
            block.sync(_wrap(g_hl))
            hld_rec[q] = r_hl

            r_r1 = self.sm["df0"].inc(16 * 4)
            def g_r1(g, q=q, w=w, pldev=self.last_dev):
                if q == 1:
                    for tl in range(4):
                        g.wait_ge(v.h, dev_rec[(0, 1, tl)])
                elif pldev is not None:
                    g.wait_ge(v.h, pldev)
                for tl in range(4):
                    tok = 4 * q + tl
                    tile = 2 * w + tok // 4
                    fr1 = self.av("fr1", tl * DM, 128, DM, BF16)
                    g.dma_start(out=fr1, in_=d["r1_sp"][tile % 2, tok % 4, :, :]
                                ).then_inc(self.sm["df0"].h, 16)
            block.sync(_wrap(g_r1))
            self._fr1w = r_r1

            for half in range(2):
                for k in range(0, IC, 2):
                    dsl = self.sm[f"dgw{k % 6}"]
                    def g_wd(g, q=q, half=half, k=k, dsl=dsl):
                        if (q, half, k) == (0, 0, 0):
                            g.wait_ge(v.h, r_sdb[KC - 1])
                        pk = None
                        pw = 61 + k if k < 4 else 55 + k
                        if k >= 6:
                            pk = (q, half, k - 5)
                        elif half == 1:
                            pk = (q, 0, pw)
                        elif q == 1:
                            pk = (0, 1, pw)
                        if pk is not None and pk in dn_rec:
                            g.wait_ge(tt.h, dn_rec[pk])
                        srcp = bass.AP(d["Wd"], k * 128 * DM + 1024 * half,
                                      [[DM, 128], [128 * DM, 2], [1, 1024]])
                        dv = self.av("fwd", (k % 6) * 1024, 128, 2048, BF16)
                        dst = bass.AP(dv.tensor, dv.offset,
                                      [[dv.ap[0][0], 128], [1024, 2], [1, 1024]])
                        g.dma_start(out=dst, in_=srcp).then_inc(dsl.h, 16)
                        wd_rec[(q, half, k)] = dsl.inc(16)
                        wd_rec[(q, half, k + 1)] = wd_rec[(q, half, k)]
                    block.gpsimd(_wrap(g_wd))

                    for kk in (k, k + 1):
                        def t_dn(ten, q=q, half=half, k=kk, dsl=dsl):
                            if k % 2 == 0:
                                ten.wait_ge(dsl.h, wd_rec[(q, half, k)])
                            if k == 0:
                                ten.wait_ge(dsh.h, hld_rec[q])
                                if (q, half) == (0, 0):
                                    ten.wait_ge(v.h, hev[IC - 1])
                                else:
                                    ph = (q, half - 1) if half == 1 else (q - 1, 1)
                                    for tl in range(4):
                                        ten.wait_ge(v.h, dev_rec[(ph[0], ph[1], tl)])
                            wv = self.av("fwd", (k % 6) * 1024, 128, 1024, BF16)
                            for tl in range(4):
                                fh = self.av("fh", tl * IC * 128, 128, IC * 128, BF16)
                                lhsT = bass.AP(fh.tensor, fh.offset + 128 * k,
                                               [[fh.ap[0][0], 128], [1, 128]])
                                pp = self.pA if tl < 2 else self.pB
                                po = 1024 * (tl % 2)
                                for sub in range(2):
                                    ten.matmul(pp[:, po + 512 * sub:po + 512 * (sub + 1)],
                                               lhsT,
                                               bass.AP(wv.tensor, wv.offset + 512 * sub,
                                                       [[wv.ap[0][0], 128], [1, 512]]),
                                               start=(k == 0), stop=(k == IC - 1))
                            dn_rec[(q, half, k)] = tt.inc()
                            _fin(ten, tt.h, 1)
                        block.tensor(_wrap(t_dn))

                for tl in range(4):
                    tok = 4 * q + tl
                    tile = 2 * w + tok // 4
                    trow = (tok % 4) * 128
                    pp = self.pA if tl < 2 else self.pB
                    po = 1024 * (tl % 2)
                    r_dev = v.inc()
                    dev_rec[(q, half, tl)] = r_dev
                    def v_dn(vec, q=q, half=half, tl=tl, pp=pp, po=po):
                        vec.wait_ge(tt.h, dn_rec[(q, half, IC - 1)])
                        vec.wait_ge(self.sm["df0"].h, self._fr1w)
                        if self.out_rec.get(("p", tl)) is not None:
                            vec.wait_ge(self.sm[f"do{tl}"].h, self.out_rec[("p", tl)])
                        fr1 = self.av("fr1", tl * DM + 1024 * half, 128, 1024, BF16)
                        fo = self.av("fo", tl * 1024, 128, 1024, F32)
                        sdb = bass.AP(sdbc.tensor, sdbc.offset + 1024 * half,
                                      [[sdbc.ap[0][0], 128], [1, 1024]])
                        vec.tensor_mul(fo, pp[:, po:po + 1024], sdb)
                        vec.drain()
                        vec.tensor_add(fo, fo, fr1)
                        _fin(vec, v.h, 1)
                    block.vector(_wrap(v_dn))

                    self.out_rec[("p", tl)] = self.sm[f"do{tl}"].inc(16)
                    def sy_o(sy, tile=tile, trow=trow, half=half, tl=tl, r_dev=r_dev):
                        sy.wait_ge(v.h, r_dev)
                        fo = self.av("fo", tl * 1024, 128, 1024, F32)
                        sy.dma_start(out=d["out"][tile, trow:trow + 128,
                                                  1024 * half:1024 * (half + 1)],
                                     in_=fo).then_inc(self.sm[f"do{tl}"].h, 16)
                    block.sync(_wrap(sy_o))
            self._fr1w = r_r1
        self.ffn_done = (self.sm["do"], self.sm["do"].n)
        self.last_dev = dev_rec[(1, 1, 3)]
        self.last_dn = dn_rec[(1, 1, IC - 1)]

    def emit_all(self, block):
        self.out_rec = {-1: 0}
        self.last_dev = None
        self.last_dn = None
        self.emit_init(block)
        for t in range(NT):
            self.emit_ln1(block, t)
            self.emit_dt(block, t)
            self.emit_inproj_xbc(block, t)
            self.emit_phaseA(block, t)
            self.emit_collective(block, t)
            self.emit_inproj_z(block, t)
            self.emit_kasm(block, t)
            for c in range(NCHUNK):
                self.emit_chunk(block, t, c)
            self.emit_outproj(block, t)
            if t % 2 == 1:
                self.emit_ffn(block, t // 2)
        dos = [self.sm[f"do{i}"] for i in range(4)]
        def sy_fin(sy):
            for dsem in dos:
                sy.wait_ge(dsem.h, dsem.n)
        block.sync(_wrap(sy_fin))


def build_program():
    nc = bass.Bass("TRN2", target_bir_lowering=False)
    p = Prog(nc)
    with nc.Block() as block:
        p.emit_all(block)
    return nc


_NC_CACHE = None


def kernel(**inputs):
    global _NC_CACHE
    shared, per_core = prep_inputs(inputs)
    if _NC_CACHE is None:
        _NC_CACHE = build_program()
    nc = _NC_CACHE
    in_maps = []
    for core in range(NCORES):
        m = dict(shared)
        m.update(per_core[core])
        in_maps.append(m)
    res = run_bass_kernel_spmd(nc, in_maps, core_ids=list(range(NCORES)))
    return unpack_output(res.results)
